# version 02
# baseline (speedup 1.0000x reference)
import numpy as np
import concourse.bass as bass
import concourse.mybir as mybir
from concourse.bass_utils import run_bass_kernel_spmd

F32 = mybir.dt.float32
BF16 = mybir.dt.bfloat16
I32 = mybir.dt.int32
AF = mybir.ActivationFunctionType
OP = mybir.AluOpType
AX = mybir.AxisListType

P = 128
D = 2048
E = 4096
HID = 5632
R = 128
NS = 16
KCONV = 4
SEQS = 2
L = 2048
T = 512
MT = SEQS * (L // T)
TSB = T // P
KD = D // P
NE = E // P
KH = HID // P
EPS = 1e-5
NCORES = 8

DMA_SEMS = ["qw0", "qw1", "qd0", "qd1", "qz", "qo", "qx", "qb", "qp"]
ENG_SEMS = ["pe", "ev", "vv", "dv", "ac", "sc", "ad"]


def _rearr_col(v, nchunk):
    return np.ascontiguousarray(np.asarray(v).reshape(nchunk, P).T).astype(np.float32)


def _prep_inputs(inp):
    asc = np.ascontiguousarray
    h = {}
    h["inwT"] = asc(inp["L0_attn_in_w"].T)
    h["skipwT"] = asc(inp["L0_attn_skip_w"].T)
    h["outwT"] = asc(inp["L0_attn_out_w"].T)
    h["wd1T"] = asc(inp["L0_attn_wd1"].T)
    h["wd2T"] = asc(inp["L0_attn_wd2"].T)
    h["wBT"] = asc(inp["L0_attn_wB"].T)
    h["wCT"] = asc(inp["L0_attn_wC"].T)
    h["gwT"] = asc(inp["L0_ffn_wq_gate"].T)
    h["uwT"] = asc(inp["L0_ffn_wq_up"].T)
    h["dwT"] = asc(inp["L0_ffn_wq_down"].T)
    h["s_gate"] = asc(np.asarray(inp["L0_ffn_s_gate"], np.float32).reshape(1, HID))
    h["s_up"] = asc(np.asarray(inp["L0_ffn_s_up"], np.float32).reshape(1, HID))
    h["s_down"] = asc(np.asarray(inp["L0_ffn_s_down"], np.float32).reshape(1, D))
    h["normw"] = _rearr_col(inp["L0_attn_norm_w"], KD)
    h["convw"] = asc(np.asarray(inp["L0_attn_conv_w"], np.float32).reshape(NE, P, KCONV)
                     .transpose(1, 0, 2).reshape(P, NE * KCONV))
    h["convb"] = _rearr_col(inp["L0_attn_conv_b"], NE)
    h["wd2b"] = _rearr_col(inp["L0_attn_wd2_b"], NE)
    h["WD"] = _rearr_col(inp["L0_attn_W_D"], NE)
    h["Alog"] = asc(np.asarray(inp["L0_attn_A_log"], np.float32).reshape(NE, P, NS)
                    .transpose(1, 0, 2).reshape(P, NE * NS))
    for k in ["inwT", "skipwT", "outwT", "wd1T", "wd2T", "wBT", "wCT"]:
        h[k] = np.asarray(h[k], np.float32)
    for k in ["gwT", "uwT", "dwT"]:
        h[k] = np.asarray(h[k], np.int32)
    return h


def build():
    nc = bass.Bass(target_bir_lowering=False)

    resid = nc.dram_tensor("resid", [SEQS, L, D], F32, kind="ExternalInput")
    inwT = nc.dram_tensor("inwT", [D, E], F32, kind="ExternalInput")
    skipwT = nc.dram_tensor("skipwT", [D, E], F32, kind="ExternalInput")
    outwT = nc.dram_tensor("outwT", [E, D], F32, kind="ExternalInput")
    wd1T = nc.dram_tensor("wd1T", [E, R], F32, kind="ExternalInput")
    wd2T = nc.dram_tensor("wd2T", [R, E], F32, kind="ExternalInput")
    wBT = nc.dram_tensor("wBT", [E, NS], F32, kind="ExternalInput")
    wCT = nc.dram_tensor("wCT", [E, NS], F32, kind="ExternalInput")
    gwT = nc.dram_tensor("gwT", [D, HID], I32, kind="ExternalInput")
    uwT = nc.dram_tensor("uwT", [D, HID], I32, kind="ExternalInput")
    dwT = nc.dram_tensor("dwT", [HID, D], I32, kind="ExternalInput")
    s_gate = nc.dram_tensor("s_gate", [1, HID], F32, kind="ExternalInput")
    s_up = nc.dram_tensor("s_up", [1, HID], F32, kind="ExternalInput")
    s_down = nc.dram_tensor("s_down", [1, D], F32, kind="ExternalInput")
    normw_in = nc.dram_tensor("normw", [P, KD], F32, kind="ExternalInput")
    convw_in = nc.dram_tensor("convw", [P, NE * KCONV], F32, kind="ExternalInput")
    convb_in = nc.dram_tensor("convb", [P, NE], F32, kind="ExternalInput")
    wd2b_in = nc.dram_tensor("wd2b", [P, NE], F32, kind="ExternalInput")
    WD_in = nc.dram_tensor("WD", [P, NE], F32, kind="ExternalInput")
    Alog_in = nc.dram_tensor("Alog", [P, NE * NS], F32, kind="ExternalInput")
    out = nc.dram_tensor("out", [SEQS, L, D], F32, kind="ExternalOutput")

    sc_in = nc.dram_tensor("sc_in", [NE, P, KD * P], BF16)
    sc_skip = nc.dram_tensor("sc_skip", [NE, P, KD * P], BF16)
    sc_out = nc.dram_tensor("sc_out", [NE, P, D], BF16)
    sc_g = nc.dram_tensor("sc_g", [KH, P, KD * P], BF16)
    sc_u = nc.dram_tensor("sc_u", [KH, P, KD * P], BF16)
    sc_d = nc.dram_tensor("sc_d", [KH, P, D], BF16)
    sc_bc = nc.dram_tensor("sc_bc", [2 * NS, T], BF16)
    sc_wd = nc.dram_tensor("sc_wd", [2, P, E], BF16)
    sc_x1 = nc.dram_tensor("sc_x1", [TSB, P, D], F32)

    from contextlib import ExitStack
    with ExitStack() as ctx:
        En = ctx.enter_context

        def sbt(name, shape, dt=BF16):
            return En(nc.sbuf_tensor(name, shape, dt))

        consts = sbt("consts", [P, 3328])
        ident = consts[:, 0:256].bitcast(F32)
        A_sb = consts[:, 256:1280].bitcast(F32)
        convw_sb = consts[:, 1280:1536].bitcast(F32)
        convb_sb = consts[:, 1536:1600].bitcast(F32)
        wd2b_sb = consts[:, 1600:1664].bitcast(F32)
        WD_sb = consts[:, 1664:1728].bitcast(F32)
        normw_sb = consts[:, 1728:1760].bitcast(F32)
        cols = consts[:, 1760:1792].bitcast(F32)
        cst = consts[:, 1792:1800].bitcast(F32)
        wB_sb = consts[:, 1800:2312]
        wC_sb = consts[:, 2312:2824]
        ones1 = consts[0:1, 2824:2952]
        ones1f = consts[0:1, 2952:3208].bitcast(F32)
        tails = consts[:, 3208:3304]

        x_fm = sbt("x_fm", [P, NE * T])
        dy_fm = sbt("dy_fm", [P, NE * T])
        z_x = x_fm[:, 0:4096].bitcast(F32)
        rn_fm = sbt("rn_fm", [P, KD * T])
        rn_tm = sbt("rn_tm", [P, D], F32)
        wbuf = sbt("wbuf", [P, 2, KD * P])
        dbuf = sbt("dbuf", [P, 2, D])

        arena = sbt("arena", [P, 49728])
        Brep = arena[:, 0:NS * T]
        Crep = arena[:, NS * T:2 * NS * T]
        h_all = arena[:, 16384:16384 + NS * T]
        i2 = arena[:, 24576:24576 + T]
        d_buf = arena[:, 25088:25088 + 2 * T]
        carry = arena[:, 26112:26112 + 2 * NE * NS].bitcast(F32)
        tap = arena[:, 27136:27136 + 4 * T]
        brow = arena[0:1, 27136:27136 + 4 * T]
        bc_nt = arena[0:2 * NS, 29184:29184 + T]
        bc_t_f = arena[:, 29696:29760].bitcast(F32)
        gt_b = arena[:, 29760:29760 + T // 2]
        up_b = arena[:, 30016:30016 + T // 2]
        x1_fm = arena[:, 30272:30272 + KD * T]
        gu_half = arena[:, 38464:38464 + KH * (T // 2)]
        z_ln2 = arena[:, 38464 + 7168:38464 + 11264].bitcast(F32)
        rms_f = arena[:, 38464:38464 + 2 * D].bitcast(F32)
        cpair = arena[:, 38464:38464 + 4096].bitcast(F32)
        conv_f = arena[:, 42560:43584].bitcast(F32)
        xin_buf = arena[:, 43584:43584 + KCONV - 1 + T]
        t1_sb = arena[:, 44168:44168 + T]
        wtile32 = arena[:, 0:2 * E].bitcast(F32)
        wq32 = arena[:, 0:2 * HID].bitcast(I32)
        wtile16 = arena[:, 2 * HID:3 * HID]
        srow = arena[0:1, 16896:16896 + 2 * HID].bitcast(F32)
        srep = arena[:, 28160:28160 + 2 * HID].bitcast(F32)

        ps = [En(nc.psum_tensor(f"ps{i}", [P, 512], F32)) for i in range(8)]

        sems = {}
        for name in DMA_SEMS + ENG_SEMS:
            sems[name] = En(nc.semaphore(name))

        block = En(nc.Block())
        plan = {k: [] for k in ["sync", "gpsimd", "tensor", "vector", "scalar"]}
        n = {k: 0 for k in sems}
        cur_scope = [None]

        def emit(eng, fn):
            plan[eng].append((cur_scope[0], fn))

        def wait(eng, sem, val):
            if val is None or val <= 0:
                return
            emit(eng, lambda e, sem=sem, val=val: e.wait_ge(sems[sem], val))

        def dma(eng, sem, dst, src):
            n[sem] += 16
            v = n[sem]
            emit(eng, lambda e, dst=dst, src=src, sem=sem:
                 e.dma_start(out=dst, in_=src).then_inc(sems[sem], 16))
            return v

        def op(eng, fn, sem=None):
            if sem is None:
                emit(eng, fn)
                return None
            n[sem] += 1
            v = n[sem]
            emit(eng, lambda e, fn=fn, sem=sem: fn(e).then_inc(sems[sem], 1))
            return v

        bank_free = {i: ("ev", 0) for i in range(8)}

        def _evs(v):
            return ("ev", v)

        def _vvs(v):
            return ("vv", v)

        def mm_group(psb, tiles, extra_wait=None):
            first = True
            nt = len(tiles)
            res = None
            for j, (lf, rf, of) in enumerate(tiles):
                if first:
                    wait("tensor", bank_free[psb][0], bank_free[psb][1])
                    if extra_wait is not None:
                        wait("tensor", extra_wait[0], extra_wait[1])
                    first = False
                st, sp = (j == 0), (j == nt - 1)
                f = (lambda e, lf=lf, rf=rf, of=of, st=st, sp=sp:
                     e.matmul(of(), lf(), rf(), start=st, stop=sp))
                if sp:
                    res = op("tensor", f, "pe")
                else:
                    op("tensor", f)
            return res

        ev_ms = op("gpsimd", lambda e: e.memset(ident[:], 0.0), "sc")
        wait("gpsimd", "sc", ev_ms)
        ev_ident = op("gpsimd", lambda e: e.affine_select(
            out=ident[:], in_=ident[:], compare_op=OP.not_equal, fill=1.0,
            base=0, pattern=[[-1, P]], channel_multiplier=1), "sc")
        ev_ones = op("gpsimd", lambda e: e.memset(ones1[:], 1.0), "sc")
        ev_onesf = op("gpsimd", lambda e: e.memset(ones1f[:], 1.0), "sc")
        op("gpsimd", lambda e: e.memset(cst[:, 0:1], -128.0))
        op("gpsimd", lambda e: e.memset(cst[:, 1:2], EPS))
        op("gpsimd", lambda e: e.memset(cst[:, 2:3], -0.5))
        ev_cst = op("gpsimd", lambda e: e.memset(cst[:, 3:4], 1.0 / D), "sc")

        dma("sync", "qw0", normw_sb[:], normw_in[:])
        dma("sync", "qw0", convw_sb[:], convw_in[:])
        dma("sync", "qw0", convb_sb[:], convb_in[:])
        dma("sync", "qw0", wd2b_sb[:], wd2b_in[:])
        dma("sync", "qw0", WD_sb[:], WD_in[:])
        v = dma("sync", "qw0", A_sb[:], Alog_in[:])
        wait("sync", "qw0", v)
        wait("scalar", "qw0", v)
        ev_aexp = op("scalar", lambda e: e.activation(
            out=A_sb[:], in_=A_sb[:], func=AF.Exp), "ac")
        wait("vector", "ac", ev_aexp)
        ev_A = op("vector", lambda e: e.tensor_scalar(
            out=A_sb[:], in0=A_sb[:], scalar1=-1.0, scalar2=None, op0=OP.mult), "dv")

        for c in range(NE):
            dma("sync", "qw0", wtile32[:, c * R:(c + 1) * R], wd1T[c * P:(c + 1) * P, :])
        wait("sync", "qw0", n["qw0"])
        wait("vector", "qw0", n["qw0"])
        ewd1 = op("vector", lambda e: e.tensor_copy(
            out=wtile16[:, 0:NE * R], in_=wtile32[:, 0:NE * R]), "dv")
        wait("gpsimd", "dv", ewd1)
        vx = dma("gpsimd", "qp", sc_wd[0, :, 0:NE * R], wtile16[:, 0:NE * R])
        wait("sync", "dv", ewd1)
        v = dma("sync", "qw0", wtile32[:], wd2T[:])
        wait("sync", "qw0", v)
        wait("vector", "qw0", v)
        wait("vector", "qp", vx)
        ewd2 = op("vector", lambda e: e.tensor_copy(
            out=wtile16[:, 0:E], in_=wtile32[:]), "dv")
        wait("gpsimd", "dv", ewd2)
        vx = dma("gpsimd", "qp", sc_wd[1], wtile16[:, 0:E])
        wait("sync", "dv", ewd2)
        for c in range(NE):
            dma("sync", "qw0", wtile32[:, c * NS:(c + 1) * NS], wBT[c * P:(c + 1) * P, :])
        wait("sync", "qw0", n["qw0"])
        wait("vector", "qw0", n["qw0"])
        ebc1 = op("vector", lambda e: e.tensor_copy(
            out=wB_sb[:], in_=wtile32[:, 0:NE * NS]), "dv")
        wait("sync", "dv", ebc1)
        for c in range(NE):
            dma("sync", "qw0", wtile32[:, c * NS:(c + 1) * NS], wCT[c * P:(c + 1) * P, :])
        wait("sync", "qw0", n["qw0"])
        wait("vector", "qw0", n["qw0"])
        ev_smallw = op("vector", lambda e: e.tensor_copy(
            out=wC_sb[:], in_=wtile32[:, 0:NE * NS]), "dv")
        wait("sync", "dv", ev_smallw)

        def prep_folded(src, scratch):
            for kd in range(KD):
                v = dma("sync", "qw0", wtile32[:], src[kd * P:(kd + 1) * P, :])
                wait("scalar", "qw0", v)
                wait("scalar", "qp", n["qp"])
                ec = op("scalar", lambda e, kd=kd: e.activation(
                    out=wtile16[:, 0:E], in_=wtile32[:], func=AF.Identity,
                    scale=normw_sb[:, kd:kd + 1]), "ac")
                wait("gpsimd", "ac", ec)
                for c in range(NE):
                    dma("gpsimd", "qp", scratch[c, :, kd * P:(kd + 1) * P],
                        wtile16[:, c * P:(c + 1) * P])
                wait("sync", "ac", ec)

        prep_folded(inwT, sc_in)
        prep_folded(skipwT, sc_skip)

        for c in range(NE):
            v = dma("sync", "qw0", wtile32[:, 0:D], outwT[c * P:(c + 1) * P, :])
            wait("scalar", "qw0", v)
            wait("scalar", "qp", n["qp"])
            ec = op("scalar", lambda e: e.activation(
                out=wtile16[:, 0:D], in_=wtile32[:, 0:D], func=AF.Identity), "ac")
            wait("gpsimd", "ac", ec)
            dma("gpsimd", "qp", sc_out[c], wtile16[:, 0:D])
            wait("sync", "ac", ec)

        prev_em = [0]

        def prep_q(src, srcscale, scratch, ncols, nm, interleave):
            v = dma("sync", "qw0", srow[:, 0:ncols], srcscale[:])
            wait("sync", "qw0", v)
            wait("tensor", "qw0", v)
            wait("tensor", "sc", ev_onesf)
            nb = (ncols + 511) // 512
            for b_ in range(nb):
                lo, hi = b_ * 512, min(ncols, (b_ + 1) * 512)
                wait("tensor", bank_free[7][0], bank_free[7][1])
                pv = op("tensor", lambda e, lo=lo, hi=hi: e.matmul(
                    ps[7][:, 0:hi - lo], ones1f[0:1, :], srow[0:1, lo:hi],
                    start=True, stop=True), "pe")
                wait("vector", "pe", pv)
                if b_ == 0:
                    wait("vector", "dv", prev_em[0])
                bank_free[7] = _vvs(op("vector", lambda e, lo=lo, hi=hi: e.tensor_copy(
                    out=srep[:, lo:hi], in_=ps[7][:, 0:hi - lo]), "vv"))
            nk = src.shape[0] // P
            for kd in range(nk):
                v = dma("sync", "qw0", wq32[:, 0:ncols], src[kd * P:(kd + 1) * P, :])
                wait("scalar", "qw0", v)
                wait("scalar", "qp", n["qp"])
                wait("scalar", "sc", ev_cst)
                ec = op("scalar", lambda e, ncols=ncols: e.activation(
                    out=wtile16[:, 0:ncols], in_=wq32[:, 0:ncols], func=AF.Identity,
                    bias=cst[:, 0:1]), "ac")
                wait("vector", "ac", ec)
                if kd == 0:
                    wait("vector", bank_free[7][0], bank_free[7][1])
                em = op("vector", lambda e, ncols=ncols: e.tensor_tensor(
                    out=wtile16[:, 0:ncols], in0=wtile16[:, 0:ncols],
                    in1=srep[:, 0:ncols], op=OP.mult), "dv")
                wait("gpsimd", "dv", em)
                if interleave:
                    for m in range(nm):
                        dma("gpsimd", "qp", scratch[m, :, kd * P:(kd + 1) * P],
                            wtile16[:, m * P:(m + 1) * P])
                else:
                    dma("gpsimd", "qp", scratch[kd], wtile16[:, 0:ncols])
                wait("sync", "ac", ec)
                prev_em[0] = em

        prep_q(gwT, s_gate, sc_g, HID, KH, True)
        prep_q(uwT, s_up, sc_u, HID, KH, True)
        prep_q(dwT, s_down, sc_d, D, KD, False)

        PREP_QP = n["qp"]
        wait("vector", "qp", PREP_QP)
        wait("scalar", "qp", PREP_QP)
        wait("tensor", "qp", PREP_QP)
        wait("gpsimd", "qp", PREP_QP)
        wait("gpsimd", "dv", n["dv"])
        op("gpsimd", lambda e: e.memset(carry[:], 0.0))
        ev_zero = op("gpsimd", lambda e: e.memset(tails[:], 0.0), "sc")

        guard_w = {0: 0, 1: 0}
        guard_d = {0: 0, 1: 0}
        scan_ev = []
        st_ = {"lastdma_rn_pe": 0, "x1_use_pe": 0, "ffn_par": 0, "prev_gu": 0,
               "prev_sil_g": 0}
        TH = T // 2

        def seq_of(mt):
            return mt // (L // T), (mt % (L // T)) * T

        def emit_ph1(mt):
            cur_scope[0] = f"m{mt}_ph1"
            sq, tok0 = seq_of(mt)
            e3_prev = None
            for ts in range(TSB):
                wait("gpsimd", "dv", n["dv"])
                va = dma("gpsimd", "qz", z_x[:],
                         resid[sq, tok0 + ts * P: tok0 + (ts + 1) * P, :])
                wait("scalar", "qz", va)
                if e3_prev:
                    wait("scalar", "dv", e3_prev)
                if mt == 0 and ts == 0:
                    wait("scalar", "sc", ev_cst)
                a1 = op("scalar", lambda e: e.activation(
                    out=rms_f[:], in_=z_x[:], func=AF.Square, accum_out=cols[:, 0:1]), "ac")
                wait("scalar", "ac", a1)
                a2 = op("scalar", lambda e: e.activation(
                    out=cols[:, 1:2], in_=cols[:, 0:1], func=AF.Ln,
                    scale=cst[:, 3:4], bias=cst[:, 1:2]), "ac")
                wait("scalar", "ac", a2)
                e2 = op("scalar", lambda e: e.activation(
                    out=cols[:, 2:3], in_=cols[:, 1:2], func=AF.Exp, scale=cst[:, 2:3]), "ac")
                wait("vector", "ac", e2)
                e3 = op("vector", lambda e: e.tensor_scalar(
                    out=rms_f[:], in0=z_x[:], scalar1=cols[:, 2:3], scalar2=None,
                    op0=OP.mult), "dv")
                e3_prev = e3
                wait("tensor", "dv", e3)
                if mt == 0 and ts == 0:
                    wait("tensor", "sc", ev_ident)
                for kd in range(KD):
                    psb = 2 + kd % 2
                    wait("tensor", bank_free[psb][0], bank_free[psb][1])
                    pv = op("tensor", lambda e, kd=kd, psb=psb: e.transpose(
                        ps[psb][:, 0:P], rms_f[:, kd * P:(kd + 1) * P], ident[:]), "pe")
                    wait("scalar", "pe", pv)
                    bank_free[psb] = _evs(op("scalar", lambda e, kd=kd, ts=ts, psb=psb:
                                             e.activation(
                        out=rn_fm[:, kd * T + ts * P: kd * T + (ts + 1) * P],
                        in_=ps[psb][:, 0:P], func=AF.Identity), "ev"))
            return n["ev"]

        def emit_ph2(mt, reset_ev, rn_ev):
            cur_scope[0] = f"m{mt}_ph2"
            if reset_ev:
                wait("vector", "sc", reset_ev)
            prev_sil = None
            for c in range(NE):
                par = c % 2
                qsem = f"qw{par}"
                wait("sync", "pe", guard_w[par])
                vw = dma("sync", qsem, wbuf[:, par], sc_in[c])
                wait("tensor", qsem, vw)
                psb = par
                pev = mm_group(psb, [
                    (lambda kd=kd, par=par: wbuf[:, par, kd * P:(kd + 1) * P],
                     lambda kd=kd: rn_fm[:, kd * T:(kd + 1) * T],
                     lambda psb=psb: ps[psb][:]) for kd in range(KD)],
                    extra_wait=("ev", rn_ev) if c < 2 else None)
                guard_w[par] = pev
                wait("scalar", "pe", pev)
                if prev_sil:
                    wait("scalar", "dv", prev_sil)
                eev = op("scalar", lambda e, psb=psb: e.activation(
                    out=xin_buf[:, KCONV - 1:KCONV - 1 + T], in_=ps[psb][:],
                    func=AF.Identity), "ev")
                bank_free[psb] = _evs(eev)
                wait("vector", "ev", eev)
                v1 = op("vector", lambda e, c=c: e.tensor_copy(
                    out=xin_buf[:, 0:KCONV - 1],
                    in_=tails[:, c * (KCONV - 1):(c + 1) * (KCONV - 1)]), "dv")
                wait("vector", "dv", v1)
                op("vector", lambda e, c=c: e.tensor_copy(
                    out=tails[:, c * (KCONV - 1):(c + 1) * (KCONV - 1)],
                    in_=xin_buf[:, T:T + KCONV - 1]))
                v2 = None
                for k in range(KCONV):
                    v2 = op("vector", lambda e, c=c, k=k: e.tensor_scalar(
                        out=tap[:, k * T:(k + 1) * T], in0=xin_buf[:, k:k + T],
                        scalar1=convw_sb[:, c * KCONV + k:c * KCONV + k + 1],
                        scalar2=None, op0=OP.mult), "dv" if k == KCONV - 1 else None)
                wait("vector", "dv", v2)
                op("vector", lambda e: e.tensor_tensor(
                    out=cpair[:, 0:T], in0=tap[:, 0:T], in1=tap[:, T:2 * T], op=OP.add))
                v3 = op("vector", lambda e: e.tensor_tensor(
                    out=cpair[:, T:2 * T], in0=tap[:, 2 * T:3 * T],
                    in1=tap[:, 3 * T:4 * T], op=OP.add), "dv")
                wait("vector", "dv", v3)
                edv = op("vector", lambda e: e.tensor_tensor(
                    out=conv_f[:, 0:T], in0=cpair[:, 0:T], in1=cpair[:, T:2 * T],
                    op=OP.add), "dv")
                prev_sil = edv
                wait("scalar", "dv", edv)
                op("scalar", lambda e, c=c: e.activation(
                    out=x_fm[:, c * T:(c + 1) * T], in_=conv_f[:, 0:T], func=AF.Silu,
                    bias=convb_sb[:, c:c + 1]), "ac")

        def emit_ph34(mt):
            cur_scope[0] = f"m{mt}_ph34"
            wdv = wbuf[:].rearrange("p a b -> p (a b)")
            wait("sync", "pe", guard_w[0])
            wait("sync", "pe", guard_w[1])
            vw1 = dma("sync", "qw0", wdv[:, 0:NE * P], sc_wd[0, :, 0:NE * P])
            wait("tensor", "qw0", vw1)
            X_AC = n["ac"]
            pev = mm_group(0, [
                (lambda c=c: wdv[:, c * P:(c + 1) * P],
                 lambda c=c: x_fm[:, c * T:(c + 1) * T],
                 lambda: ps[0][:]) for c in range(NE)],
                extra_wait=("ac", X_AC))
            guard_w[0] = pev
            guard_w[1] = pev
            wait("scalar", "pe", pev)
            e_t1 = op("scalar", lambda e: e.activation(
                out=t1_sb, in_=ps[0][:], func=AF.Identity), "ev")
            bank_free[0] = _evs(e_t1)
            wait("sync", "pe", pev)
            vw2 = dma("sync", "qw0", wdv[:, 0:E], sc_wd[1, :, 0:E])
            wait("tensor", "qw0", vw2)
            prev_ln = None
            for c in range(NE):
                psb = c % 2
                wait("tensor", bank_free[psb][0], bank_free[psb][1])
                if c == 0:
                    wait("tensor", "ev", e_t1)
                pev = op("tensor", lambda e, c=c, psb=psb: e.matmul(
                    ps[psb][:], wdv[:, c * P:(c + 1) * P], t1_sb,
                    start=True, stop=True), "pe")
                wait("scalar", "pe", pev)
                if prev_ln:
                    wait("scalar", "ev", prev_ln)
                a4 = op("scalar", lambda e, c=c, psb=psb: e.activation(
                    out=conv_f[:, 0:T], in_=ps[psb][:], func=AF.Exp,
                    bias=wd2b_sb[:, c:c + 1]), "ac")
                wait("scalar", "ac", a4)
                lv = op("scalar", lambda e, c=c: e.activation(
                    out=dy_fm[:, c * T:(c + 1) * T], in_=conv_f[:, 0:T],
                    func=AF.Ln, bias=1.0), "ev")
                bank_free[psb] = _evs(lv)
                prev_ln = lv
            guard_w[0] = n["pe"]
            guard_w[1] = n["pe"]

        def emit_ph5(mt):
            cur_scope[0] = f"m{mt}_ph5"
            X_AC = n["ac"]
            for ts in range(TSB):
                for wi, (wsb, psb, off) in enumerate([(wB_sb, 2, 0), (wC_sb, 3, NS)]):
                    pev = mm_group(psb, [
                        (lambda c=c, ts=ts: x_fm[:, c * T + ts * P: c * T + (ts + 1) * P],
                         lambda c=c, wsb=wsb: wsb[:, c * NS:(c + 1) * NS],
                         lambda psb=psb: ps[psb][:, 0:NS]) for c in range(NE)])
                    wait("scalar", "pe", pev)
                    bank_free[psb] = _evs(op("scalar", lambda e, psb=psb, off=off:
                                             e.activation(
                        out=bc_t_f[:, off:off + NS], in_=ps[psb][:, 0:NS],
                        func=AF.Identity), "ev"))
                wait("tensor", "ev", n["ev"])
                wait("tensor", bank_free[2][0], bank_free[2][1])
                pv = op("tensor", lambda e: e.transpose(
                    ps[2][0:2 * NS, 0:P], bc_t_f[:], ident[:]), "pe")
                wait("scalar", "pe", pv)
                bank_free[2] = _evs(op("scalar", lambda e, ts=ts: e.activation(
                    out=bc_nt[:, ts * P:(ts + 1) * P], in_=ps[2][0:2 * NS, 0:P],
                    func=AF.Identity), "ev"))
            wait("gpsimd", "ev", n["ev"])
            wait("gpsimd", "qb", n["qb"])
            dma("gpsimd", "qb", sc_bc[:], bc_nt[:])
            wait("gpsimd", "qb", n["qb"])
            BCDST = [Brep, Crep]
            wait("tensor", "sc", ev_ones)
            for g in range(8):
                dma("gpsimd", "qb", brow[:, 0:4 * T],
                    sc_bc[:].rearrange("a b -> (a b)")[None, g * 4 * T:(g + 1) * 4 * T])
                wait("tensor", "qb", n["qb"])
                for j in range(4):
                    nidx = (g * 4 + j) % NS
                    dst = BCDST[(g * 4 + j) // NS]
                    psb = 4 + (j % 2)
                    wait("tensor", bank_free[psb][0], bank_free[psb][1])
                    pv = op("tensor", lambda e, j=j, psb=psb: e.matmul(
                        ps[psb][:], ones1[0:1, :], brow[:, j * T:(j + 1) * T],
                        start=True, stop=True), "pe")
                    wait("scalar", "pe", pv)
                    bank_free[psb] = _evs(op("scalar", lambda e, nidx=nidx, dst=dst,
                                             psb=psb: e.activation(
                        out=dst[:, nidx * T:(nidx + 1) * T],
                        in_=ps[psb][:], func=AF.Identity), "ev"))
                if g < 7:
                    wait("gpsimd", "pe", n["pe"])

        def build_ffn_units(mt):
            sq, tok0 = seq_of(mt)
            units = []
            gu_ev = {}

            def gate_up_unit(th, kh):
                parw = (th * KH + kh) % 2
                rot = kh % 2
                bg, bu = (0, 1) if rot == 0 else (2, 3)
                qsw = f"qw{parw}"
                qsd = f"qd{parw}"
                wait("sync", "pe", guard_w[parw])
                vg = dma("sync", qsw, wbuf[:, parw], sc_g[kh])
                wait("sync", "pe", guard_d[parw])
                vu = dma("sync", qsd, dbuf[:, parw], sc_u[kh])
                wait("tensor", qsw, vg)
                gev = mm_group(bg, [
                    (lambda kd=kd, parw=parw: wbuf[:, parw, kd * P:(kd + 1) * P],
                     lambda kd=kd, th=th: x1_fm[:, kd * T + th * TH: kd * T + (th + 1) * TH],
                     lambda bg=bg: ps[bg][:, 0:TH]) for kd in range(KD)])
                guard_w[parw] = gev
                wait("tensor", qsd, vu)
                uev = mm_group(bu, [
                    (lambda kd=kd, parw=parw: dbuf[:, parw, kd * P:(kd + 1) * P],
                     lambda kd=kd, th=th: x1_fm[:, kd * T + th * TH: kd * T + (th + 1) * TH],
                     lambda bu=bu: ps[bu][:, 0:TH]) for kd in range(KD)])
                guard_d[parw] = uev
                st_["x1_use_pe"] = uev
                wait("scalar", "pe", gev)
                if st_["prev_gu"]:
                    wait("scalar", "dv", st_["prev_gu"])
                eg = op("scalar", lambda e, bg=bg: e.activation(
                    out=gt_b, in_=ps[bg][:, 0:TH], func=AF.Silu), "ev")
                bank_free[bg] = _evs(eg)
                wait("vector", "pe", uev)
                if st_["prev_gu"]:
                    wait("vector", "dv", st_["prev_gu"])
                eu = op("vector", lambda e, bu=bu: e.tensor_copy(
                    out=up_b, in_=ps[bu][:, 0:TH]), "vv")
                bank_free[bu] = _vvs(eu)
                wait("vector", "vv", eu)
                wait("vector", "ev", eg)
                g_ = op("vector", lambda e, kh=kh: e.tensor_tensor(
                    out=gu_half[:, kh * TH:(kh + 1) * TH], in0=gt_b, in1=up_b,
                    op=OP.mult), "dv")
                st_["prev_gu"] = g_
                gu_ev[(th, kh)] = g_

            def down_unit(th, kh):
                parw = kh % 2
                qsd = f"qd{parw}"
                wait("sync", "pe", guard_d[parw])
                vw = dma("sync", qsd, dbuf[:, parw], sc_d[kh])
                wait("tensor", qsd, vw)
                wait("tensor", "dv", gu_ev[(th, kh)])
                for j in range(2):
                    for dg in range(4):
                        psb = j * 4 + dg
                        stt, sp = (kh == 0), (kh == KH - 1)
                        if stt:
                            wait("tensor", bank_free[psb][0], bank_free[psb][1])
                        f = (lambda e, kh=kh, parw=parw, j=j, dg=dg, psb=psb,
                             stt=stt, sp=sp:
                             e.matmul(ps[psb][:],
                                      gu_half[:, kh * TH + j * P: kh * TH + (j + 1) * P],
                                      dbuf[:, parw, dg * 512:(dg + 1) * 512],
                                      start=stt, stop=sp))
                        if j == 1 and dg == 3:
                            op("tensor", f, "pe")
                        else:
                            op("tensor", f)
                guard_d[parw] = n["pe"]
                st_["down_pe"] = n["pe"]

            def ln2_unit(th, jt):
                tsl = 2 * th + jt
                DOWN_PE = st_["down_pe"]
                wait("gpsimd", "dv", n["dv"])
                wait("gpsimd", "qx", n["qx"])
                wait("gpsimd", "pe", DOWN_PE)
                va = dma("gpsimd", "qx", z_ln2[:], sc_x1[tsl])
                wait("vector", "qx", va)
                wait("vector", "pe", DOWN_PE)
                for dg in range(4):
                    psb = jt * 4 + dg
                    bank_free[psb] = _vvs(op("vector", lambda e, psb=psb, dg=dg:
                                             e.tensor_tensor(
                        out=z_ln2[:, dg * 512:(dg + 1) * 512],
                        in0=z_ln2[:, dg * 512:(dg + 1) * 512],
                        in1=ps[psb][:], op=OP.add), "vv"))
                wait("vector", bank_free[jt * 4 + 3][0], bank_free[jt * 4 + 3][1])
                zr = op("vector", lambda e: e.tensor_reduce(
                    out=cols[:, 3:4], in_=z_ln2[:], axis=AX.X, op=OP.add), "dv")
                wait("vector", "dv", zr)
                edv = op("vector", lambda e: e.tensor_scalar(
                    out=cols[:, 4:5], in0=cols[:, 3:4], scalar1=1.0 / D,
                    scalar2=None, op0=OP.mult), "dv")
                wait("scalar", "dv", edv)
                wait("scalar", "qo", n["qo"])
                ea = op("scalar", lambda e: e.activation(
                    out=rn_tm[:], in_=z_ln2[:], func=AF.Square,
                    accum_out=cols[:, 5:6]), "ac")
                wait("vector", "ac", ea)
                op("vector", lambda e: e.tensor_scalar(
                    out=cols[:, 6:7], in0=cols[:, 5:6], scalar1=1.0 / D,
                    scalar2=None, op0=OP.mult))
                c7 = op("vector", lambda e: e.tensor_scalar(
                    out=cols[:, 7:8], in0=cols[:, 4:5], scalar1=cols[:, 4:5],
                    scalar2=None, op0=OP.mult), "dv")
                wait("vector", "dv", c7)
                ev2 = op("vector", lambda e: e.tensor_tensor(
                    out=cols[:, 8:9], in0=cols[:, 6:7], in1=cols[:, 7:8],
                    op=OP.subtract), "dv")
                wait("scalar", "dv", ev2)
                a9 = op("scalar", lambda e: e.activation(
                    out=cols[:, 9:10], in_=cols[:, 8:9], func=AF.Ln,
                    bias=cst[:, 1:2]), "ac")
                wait("scalar", "ac", a9)
                er = op("scalar", lambda e: e.activation(
                    out=cols[:, 10:11], in_=cols[:, 9:10], func=AF.Exp,
                    scale=cst[:, 2:3]), "ac")
                wait("vector", "ac", er)
                c11 = op("vector", lambda e: e.tensor_scalar(
                    out=cols[:, 11:12], in0=cols[:, 4:5], scalar1=cols[:, 10:11],
                    scalar2=-1.0, op0=OP.mult, op1=OP.mult), "dv")
                wait("vector", "dv", c11)
                e5 = op("vector", lambda e: e.tensor_scalar(
                    out=rn_tm[:], in0=z_ln2[:], scalar1=cols[:, 10:11],
                    scalar2=cols[:, 11:12], op0=OP.mult, op1=OP.add), "dv")
                wait("gpsimd", "dv", e5)
                dma("gpsimd", "qo", out[sq, tok0 + tsl * P: tok0 + (tsl + 1) * P, :],
                    rn_tm[:])

            for th in range(2):
                for kh in range(KH):
                    units.append(lambda th=th, kh=kh: gate_up_unit(th, kh))
                for kh in range(KH):
                    units.append(lambda th=th, kh=kh: down_unit(th, kh))
                for jt in range(2):
                    units.append(lambda th=th, jt=jt: ln2_unit(th, jt))
            return units

        def emit_scan(mt, units, reset_ev):
            cur_scope[0] = f"m{mt}_ph6"
            wait("vector", "ev", n["ev"])
            wait("vector", "dv", ev_A)
            wait("scalar", "dv", ev_A)
            if reset_ev:
                wait("vector", "sc", reset_ev)
                wait("scalar", "sc", reset_ev)
            ui = 0
            nu = len(units)
            for c in range(NE):
                vdx = op("vector", lambda e, c=c: e.tensor_tensor(
                    out=tap[:, 0:T], in0=dy_fm[:, c * T:(c + 1) * T],
                    in1=x_fm[:, c * T:(c + 1) * T], op=OP.mult), "dv")
                prev_i = vdx
                for nn_ in range(NS):
                    kkg = len(scan_ev)
                    par = kkg % 2
                    if kkg >= 2:
                        wait("scalar", "dv", scan_ev[kkg - 2])
                    ad_v = op("scalar", lambda e, c=c, nn_=nn_, par=par: e.activation(
                        out=d_buf[:, par * T:(par + 1) * T],
                        in_=dy_fm[:, c * T:(c + 1) * T],
                        func=AF.Exp, scale=A_sb[:, c * NS + nn_:c * NS + nn_ + 1]), "ad")
                    wait("vector", "dv", prev_i if nn_ == 0 else scan_ev[-1])
                    iv = op("vector", lambda e, nn_=nn_: e.tensor_tensor(
                        out=i2[:], in0=tap[:, 0:T],
                        in1=Brep[:, nn_ * T:(nn_ + 1) * T], op=OP.mult), "dv")
                    wait("vector", "ad", ad_v)
                    wait("vector", "dv", iv)
                    sev = op("vector", lambda e, nn_=nn_, c=c, par=par:
                             e.tensor_tensor_scan(
                        out=h_all[:, nn_ * T:(nn_ + 1) * T],
                        data0=d_buf[:, par * T:(par + 1) * T],
                        data1=i2[:],
                        initial=carry[:, c * NS + nn_:c * NS + nn_ + 1],
                        op0=OP.mult, op1=OP.add), "dv")
                    scan_ev.append(sev)
                wait("vector", "dv", scan_ev[-1])
                vcc = op("vector", lambda e, c=c: e.tensor_copy(
                    out=carry[:, c * NS:(c + 1) * NS],
                    in_=h_all.rearrange("p (n t) -> p n t", n=NS)[:, :, T - 1]), "dv")
                wait("vector", "dv", vcc)
                vt0 = op("vector", lambda e: e.tensor_tensor(
                    out=h_all[:], in0=h_all[:], in1=Crep[:], op=OP.mult), "dv")
                wait("vector", "dv", vt0)
                vt1 = op("vector", lambda e: e.tensor_tensor(
                    out=h_all[:, 0:8 * T], in0=h_all[:, 0:8 * T],
                    in1=h_all[:, 8 * T:16 * T], op=OP.add), "dv")
                wait("vector", "dv", vt1)
                vt2 = op("vector", lambda e: e.tensor_tensor(
                    out=h_all[:, 0:4 * T], in0=h_all[:, 0:4 * T],
                    in1=h_all[:, 4 * T:8 * T], op=OP.add), "dv")
                wait("vector", "dv", vt2)
                vt3 = op("vector", lambda e: e.tensor_tensor(
                    out=h_all[:, 0:2 * T], in0=h_all[:, 0:2 * T],
                    in1=h_all[:, 2 * T:4 * T], op=OP.add), "dv")
                wait("vector", "dv", vt3)
                op("vector", lambda e, c=c: e.tensor_tensor(
                    out=dy_fm[:, c * T:(c + 1) * T], in0=h_all[:, 0:T],
                    in1=h_all[:, T:2 * T], op=OP.add), "dv")
                tgt = (c + 1) * nu // NE
                while ui < tgt:
                    units[ui]()
                    ui += 1
            return n["dv"]

        def emit_ph7(mt, SCAN_DV):
            cur_scope[0] = f"m{mt}_ph7"
            sskip = tap[:, T:2 * T]
            ytmp = tap[:, 2 * T:3 * T]
            prev_ym = None
            for c in range(NE):
                par = c % 2
                qsem = f"qw{par}"
                wait("sync", "pe", guard_w[par])
                vw = dma("sync", qsem, wbuf[:, par], sc_skip[c])
                wait("tensor", qsem, vw)
                psb = par
                pev = mm_group(psb, [
                    (lambda kd=kd, par=par: wbuf[:, par, kd * P:(kd + 1) * P],
                     lambda kd=kd: rn_fm[:, kd * T:(kd + 1) * T],
                     lambda psb=psb: ps[psb][:]) for kd in range(KD)],
                    extra_wait=("dv", SCAN_DV) if c < 2 else None)
                guard_w[par] = pev
                wait("scalar", "pe", pev)
                if prev_ym:
                    wait("scalar", "dv", prev_ym)
                esk = op("scalar", lambda e, psb=psb: e.activation(
                    out=sskip, in_=ps[psb][:], func=AF.Silu), "ev")
                bank_free[psb] = _evs(esk)
                wait("vector", "ev", esk)
                y1 = op("vector", lambda e, c=c: e.tensor_scalar(
                    out=ytmp, in0=x_fm[:, c * T:(c + 1) * T],
                    scalar1=WD_sb[:, c:c + 1], scalar2=None, op0=OP.mult), "dv")
                wait("vector", "dv", y1)
                y2 = op("vector", lambda e, c=c: e.tensor_tensor(
                    out=ytmp, in0=ytmp, in1=dy_fm[:, c * T:(c + 1) * T], op=OP.add), "dv")
                wait("vector", "dv", y2)
                prev_ym = op("vector", lambda e, c=c: e.tensor_tensor(
                    out=dy_fm[:, c * T:(c + 1) * T], in0=ytmp, in1=sskip,
                    op=OP.mult), "dv")
            return n["dv"]

        def emit_ph8(mt, Y_DV):
            cur_scope[0] = f"m{mt}_ph8"
            sq, tok0 = seq_of(mt)
            for half in range(2):
                for c in range(NE):
                    par = c % 2
                    qsem = f"qw{par}"
                    wait("sync", "pe", guard_w[par])
                    vw = dma("sync", qsem, wbuf[:, par], sc_out[c])
                    wait("tensor", qsem, vw)
                    if half == 0 and c < 2:
                        wait("tensor", "dv", Y_DV)
                    for j in range(8):
                        tsl = 2 * half + j // 4
                        dg = j % 4
                        psb = j
                        stt, sp = (c == 0), (c == NE - 1)
                        if stt:
                            wait("tensor", bank_free[psb][0], bank_free[psb][1])
                        f = (lambda e, c=c, par=par, tsl=tsl, dg=dg, psb=psb, stt=stt,
                             sp=sp:
                             e.matmul(ps[psb][:],
                                      dy_fm[:, c * T + tsl * P: c * T + (tsl + 1) * P],
                                      wbuf[:, par, dg * 512:(dg + 1) * 512],
                                      start=stt, stop=sp))
                        if j == 7:
                            op("tensor", f, "pe")
                        else:
                            op("tensor", f)
                    guard_w[par] = n["pe"]
                OUT_PE = n["pe"]
                for jt in range(2):
                    tsl = 2 * half + jt
                    wait("gpsimd", "dv", n["dv"])
                    va = dma("gpsimd", "qz", z_x[:],
                             resid[sq, tok0 + tsl * P: tok0 + (tsl + 1) * P, :])
                    wait("vector", "qz", va)
                    wait("vector", "pe", OUT_PE)
                    z0 = op("vector", lambda e: e.tensor_scalar(
                        out=z_x[:], in0=z_x[:], scalar1=2.0, scalar2=None,
                        op0=OP.mult), "dv")
                    wait("vector", "dv", z0)
                    for dg in range(4):
                        psb = jt * 4 + dg
                        bank_free[psb] = _vvs(op("vector", lambda e, psb=psb, dg=dg:
                                                 e.tensor_tensor(
                            out=z_x[:, dg * 512:(dg + 1) * 512],
                            in0=z_x[:, dg * 512:(dg + 1) * 512],
                            in1=ps[psb][:], op=OP.add), "vv"))
                    wait("vector", bank_free[jt * 4 + 3][0], bank_free[jt * 4 + 3][1])
                    zr = op("vector", lambda e: e.tensor_reduce(
                        out=cols[:, 3:4], in_=z_x[:], axis=AX.X, op=OP.add), "dv")
                    wait("vector", "dv", zr)
                    edv = op("vector", lambda e: e.tensor_scalar(
                        out=cols[:, 4:5], in0=cols[:, 3:4], scalar1=1.0 / D,
                        scalar2=None, op0=OP.mult), "dv")
                    wait("scalar", "dv", edv)
                    wait("scalar", "qx", n["qx"])
                    wait("scalar", "qo", n["qo"])
                    wait("scalar", "pe", st_["lastdma_rn_pe"])
                    ea = op("scalar", lambda e: e.activation(
                        out=rn_tm[:], in_=z_x[:], func=AF.Square,
                        accum_out=cols[:, 5:6]), "ac")
                    wait("vector", "ac", ea)
                    op("vector", lambda e: e.tensor_scalar(
                        out=cols[:, 6:7], in0=cols[:, 5:6], scalar1=1.0 / D,
                        scalar2=None, op0=OP.mult))
                    c7 = op("vector", lambda e: e.tensor_scalar(
                        out=cols[:, 7:8], in0=cols[:, 4:5], scalar1=cols[:, 4:5],
                        scalar2=None, op0=OP.mult), "dv")
                    wait("vector", "dv", c7)
                    ev2 = op("vector", lambda e: e.tensor_tensor(
                        out=cols[:, 8:9], in0=cols[:, 6:7], in1=cols[:, 7:8],
                        op=OP.subtract), "dv")
                    wait("scalar", "dv", ev2)
                    a9 = op("scalar", lambda e: e.activation(
                        out=cols[:, 9:10], in_=cols[:, 8:9], func=AF.Ln,
                        bias=cst[:, 1:2]), "ac")
                    wait("scalar", "ac", a9)
                    er = op("scalar", lambda e: e.activation(
                        out=cols[:, 10:11], in_=cols[:, 9:10], func=AF.Exp,
                        scale=cst[:, 2:3]), "ac")
                    wait("vector", "ac", er)
                    c11 = op("vector", lambda e: e.tensor_scalar(
                        out=cols[:, 11:12], in0=cols[:, 4:5], scalar1=cols[:, 10:11],
                        scalar2=-1.0, op0=OP.mult, op1=OP.mult), "dv")
                    wait("vector", "dv", c11)
                    e4 = op("vector", lambda e: e.tensor_scalar(
                        out=rn_tm[:], in0=z_x[:], scalar1=cols[:, 10:11],
                        scalar2=cols[:, 11:12], op0=OP.mult, op1=OP.add), "dv")
                    wait("gpsimd", "dv", e4)
                    wait("gpsimd", "qx", n["qx"])
                    dma("gpsimd", "qx", sc_x1[tsl], rn_tm[:])
                    wait("tensor", "dv", e4)
                    if half == 0 and jt == 0:
                        wait("scalar", "pe", st_["x1_use_pe"])
                    for kd in range(KD):
                        psb = kd % 2
                        wait("tensor", bank_free[psb][0], bank_free[psb][1])
                        pv = op("tensor", lambda e, kd=kd, psb=psb: e.transpose(
                            ps[psb][:, 0:P], rn_tm[:, kd * P:(kd + 1) * P], ident[:]), "pe")
                        wait("scalar", "pe", pv)
                        bank_free[psb] = _evs(op("scalar", lambda e, kd=kd, tsl=tsl,
                                                 psb=psb: e.activation(
                            out=x1_fm[:, kd * T + tsl * P: kd * T + (tsl + 1) * P],
                            in_=ps[psb][:, 0:P], func=AF.Identity), "ev"))
                    st_["lastdma_rn_pe"] = n["pe"]

        rn_ev0 = emit_ph1(0)
        emit_ph2(0, ev_zero, rn_ev0)
        emit_ph34(0)
        emit_ph5(0)
        prev_units = []
        pending_reset = ev_zero
        for mt in range(MT):
            SCAN_DV = emit_scan(mt, prev_units, pending_reset)
            Y_DV = emit_ph7(mt, SCAN_DV)
            emit_ph8(mt, Y_DV)
            prev_units = build_ffn_units(mt)
            if mt + 1 < MT:
                nxt = mt + 1
                if nxt % (L // T) == 0:
                    wait("gpsimd", "dv", n["dv"])
                    op("gpsimd", lambda e: e.memset(carry[:], 0.0))
                    reset_ev = op("gpsimd", lambda e: e.memset(tails[:], 0.0), "sc")
                else:
                    reset_ev = None
                pending_reset = reset_ev
                rn_evn = emit_ph1(nxt)
                emit_ph2(nxt, reset_ev, rn_evn)
                emit_ph34(nxt)
                emit_ph5(nxt)
        cur_scope[0] = "tail_ffn"
        for u in prev_units:
            u()

        wait("gpsimd", "qo", n["qo"])
        wait("sync", "qo", n["qo"])

        from contextlib import nullcontext

        def register(engname):
            def body(eng):
                for scope, f in plan[engname]:
                    cm = nc.named_scope(scope) if scope else nullcontext()
                    with cm:
                        f(eng)
            return body

        block.sync(register("sync"))
        block.gpsimd(register("gpsimd"))
        block.tensor(register("tensor"))
        block.vector(register("vector"))
        block.scalar(register("scalar"))

    return nc


def kernel(**inputs):
    h = _prep_inputs(inputs)
    resid = np.ascontiguousarray(np.asarray(inputs["L0_attn_resid"], dtype=np.float32))
    nc = build()
    in_maps = []
    for core in range(NCORES):
        in_maps.append({
            "resid": resid[core * SEQS:(core + 1) * SEQS],
            "inwT": h["inwT"], "skipwT": h["skipwT"], "outwT": h["outwT"],
            "wd1T": h["wd1T"], "wd2T": h["wd2T"], "wBT": h["wBT"], "wCT": h["wCT"],
            "gwT": h["gwT"], "uwT": h["uwT"], "dwT": h["dwT"],
            "s_gate": h["s_gate"], "s_up": h["s_up"], "s_down": h["s_down"],
            "normw": h["normw"], "convw": h["convw"], "convb": h["convb"],
            "wd2b": h["wd2b"], "WD": h["WD"], "Alog": h["Alog"],
        })
    res = run_bass_kernel_spmd(nc, in_maps, core_ids=list(range(NCORES)))
    outs = [np.asarray(r["out"]) for r in res.results]
    return np.concatenate(outs, axis=0).reshape(16, L, D).astype(np.float32)


# version 04
# speedup vs baseline: 1.0702x; 1.0702x over previous
import numpy as np
import concourse.bass as bass
import concourse.mybir as mybir
from concourse.bass_utils import run_bass_kernel_spmd

F32 = mybir.dt.float32
BF16 = mybir.dt.bfloat16
I32 = mybir.dt.int32
AF = mybir.ActivationFunctionType
OP = mybir.AluOpType
AX = mybir.AxisListType

P = 128
D = 2048
E = 4096
HID = 5632
R = 128
NS = 16
KCONV = 4
SEQS = 2
L = 2048
T = 512
MT = SEQS * (L // T)
TSB = T // P
KD = D // P
NE = E // P
KH = HID // P
EPS = 1e-5
NCORES = 8

DMA_SEMS = ["qw0", "qw1", "qd0", "qd1", "qz", "qo", "qx", "qb", "qp"]
ENG_SEMS = ["pe", "ev", "vv", "dv", "ac", "sc", "ad"]


def _rearr_col(v, nchunk):
    return np.ascontiguousarray(np.asarray(v).reshape(nchunk, P).T).astype(np.float32)


def _prep_inputs(inp):
    asc = np.ascontiguousarray
    h = {}
    h["inwT"] = asc(inp["L0_attn_in_w"].T)
    h["skipwT"] = asc(inp["L0_attn_skip_w"].T)
    h["outwT"] = asc(inp["L0_attn_out_w"].T)
    h["wd1T"] = asc(inp["L0_attn_wd1"].T)
    h["wd2T"] = asc(inp["L0_attn_wd2"].T)
    h["wBT"] = asc(inp["L0_attn_wB"].T)
    h["wCT"] = asc(inp["L0_attn_wC"].T)
    h["gwT"] = asc(inp["L0_ffn_wq_gate"].T)
    h["uwT"] = asc(inp["L0_ffn_wq_up"].T)
    h["dwT"] = asc(inp["L0_ffn_wq_down"].T)
    h["s_gate"] = asc(np.asarray(inp["L0_ffn_s_gate"], np.float32).reshape(1, HID))
    h["s_up"] = asc(np.asarray(inp["L0_ffn_s_up"], np.float32).reshape(1, HID))
    h["s_down"] = asc(np.asarray(inp["L0_ffn_s_down"], np.float32).reshape(1, D))
    h["normw"] = _rearr_col(inp["L0_attn_norm_w"], KD)
    h["convw"] = asc(np.asarray(inp["L0_attn_conv_w"], np.float32).reshape(NE, P, KCONV)
                     .transpose(1, 0, 2).reshape(P, NE * KCONV))
    h["convb"] = _rearr_col(inp["L0_attn_conv_b"], NE)
    h["wd2b"] = _rearr_col(inp["L0_attn_wd2_b"], NE)
    h["WD"] = _rearr_col(inp["L0_attn_W_D"], NE)
    h["Alog"] = asc(np.asarray(inp["L0_attn_A_log"], np.float32).reshape(NE, P, NS)
                    .transpose(1, 0, 2).reshape(P, NE * NS))
    for k in ["inwT", "skipwT", "outwT", "wd1T", "wd2T", "wBT", "wCT"]:
        h[k] = np.asarray(h[k], np.float32)
    for k in ["gwT", "uwT", "dwT"]:
        h[k] = np.asarray(h[k], np.int32)
    return h


def build():
    nc = bass.Bass(target_bir_lowering=False)

    resid = nc.dram_tensor("resid", [SEQS, L, D], F32, kind="ExternalInput")
    inwT = nc.dram_tensor("inwT", [D, E], F32, kind="ExternalInput")
    skipwT = nc.dram_tensor("skipwT", [D, E], F32, kind="ExternalInput")
    outwT = nc.dram_tensor("outwT", [E, D], F32, kind="ExternalInput")
    wd1T = nc.dram_tensor("wd1T", [E, R], F32, kind="ExternalInput")
    wd2T = nc.dram_tensor("wd2T", [R, E], F32, kind="ExternalInput")
    wBT = nc.dram_tensor("wBT", [E, NS], F32, kind="ExternalInput")
    wCT = nc.dram_tensor("wCT", [E, NS], F32, kind="ExternalInput")
    gwT = nc.dram_tensor("gwT", [D, HID], I32, kind="ExternalInput")
    uwT = nc.dram_tensor("uwT", [D, HID], I32, kind="ExternalInput")
    dwT = nc.dram_tensor("dwT", [HID, D], I32, kind="ExternalInput")
    s_gate = nc.dram_tensor("s_gate", [1, HID], F32, kind="ExternalInput")
    s_up = nc.dram_tensor("s_up", [1, HID], F32, kind="ExternalInput")
    s_down = nc.dram_tensor("s_down", [1, D], F32, kind="ExternalInput")
    normw_in = nc.dram_tensor("normw", [P, KD], F32, kind="ExternalInput")
    convw_in = nc.dram_tensor("convw", [P, NE * KCONV], F32, kind="ExternalInput")
    convb_in = nc.dram_tensor("convb", [P, NE], F32, kind="ExternalInput")
    wd2b_in = nc.dram_tensor("wd2b", [P, NE], F32, kind="ExternalInput")
    WD_in = nc.dram_tensor("WD", [P, NE], F32, kind="ExternalInput")
    Alog_in = nc.dram_tensor("Alog", [P, NE * NS], F32, kind="ExternalInput")
    out = nc.dram_tensor("out", [SEQS, L, D], F32, kind="ExternalOutput")

    sc_in = nc.dram_tensor("sc_in", [NE, P, KD * P], BF16)
    sc_skip = nc.dram_tensor("sc_skip", [NE, P, KD * P], BF16)
    sc_out = nc.dram_tensor("sc_out", [NE, P, D], BF16)
    sc_g = nc.dram_tensor("sc_g", [KH, P, KD * P], BF16)
    sc_u = nc.dram_tensor("sc_u", [KH, P, KD * P], BF16)
    sc_d = nc.dram_tensor("sc_d", [KH, P, D], BF16)
    sc_bc = nc.dram_tensor("sc_bc", [2 * NS, T], BF16)
    sc_wd = nc.dram_tensor("sc_wd", [2, P, E], BF16)
    sc_x1 = nc.dram_tensor("sc_x1", [TSB, P, D], F32)

    from contextlib import ExitStack
    with ExitStack() as ctx:
        En = ctx.enter_context

        def sbt(name, shape, dt=BF16):
            return En(nc.sbuf_tensor(name, shape, dt))

        consts = sbt("consts", [P, 3328])
        ident = consts[:, 0:256].bitcast(F32)
        A_sb = consts[:, 256:1280].bitcast(F32)
        convw_sb = consts[:, 1280:1536].bitcast(F32)
        convb_sb = consts[:, 1536:1600].bitcast(F32)
        wd2b_sb = consts[:, 1600:1664].bitcast(F32)
        WD_sb = consts[:, 1664:1728].bitcast(F32)
        normw_sb = consts[:, 1728:1760].bitcast(F32)
        cols = consts[:, 1760:1792].bitcast(F32)
        cst = consts[:, 1792:1808].bitcast(F32)
        wB_sb = consts[:, 1808:2320]
        wC_sb = consts[:, 2320:2832]
        ones1 = consts[0:1, 2832:2960]
        ones1f = consts[0:1, 2960:3216].bitcast(F32)
        tails = consts[:, 3216:3312]

        x_fm = sbt("x_fm", [P, NE * T])
        dy_fm = sbt("dy_fm", [P, NE * T])
        z_x = x_fm[:, 0:4096].bitcast(F32)
        rn_fm = sbt("rn_fm", [P, KD * T])
        rn_tm = sbt("rn_tm", [P, D], F32)
        wbuf = sbt("wbuf", [P, 2, KD * P])
        dbuf = sbt("dbuf", [P, 2, D])

        arena = sbt("arena", [P, 49728])
        Brep = arena[:, 0:NS * T]
        Crep = arena[:, NS * T:2 * NS * T]
        h_all = arena[:, 16384:16384 + NS * T]
        i2 = arena[:, 24576:24576 + T]
        d_buf = arena[:, 25088:25088 + 2 * T]
        carry = arena[:, 26112:26112 + 2 * NE * NS].bitcast(F32)
        tap = arena[:, 27136:27136 + 4 * T]
        brow = arena[0:1, 27136:27136 + 4 * T]
        bc_nt = arena[0:2 * NS, 29184:29184 + T]
        bc_t_f = arena[:, 29696:29760].bitcast(F32)
        gt_b = arena[:, 29760:29760 + T // 2]
        up_b = arena[:, 30016:30016 + T // 2]
        x1_fm = arena[:, 30272:30272 + KD * T]
        gu_half = arena[:, 38464:38464 + KH * (T // 2)]
        z_ln2 = arena[:, 38464 + 7168:38464 + 11264].bitcast(F32)
        rms_f = arena[:, 38464:38464 + 2 * D].bitcast(F32)
        cpair = arena[:, 38464:38464 + 4096].bitcast(F32)
        conv_f = arena[:, 42560:43584].bitcast(F32)
        xin_buf = arena[:, 43584:43584 + KCONV - 1 + T]
        t1_sb = arena[:, 44168:44168 + T]
        wtile32 = arena[:, 0:2 * E].bitcast(F32)
        wq32 = arena[:, 0:2 * HID].bitcast(I32)
        wtile16 = arena[:, 2 * HID:3 * HID]
        srow = arena[0:1, 16896:16896 + 2 * HID].bitcast(F32)
        srep = arena[:, 28160:28160 + 2 * HID].bitcast(F32)

        ps = [En(nc.psum_tensor(f"ps{i}", [P, 512], F32)) for i in range(8)]

        sems = {}
        for name in DMA_SEMS + ENG_SEMS:
            sems[name] = En(nc.semaphore(name))

        block = En(nc.Block())
        plan = {k: [] for k in ["sync", "gpsimd", "tensor", "vector", "scalar"]}
        n = {k: 0 for k in sems}
        cur_scope = [None]

        def emit(eng, fn):
            plan[eng].append((cur_scope[0], fn))

        def wait(eng, sem, val):
            if val is None or val <= 0:
                return
            emit(eng, lambda e, sem=sem, val=val: e.wait_ge(sems[sem], val))

        def dma(eng, sem, dst, src):
            n[sem] += 16
            v = n[sem]
            emit(eng, lambda e, dst=dst, src=src, sem=sem:
                 e.dma_start(out=dst, in_=src).then_inc(sems[sem], 16))
            return v

        def op(eng, fn, sem=None):
            if sem is None:
                emit(eng, fn)
                return None
            n[sem] += 1
            v = n[sem]
            emit(eng, lambda e, fn=fn, sem=sem: fn(e).then_inc(sems[sem], 1))
            return v

        bank_free = {i: ("ev", 0) for i in range(8)}

        def _evs(v):
            return ("ev", v)

        def _vvs(v):
            return ("vv", v)

        def mm_group(psb, tiles, extra_wait=None):
            first = True
            nt = len(tiles)
            res = None
            for j, (lf, rf, of) in enumerate(tiles):
                if first:
                    wait("tensor", bank_free[psb][0], bank_free[psb][1])
                    if extra_wait is not None:
                        wait("tensor", extra_wait[0], extra_wait[1])
                    first = False
                st, sp = (j == 0), (j == nt - 1)
                f = (lambda e, lf=lf, rf=rf, of=of, st=st, sp=sp:
                     e.matmul(of(), lf(), rf(), start=st, stop=sp))
                if sp:
                    res = op("tensor", f, "pe")
                else:
                    op("tensor", f)
            return res

        ev_ms = op("gpsimd", lambda e: e.memset(ident[:], 0.0), "sc")
        wait("gpsimd", "sc", ev_ms)
        ev_ident = op("gpsimd", lambda e: e.affine_select(
            out=ident[:], in_=ident[:], compare_op=OP.not_equal, fill=1.0,
            base=0, pattern=[[-1, P]], channel_multiplier=1), "sc")
        ev_ones = op("gpsimd", lambda e: e.memset(ones1[:], 1.0), "sc")
        ev_onesf = op("gpsimd", lambda e: e.memset(ones1f[:], 1.0), "sc")
        op("gpsimd", lambda e: e.memset(cst[:, 0:1], -128.0))
        op("gpsimd", lambda e: e.memset(cst[:, 1:2], EPS))
        op("gpsimd", lambda e: e.memset(cst[:, 2:3], -0.5))
        op("gpsimd", lambda e: e.memset(cst[:, 3:4], 1.0 / D))
        ev_cst = op("gpsimd", lambda e: e.memset(cst[:, 4:5], 0.5), "sc")

        dma("sync", "qw0", normw_sb[:], normw_in[:])
        dma("sync", "qw0", convw_sb[:], convw_in[:])
        dma("sync", "qw0", convb_sb[:], convb_in[:])
        dma("sync", "qw0", wd2b_sb[:], wd2b_in[:])
        dma("sync", "qw0", WD_sb[:], WD_in[:])
        v = dma("sync", "qw0", A_sb[:], Alog_in[:])
        wait("sync", "qw0", v)
        wait("scalar", "qw0", v)
        ev_aexp = op("scalar", lambda e: e.activation(
            out=A_sb[:], in_=A_sb[:], func=AF.Exp), "ac")
        wait("vector", "ac", ev_aexp)
        ev_A = op("vector", lambda e: e.tensor_scalar(
            out=A_sb[:], in0=A_sb[:], scalar1=-1.0, scalar2=None, op0=OP.mult), "dv")

        for c in range(NE):
            dma("sync", "qw0", wtile32[:, c * R:(c + 1) * R], wd1T[c * P:(c + 1) * P, :])
        wait("sync", "qw0", n["qw0"])
        wait("vector", "qw0", n["qw0"])
        ewd1 = op("vector", lambda e: e.tensor_copy(
            out=wtile16[:, 0:NE * R], in_=wtile32[:, 0:NE * R]), "dv")
        wait("gpsimd", "dv", ewd1)
        vx = dma("gpsimd", "qp", sc_wd[0, :, 0:NE * R], wtile16[:, 0:NE * R])
        wait("sync", "dv", ewd1)
        v = dma("sync", "qw0", wtile32[:], wd2T[:])
        wait("sync", "qw0", v)
        wait("vector", "qw0", v)
        wait("vector", "qp", vx)
        ewd2 = op("vector", lambda e: e.tensor_copy(
            out=wtile16[:, 0:E], in_=wtile32[:]), "dv")
        wait("gpsimd", "dv", ewd2)
        vx = dma("gpsimd", "qp", sc_wd[1], wtile16[:, 0:E])
        wait("sync", "dv", ewd2)
        for c in range(NE):
            dma("sync", "qw0", wtile32[:, c * NS:(c + 1) * NS], wBT[c * P:(c + 1) * P, :])
        wait("sync", "qw0", n["qw0"])
        wait("vector", "qw0", n["qw0"])
        ebc1 = op("vector", lambda e: e.tensor_copy(
            out=wB_sb[:], in_=wtile32[:, 0:NE * NS]), "dv")
        wait("sync", "dv", ebc1)
        for c in range(NE):
            dma("sync", "qw0", wtile32[:, c * NS:(c + 1) * NS], wCT[c * P:(c + 1) * P, :])
        wait("sync", "qw0", n["qw0"])
        wait("vector", "qw0", n["qw0"])
        ev_smallw = op("vector", lambda e: e.tensor_copy(
            out=wC_sb[:], in_=wtile32[:, 0:NE * NS]), "dv")
        wait("sync", "dv", ev_smallw)

        def prep_folded(src, scratch):
            for kd in range(KD):
                v = dma("sync", "qw0", wtile32[:], src[kd * P:(kd + 1) * P, :])
                wait("scalar", "qw0", v)
                wait("scalar", "qp", n["qp"])
                ec = op("scalar", lambda e, kd=kd: e.activation(
                    out=wtile16[:, 0:E], in_=wtile32[:], func=AF.Identity,
                    scale=normw_sb[:, kd:kd + 1]), "ac")
                wait("gpsimd", "ac", ec)
                for c in range(NE):
                    dma("gpsimd", "qp", scratch[c, :, kd * P:(kd + 1) * P],
                        wtile16[:, c * P:(c + 1) * P])
                wait("sync", "ac", ec)

        prep_folded(inwT, sc_in)
        prep_folded(skipwT, sc_skip)

        for c in range(NE):
            v = dma("sync", "qw0", wtile32[:, 0:D], outwT[c * P:(c + 1) * P, :])
            wait("scalar", "qw0", v)
            wait("scalar", "qp", n["qp"])
            ec = op("scalar", lambda e: e.activation(
                out=wtile16[:, 0:D], in_=wtile32[:, 0:D], func=AF.Identity), "ac")
            wait("gpsimd", "ac", ec)
            dma("gpsimd", "qp", sc_out[c], wtile16[:, 0:D])
            wait("sync", "ac", ec)

        prev_em = [0]

        def prep_q(src, srcscale, scratch, ncols, nm, interleave):
            v = dma("sync", "qw0", srow[:, 0:ncols], srcscale[:])
            wait("sync", "qw0", v)
            wait("tensor", "qw0", v)
            wait("tensor", "sc", ev_onesf)
            nb = (ncols + 511) // 512
            for b_ in range(nb):
                lo, hi = b_ * 512, min(ncols, (b_ + 1) * 512)
                wait("tensor", bank_free[7][0], bank_free[7][1])
                pv = op("tensor", lambda e, lo=lo, hi=hi: e.matmul(
                    ps[7][:, 0:hi - lo], ones1f[0:1, :], srow[0:1, lo:hi],
                    start=True, stop=True), "pe")
                wait("vector", "pe", pv)
                if b_ == 0:
                    wait("vector", "dv", prev_em[0])
                bank_free[7] = _vvs(op("vector", lambda e, lo=lo, hi=hi: e.tensor_copy(
                    out=srep[:, lo:hi], in_=ps[7][:, 0:hi - lo]), "vv"))
            nk = src.shape[0] // P
            for kd in range(nk):
                v = dma("sync", "qw0", wq32[:, 0:ncols], src[kd * P:(kd + 1) * P, :])
                wait("scalar", "qw0", v)
                wait("scalar", "qp", n["qp"])
                wait("scalar", "sc", ev_cst)
                ec = op("scalar", lambda e, ncols=ncols: e.activation(
                    out=wtile16[:, 0:ncols], in_=wq32[:, 0:ncols], func=AF.Identity,
                    bias=cst[:, 0:1]), "ac")
                wait("vector", "ac", ec)
                if kd == 0:
                    wait("vector", bank_free[7][0], bank_free[7][1])
                em = op("vector", lambda e, ncols=ncols: e.tensor_tensor(
                    out=wtile16[:, 0:ncols], in0=wtile16[:, 0:ncols],
                    in1=srep[:, 0:ncols], op=OP.mult), "dv")
                wait("gpsimd", "dv", em)
                if interleave:
                    for m in range(nm):
                        dma("gpsimd", "qp", scratch[m, :, kd * P:(kd + 1) * P],
                            wtile16[:, m * P:(m + 1) * P])
                else:
                    dma("gpsimd", "qp", scratch[kd], wtile16[:, 0:ncols])
                wait("sync", "ac", ec)
                prev_em[0] = em

        prep_q(gwT, s_gate, sc_g, HID, KH, True)
        prep_q(uwT, s_up, sc_u, HID, KH, True)
        prep_q(dwT, s_down, sc_d, D, KD, False)

        PREP_QP = n["qp"]
        wait("vector", "qp", PREP_QP)
        wait("scalar", "qp", PREP_QP)
        wait("tensor", "qp", PREP_QP)
        wait("gpsimd", "qp", PREP_QP)
        wait("gpsimd", "dv", n["dv"])
        op("gpsimd", lambda e: e.memset(carry[:], 0.0))
        ev_zero = op("gpsimd", lambda e: e.memset(tails[:], 0.0), "sc")

        guard_w = {0: 0, 1: 0}
        guard_d = {0: 0, 1: 0}
        scan_ev = []
        st_ = {"lastdma_rn_pe": 0, "x1_use_pe": 0, "ffn_par": 0, "prev_gu": 0,
               "prev_sil_g": 0}
        TH = T // 2

        def seq_of(mt):
            return mt // (L // T), (mt % (L // T)) * T

        def emit_ph1(mt):
            cur_scope[0] = f"m{mt}_ph1"
            sq, tok0 = seq_of(mt)
            e3_prev = None
            for ts in range(TSB):
                wait("gpsimd", "dv", n["dv"])
                va = dma("gpsimd", "qz", z_x[:],
                         resid[sq, tok0 + ts * P: tok0 + (ts + 1) * P, :])
                wait("scalar", "qz", va)
                if e3_prev:
                    wait("scalar", "dv", e3_prev)
                if mt == 0 and ts == 0:
                    wait("scalar", "sc", ev_cst)
                a1 = op("scalar", lambda e: e.activation(
                    out=rms_f[:], in_=z_x[:], func=AF.Square, accum_out=cols[:, 0:1]), "ac")
                wait("scalar", "ac", a1)
                a2 = op("scalar", lambda e: e.activation(
                    out=cols[:, 1:2], in_=cols[:, 0:1], func=AF.Ln,
                    scale=cst[:, 3:4], bias=cst[:, 1:2]), "ac")
                wait("scalar", "ac", a2)
                e2 = op("scalar", lambda e: e.activation(
                    out=cols[:, 2:3], in_=cols[:, 1:2], func=AF.Exp, scale=cst[:, 2:3]), "ac")
                wait("vector", "ac", e2)
                e3 = op("vector", lambda e: e.tensor_scalar(
                    out=rms_f[:], in0=z_x[:], scalar1=cols[:, 2:3], scalar2=None,
                    op0=OP.mult), "dv")
                e3_prev = e3
                wait("tensor", "dv", e3)
                if mt == 0 and ts == 0:
                    wait("tensor", "sc", ev_ident)
                for kd in range(KD):
                    psb = 2 + kd % 2
                    wait("tensor", bank_free[psb][0], bank_free[psb][1])
                    pv = op("tensor", lambda e, kd=kd, psb=psb: e.transpose(
                        ps[psb][:, 0:P], rms_f[:, kd * P:(kd + 1) * P], ident[:]), "pe")
                    wait("scalar", "pe", pv)
                    bank_free[psb] = _evs(op("scalar", lambda e, kd=kd, ts=ts, psb=psb:
                                             e.activation(
                        out=rn_fm[:, kd * T + ts * P: kd * T + (ts + 1) * P],
                        in_=ps[psb][:, 0:P], func=AF.Identity), "ev"))
            return n["ev"]

        def emit_ph2(mt, reset_ev, rn_ev):
            cur_scope[0] = f"m{mt}_ph2"
            if reset_ev:
                wait("vector", "sc", reset_ev)
            prev_sil = None
            for c in range(NE):
                par = c % 2
                qsem = f"qw{par}"
                wait("sync", "pe", guard_w[par])
                vw = dma("sync", qsem, wbuf[:, par], sc_in[c])
                wait("tensor", qsem, vw)
                psb = par
                pev = mm_group(psb, [
                    (lambda kd=kd, par=par: wbuf[:, par, kd * P:(kd + 1) * P],
                     lambda kd=kd: rn_fm[:, kd * T:(kd + 1) * T],
                     lambda psb=psb: ps[psb][:]) for kd in range(KD)],
                    extra_wait=("ev", rn_ev) if c < 2 else None)
                guard_w[par] = pev
                wait("scalar", "pe", pev)
                if prev_sil:
                    wait("scalar", "dv", prev_sil)
                eev = op("scalar", lambda e, psb=psb: e.activation(
                    out=xin_buf[:, KCONV - 1:KCONV - 1 + T], in_=ps[psb][:],
                    func=AF.Identity), "ev")
                bank_free[psb] = _evs(eev)
                wait("vector", "ev", eev)
                v1 = op("vector", lambda e, c=c: e.tensor_copy(
                    out=xin_buf[:, 0:KCONV - 1],
                    in_=tails[:, c * (KCONV - 1):(c + 1) * (KCONV - 1)]), "dv")
                wait("vector", "dv", v1)
                op("vector", lambda e, c=c: e.tensor_copy(
                    out=tails[:, c * (KCONV - 1):(c + 1) * (KCONV - 1)],
                    in_=xin_buf[:, T:T + KCONV - 1]))
                v2 = None
                for k in range(KCONV):
                    v2 = op("vector", lambda e, c=c, k=k: e.tensor_scalar(
                        out=tap[:, k * T:(k + 1) * T], in0=xin_buf[:, k:k + T],
                        scalar1=convw_sb[:, c * KCONV + k:c * KCONV + k + 1],
                        scalar2=None, op0=OP.mult), "dv" if k == KCONV - 1 else None)
                wait("vector", "dv", v2)
                op("vector", lambda e: e.tensor_tensor(
                    out=cpair[:, 0:T], in0=tap[:, 0:T], in1=tap[:, T:2 * T], op=OP.add))
                v3 = op("vector", lambda e: e.tensor_tensor(
                    out=cpair[:, T:2 * T], in0=tap[:, 2 * T:3 * T],
                    in1=tap[:, 3 * T:4 * T], op=OP.add), "dv")
                wait("vector", "dv", v3)
                edv = op("vector", lambda e: e.tensor_tensor(
                    out=conv_f[:, 0:T], in0=cpair[:, 0:T], in1=cpair[:, T:2 * T],
                    op=OP.add), "dv")
                prev_sil = edv
                wait("scalar", "dv", edv)
                op("scalar", lambda e, c=c: e.activation(
                    out=x_fm[:, c * T:(c + 1) * T], in_=conv_f[:, 0:T], func=AF.Silu,
                    bias=convb_sb[:, c:c + 1]), "ac")

        def emit_ph34(mt):
            cur_scope[0] = f"m{mt}_ph34"
            wdv = wbuf[:].rearrange("p a b -> p (a b)")
            wait("sync", "pe", guard_w[0])
            wait("sync", "pe", guard_w[1])
            vw1 = dma("sync", "qw0", wdv[:, 0:NE * P], sc_wd[0, :, 0:NE * P])
            wait("tensor", "qw0", vw1)
            X_AC = n["ac"]
            pev = mm_group(0, [
                (lambda c=c: wdv[:, c * P:(c + 1) * P],
                 lambda c=c: x_fm[:, c * T:(c + 1) * T],
                 lambda: ps[0][:]) for c in range(NE)],
                extra_wait=("ac", X_AC))
            guard_w[0] = pev
            guard_w[1] = pev
            wait("scalar", "pe", pev)
            e_t1 = op("scalar", lambda e: e.activation(
                out=t1_sb, in_=ps[0][:], func=AF.Identity), "ev")
            bank_free[0] = _evs(e_t1)
            wait("sync", "pe", pev)
            vw2 = dma("sync", "qw0", wdv[:, 0:E], sc_wd[1, :, 0:E])
            wait("tensor", "qw0", vw2)
            prev_ln = None
            for c in range(NE):
                psb = c % 2
                wait("tensor", bank_free[psb][0], bank_free[psb][1])
                if c == 0:
                    wait("tensor", "ev", e_t1)
                pev = op("tensor", lambda e, c=c, psb=psb: e.matmul(
                    ps[psb][:], wdv[:, c * P:(c + 1) * P], t1_sb,
                    start=True, stop=True), "pe")
                wait("scalar", "pe", pev)
                if prev_ln:
                    wait("scalar", "ev", prev_ln)
                a4 = op("scalar", lambda e, c=c, psb=psb: e.activation(
                    out=conv_f[:, 0:T], in_=ps[psb][:], func=AF.Exp,
                    bias=wd2b_sb[:, c:c + 1]), "ac")
                wait("scalar", "ac", a4)
                lv = op("scalar", lambda e, c=c: e.activation(
                    out=dy_fm[:, c * T:(c + 1) * T], in_=conv_f[:, 0:T],
                    func=AF.Ln, bias=1.0), "ev")
                bank_free[psb] = _evs(lv)
                prev_ln = lv
            guard_w[0] = n["pe"]
            guard_w[1] = n["pe"]

        def emit_ph5(mt):
            cur_scope[0] = f"m{mt}_ph5"
            X_AC = n["ac"]
            for ts in range(TSB):
                for wi, (wsb, psb, off) in enumerate([(wB_sb, 2, 0), (wC_sb, 3, NS)]):
                    pev = mm_group(psb, [
                        (lambda c=c, ts=ts: x_fm[:, c * T + ts * P: c * T + (ts + 1) * P],
                         lambda c=c, wsb=wsb: wsb[:, c * NS:(c + 1) * NS],
                         lambda psb=psb: ps[psb][:, 0:NS]) for c in range(NE)],
                        extra_wait=("ac", X_AC) if ts == 0 and wi == 0 else None)
                    wait("scalar", "pe", pev)
                    bank_free[psb] = _evs(op("scalar", lambda e, psb=psb, off=off:
                                             e.activation(
                        out=bc_t_f[:, off:off + NS], in_=ps[psb][:, 0:NS],
                        func=AF.Identity), "ev"))
                wait("tensor", "ev", n["ev"])
                wait("tensor", bank_free[2][0], bank_free[2][1])
                pv = op("tensor", lambda e: e.transpose(
                    ps[2][0:2 * NS, 0:P], bc_t_f[:], ident[:]), "pe")
                wait("scalar", "pe", pv)
                bank_free[2] = _evs(op("scalar", lambda e, ts=ts: e.activation(
                    out=bc_nt[:, ts * P:(ts + 1) * P], in_=ps[2][0:2 * NS, 0:P],
                    func=AF.Identity), "ev"))
            wait("gpsimd", "ev", n["ev"])
            wait("gpsimd", "qb", n["qb"])
            dma("gpsimd", "qb", sc_bc[:], bc_nt[:])
            wait("gpsimd", "qb", n["qb"])
            BCDST = [Brep, Crep]
            wait("tensor", "sc", ev_ones)
            for g in range(8):
                dma("gpsimd", "qb", brow[:, 0:4 * T],
                    sc_bc[:].rearrange("a b -> (a b)")[None, g * 4 * T:(g + 1) * 4 * T])
                wait("tensor", "qb", n["qb"])
                for j in range(4):
                    nidx = (g * 4 + j) % NS
                    dst = BCDST[(g * 4 + j) // NS]
                    psb = 4 + (j % 2)
                    wait("tensor", bank_free[psb][0], bank_free[psb][1])
                    pv = op("tensor", lambda e, j=j, psb=psb: e.matmul(
                        ps[psb][:], ones1[0:1, :], brow[:, j * T:(j + 1) * T],
                        start=True, stop=True), "pe")
                    wait("scalar", "pe", pv)
                    bank_free[psb] = _evs(op("scalar", lambda e, nidx=nidx, dst=dst,
                                             psb=psb: e.activation(
                        out=dst[:, nidx * T:(nidx + 1) * T],
                        in_=ps[psb][:], func=AF.Identity), "ev"))
                if g < 7:
                    wait("gpsimd", "pe", n["pe"])

        def build_ffn_units(mt):
            sq, tok0 = seq_of(mt)
            units = []
            gu_ev = {}

            pend = {}

            def gu_issue(th, kh):
                parw = (th * KH + kh) % 2
                rot = kh % 2
                bg, bu = (0, 1) if rot == 0 else (2, 3)
                qsw = f"qw{parw}"
                qsd = f"qd{parw}"
                wait("sync", "pe", guard_w[parw])
                vg = dma("sync", qsw, wbuf[:, parw], sc_g[kh])
                wait("sync", "pe", guard_d[parw])
                vu = dma("sync", qsd, dbuf[:, parw], sc_u[kh])
                wait("tensor", qsw, vg)
                gev = mm_group(bg, [
                    (lambda kd=kd, parw=parw: wbuf[:, parw, kd * P:(kd + 1) * P],
                     lambda kd=kd, th=th: x1_fm[:, kd * T + th * TH: kd * T + (th + 1) * TH],
                     lambda bg=bg: ps[bg][:, 0:TH]) for kd in range(KD)])
                guard_w[parw] = gev
                wait("tensor", qsd, vu)
                uev = mm_group(bu, [
                    (lambda kd=kd, parw=parw: dbuf[:, parw, kd * P:(kd + 1) * P],
                     lambda kd=kd, th=th: x1_fm[:, kd * T + th * TH: kd * T + (th + 1) * TH],
                     lambda bu=bu: ps[bu][:, 0:TH]) for kd in range(KD)])
                guard_d[parw] = uev
                st_["x1_use_pe"] = uev
                pend[(th, kh)] = (bg, bu, gev, uev)

            def gu_drain(th, kh):
                bg, bu, gev, uev = pend.pop((th, kh))
                wait("scalar", "pe", gev)
                if st_["prev_gu"]:
                    wait("scalar", "dv", st_["prev_gu"])
                eg = op("scalar", lambda e, bg=bg: e.activation(
                    out=gt_b, in_=ps[bg][:, 0:TH], func=AF.Tanh, scale=cst[:, 4:5]), "ev")
                wait("vector", "pe", uev)
                if st_["prev_gu"]:
                    wait("vector", "dv", st_["prev_gu"])
                eu = op("vector", lambda e, bu=bu: e.tensor_copy(
                    out=up_b, in_=ps[bu][:, 0:TH]), "vv")
                bank_free[bu] = _vvs(eu)
                wait("vector", "ev", eg)
                sg = op("vector", lambda e: e.tensor_scalar(
                    out=gt_b, in0=gt_b, scalar1=0.5, scalar2=0.5,
                    op0=OP.mult, op1=OP.add), "vv")
                wait("vector", "vv", sg)
                gg = op("vector", lambda e, bg=bg: e.tensor_tensor(
                    out=gt_b, in0=gt_b, in1=ps[bg][:, 0:TH], op=OP.mult), "vv")
                bank_free[bg] = _vvs(gg)
                wait("vector", "vv", gg)
                wait("vector", "vv", eu)
                g_ = op("vector", lambda e, kh=kh: e.tensor_tensor(
                    out=gu_half[:, kh * TH:(kh + 1) * TH], in0=gt_b, in1=up_b,
                    op=OP.mult), "dv")
                st_["prev_gu"] = g_
                gu_ev[(th, kh)] = g_

            def down_unit(th, j, kh, bankbase):
                parw = (kh + 1) % 2 if j == 0 else kh % 2
                qsd = f"qd{parw}"
                wait("sync", "pe", guard_d[parw])
                vw = dma("sync", qsd, dbuf[:, parw], sc_d[kh])
                wait("tensor", qsd, vw)
                wait("tensor", "dv", gu_ev[(th, kh)])
                for dg in range(4):
                    psb = bankbase + dg
                    stt, sp = (kh == 0), (kh == KH - 1)
                    if stt:
                        wait("tensor", bank_free[psb][0], bank_free[psb][1])
                    f = (lambda e, kh=kh, parw=parw, j=j, dg=dg, psb=psb,
                         stt=stt, sp=sp:
                         e.matmul(ps[psb][:],
                                  gu_half[:, kh * TH + j * P: kh * TH + (j + 1) * P],
                                  dbuf[:, parw, dg * 512:(dg + 1) * 512],
                                  start=stt, stop=sp))
                    if dg == 3:
                        op("tensor", f, "pe")
                    else:
                        op("tensor", f)
                guard_d[parw] = n["pe"]
                st_["down_pe"] = n["pe"]

            def ln2_unit(th, jt, bankbase):
                tsl = 2 * th + jt
                DOWN_PE = st_["down_pe"]
                wait("gpsimd", "dv", n["dv"])
                wait("gpsimd", "qx", n["qx"])
                wait("gpsimd", "pe", DOWN_PE)
                va = dma("gpsimd", "qx", z_ln2[:], sc_x1[tsl])
                wait("vector", "qx", va)
                wait("vector", "pe", DOWN_PE)
                for dg in range(4):
                    psb = bankbase + dg
                    bank_free[psb] = _vvs(op("vector", lambda e, psb=psb, dg=dg:
                                             e.tensor_tensor(
                        out=z_ln2[:, dg * 512:(dg + 1) * 512],
                        in0=z_ln2[:, dg * 512:(dg + 1) * 512],
                        in1=ps[psb][:], op=OP.add), "vv"))
                wait("vector", bank_free[bankbase + 3][0], bank_free[bankbase + 3][1])
                zr = op("vector", lambda e: e.tensor_reduce(
                    out=cols[:, 3:4], in_=z_ln2[:], axis=AX.X, op=OP.add), "dv")
                wait("vector", "dv", zr)
                edv = op("vector", lambda e: e.tensor_scalar(
                    out=cols[:, 4:5], in0=cols[:, 3:4], scalar1=1.0 / D,
                    scalar2=None, op0=OP.mult), "dv")
                wait("scalar", "dv", edv)
                wait("scalar", "qo", n["qo"])
                ea = op("scalar", lambda e: e.activation(
                    out=rn_tm[:], in_=z_ln2[:], func=AF.Square,
                    accum_out=cols[:, 5:6]), "ac")
                wait("vector", "ac", ea)
                op("vector", lambda e: e.tensor_scalar(
                    out=cols[:, 6:7], in0=cols[:, 5:6], scalar1=1.0 / D,
                    scalar2=None, op0=OP.mult))
                c7 = op("vector", lambda e: e.tensor_scalar(
                    out=cols[:, 7:8], in0=cols[:, 4:5], scalar1=cols[:, 4:5],
                    scalar2=None, op0=OP.mult), "dv")
                wait("vector", "dv", c7)
                ev2 = op("vector", lambda e: e.tensor_tensor(
                    out=cols[:, 8:9], in0=cols[:, 6:7], in1=cols[:, 7:8],
                    op=OP.subtract), "dv")
                wait("scalar", "dv", ev2)
                a9 = op("scalar", lambda e: e.activation(
                    out=cols[:, 9:10], in_=cols[:, 8:9], func=AF.Ln,
                    bias=cst[:, 1:2]), "ac")
                wait("scalar", "ac", a9)
                er = op("scalar", lambda e: e.activation(
                    out=cols[:, 10:11], in_=cols[:, 9:10], func=AF.Exp,
                    scale=cst[:, 2:3]), "ac")
                wait("vector", "ac", er)
                c11 = op("vector", lambda e: e.tensor_scalar(
                    out=cols[:, 11:12], in0=cols[:, 4:5], scalar1=cols[:, 10:11],
                    scalar2=-1.0, op0=OP.mult, op1=OP.mult), "dv")
                wait("vector", "dv", c11)
                e5 = op("vector", lambda e: e.tensor_scalar(
                    out=rn_tm[:], in0=z_ln2[:], scalar1=cols[:, 10:11],
                    scalar2=cols[:, 11:12], op0=OP.mult, op1=OP.add), "dv")
                wait("gpsimd", "dv", e5)
                dma("gpsimd", "qo", out[sq, tok0 + tsl * P: tok0 + (tsl + 1) * P, :],
                    rn_tm[:])

            for th in range(2):
                for kh in range(KH + 2):
                    def stage(th=th, kh=kh):
                        if kh < KH:
                            gu_issue(th, kh)
                        if 1 <= kh <= KH:
                            gu_drain(th, kh - 1)
                        if kh >= 2:
                            down_unit(th, 0, kh - 2, 4)
                    units.append(stage)
                for kh in range(KH):
                    units.append(lambda th=th, kh=kh: down_unit(th, 1, kh, 0))
                units.append(lambda th=th: ln2_unit(th, 0, 4))
                units.append(lambda th=th: ln2_unit(th, 1, 0))
            return units

        def emit_scan(mt, units, reset_ev):
            cur_scope[0] = f"m{mt}_ph6"
            wait("vector", "ev", n["ev"])
            wait("vector", "dv", ev_A)
            wait("scalar", "dv", ev_A)
            if reset_ev:
                wait("vector", "sc", reset_ev)
                wait("scalar", "sc", reset_ev)
            ui = 0
            nu = len(units)
            for c in range(NE):
                vdx = op("vector", lambda e, c=c: e.tensor_tensor(
                    out=tap[:, 0:T], in0=dy_fm[:, c * T:(c + 1) * T],
                    in1=x_fm[:, c * T:(c + 1) * T], op=OP.mult), "dv")
                prev_i = vdx
                for nn_ in range(NS):
                    kkg = len(scan_ev)
                    par = kkg % 2
                    if kkg >= 2:
                        wait("scalar", "dv", scan_ev[kkg - 2])
                    ad_v = op("scalar", lambda e, c=c, nn_=nn_, par=par: e.activation(
                        out=d_buf[:, par * T:(par + 1) * T],
                        in_=dy_fm[:, c * T:(c + 1) * T],
                        func=AF.Exp, scale=A_sb[:, c * NS + nn_:c * NS + nn_ + 1]), "ad")
                    wait("vector", "dv", prev_i if nn_ == 0 else scan_ev[-1])
                    iv = op("vector", lambda e, nn_=nn_: e.tensor_tensor(
                        out=i2[:], in0=tap[:, 0:T],
                        in1=Brep[:, nn_ * T:(nn_ + 1) * T], op=OP.mult), "dv")
                    wait("vector", "ad", ad_v)
                    wait("vector", "dv", iv)
                    sev = op("vector", lambda e, nn_=nn_, c=c, par=par:
                             e.tensor_tensor_scan(
                        out=h_all[:, nn_ * T:(nn_ + 1) * T],
                        data0=d_buf[:, par * T:(par + 1) * T],
                        data1=i2[:],
                        initial=carry[:, c * NS + nn_:c * NS + nn_ + 1],
                        op0=OP.mult, op1=OP.add), "dv")
                    scan_ev.append(sev)
                wait("vector", "dv", scan_ev[-1])
                vcc = op("vector", lambda e, c=c: e.tensor_copy(
                    out=carry[:, c * NS:(c + 1) * NS],
                    in_=h_all.rearrange("p (n t) -> p n t", n=NS)[:, :, T - 1]), "dv")
                wait("vector", "dv", vcc)
                vt0 = op("vector", lambda e: e.tensor_tensor(
                    out=h_all[:], in0=h_all[:], in1=Crep[:], op=OP.mult), "dv")
                wait("vector", "dv", vt0)
                vt1 = op("vector", lambda e: e.tensor_tensor(
                    out=h_all[:, 0:8 * T], in0=h_all[:, 0:8 * T],
                    in1=h_all[:, 8 * T:16 * T], op=OP.add), "dv")
                wait("vector", "dv", vt1)
                vt2 = op("vector", lambda e: e.tensor_tensor(
                    out=h_all[:, 0:4 * T], in0=h_all[:, 0:4 * T],
                    in1=h_all[:, 4 * T:8 * T], op=OP.add), "dv")
                wait("vector", "dv", vt2)
                vt3 = op("vector", lambda e: e.tensor_tensor(
                    out=h_all[:, 0:2 * T], in0=h_all[:, 0:2 * T],
                    in1=h_all[:, 2 * T:4 * T], op=OP.add), "dv")
                wait("vector", "dv", vt3)
                op("vector", lambda e, c=c: e.tensor_tensor(
                    out=dy_fm[:, c * T:(c + 1) * T], in0=h_all[:, 0:T],
                    in1=h_all[:, T:2 * T], op=OP.add), "dv")
                tgt = (c + 1) * nu // NE
                while ui < tgt:
                    units[ui]()
                    ui += 1
            return n["dv"]

        def emit_ph7(mt, SCAN_DV):
            cur_scope[0] = f"m{mt}_ph7"
            sskip = tap[:, T:2 * T]
            ytmp = tap[:, 2 * T:3 * T]
            prev_ym = None
            for c in range(NE):
                par = c % 2
                qsem = f"qw{par}"
                wait("sync", "pe", guard_w[par])
                vw = dma("sync", qsem, wbuf[:, par], sc_skip[c])
                wait("tensor", qsem, vw)
                psb = par
                pev = mm_group(psb, [
                    (lambda kd=kd, par=par: wbuf[:, par, kd * P:(kd + 1) * P],
                     lambda kd=kd: rn_fm[:, kd * T:(kd + 1) * T],
                     lambda psb=psb: ps[psb][:]) for kd in range(KD)],
                    extra_wait=("dv", SCAN_DV) if c < 2 else None)
                guard_w[par] = pev
                wait("scalar", "pe", pev)
                if prev_ym:
                    wait("scalar", "dv", prev_ym)
                esk = op("scalar", lambda e, psb=psb: e.activation(
                    out=sskip, in_=ps[psb][:], func=AF.Silu), "ev")
                bank_free[psb] = _evs(esk)
                wait("vector", "ev", esk)
                y1 = op("vector", lambda e, c=c: e.tensor_scalar(
                    out=ytmp, in0=x_fm[:, c * T:(c + 1) * T],
                    scalar1=WD_sb[:, c:c + 1], scalar2=None, op0=OP.mult), "dv")
                wait("vector", "dv", y1)
                y2 = op("vector", lambda e, c=c: e.tensor_tensor(
                    out=ytmp, in0=ytmp, in1=dy_fm[:, c * T:(c + 1) * T], op=OP.add), "dv")
                wait("vector", "dv", y2)
                prev_ym = op("vector", lambda e, c=c: e.tensor_tensor(
                    out=dy_fm[:, c * T:(c + 1) * T], in0=ytmp, in1=sskip,
                    op=OP.mult), "dv")
            return n["dv"]

        def emit_ph8(mt, Y_DV):
            cur_scope[0] = f"m{mt}_ph8"
            sq, tok0 = seq_of(mt)
            for half in range(2):
                for c in range(NE):
                    par = c % 2
                    qsem = f"qw{par}"
                    wait("sync", "pe", guard_w[par])
                    vw = dma("sync", qsem, wbuf[:, par], sc_out[c])
                    wait("tensor", qsem, vw)
                    if half == 0 and c < 2:
                        wait("tensor", "dv", Y_DV)
                    for j in range(8):
                        tsl = 2 * half + j // 4
                        dg = j % 4
                        psb = j
                        stt, sp = (c == 0), (c == NE - 1)
                        if stt:
                            wait("tensor", bank_free[psb][0], bank_free[psb][1])
                        f = (lambda e, c=c, par=par, tsl=tsl, dg=dg, psb=psb, stt=stt,
                             sp=sp:
                             e.matmul(ps[psb][:],
                                      dy_fm[:, c * T + tsl * P: c * T + (tsl + 1) * P],
                                      wbuf[:, par, dg * 512:(dg + 1) * 512],
                                      start=stt, stop=sp))
                        if j == 7:
                            op("tensor", f, "pe")
                        else:
                            op("tensor", f)
                    guard_w[par] = n["pe"]
                OUT_PE = n["pe"]
                for jt in range(2):
                    tsl = 2 * half + jt
                    wait("gpsimd", "dv", n["dv"])
                    va = dma("gpsimd", "qz", z_x[:],
                             resid[sq, tok0 + tsl * P: tok0 + (tsl + 1) * P, :])
                    wait("vector", "qz", va)
                    wait("vector", "pe", OUT_PE)
                    z0 = op("vector", lambda e: e.tensor_scalar(
                        out=z_x[:], in0=z_x[:], scalar1=2.0, scalar2=None,
                        op0=OP.mult), "dv")
                    wait("vector", "dv", z0)
                    for dg in range(4):
                        psb = jt * 4 + dg
                        bank_free[psb] = _vvs(op("vector", lambda e, psb=psb, dg=dg:
                                                 e.tensor_tensor(
                            out=z_x[:, dg * 512:(dg + 1) * 512],
                            in0=z_x[:, dg * 512:(dg + 1) * 512],
                            in1=ps[psb][:], op=OP.add), "vv"))
                    wait("vector", bank_free[jt * 4 + 3][0], bank_free[jt * 4 + 3][1])
                    zr = op("vector", lambda e: e.tensor_reduce(
                        out=cols[:, 3:4], in_=z_x[:], axis=AX.X, op=OP.add), "dv")
                    wait("vector", "dv", zr)
                    edv = op("vector", lambda e: e.tensor_scalar(
                        out=cols[:, 4:5], in0=cols[:, 3:4], scalar1=1.0 / D,
                        scalar2=None, op0=OP.mult), "dv")
                    wait("scalar", "dv", edv)
                    wait("scalar", "qx", n["qx"])
                    wait("scalar", "qo", n["qo"])
                    wait("scalar", "pe", st_["lastdma_rn_pe"])
                    ea = op("scalar", lambda e: e.activation(
                        out=rn_tm[:], in_=z_x[:], func=AF.Square,
                        accum_out=cols[:, 5:6]), "ac")
                    wait("vector", "ac", ea)
                    op("vector", lambda e: e.tensor_scalar(
                        out=cols[:, 6:7], in0=cols[:, 5:6], scalar1=1.0 / D,
                        scalar2=None, op0=OP.mult))
                    c7 = op("vector", lambda e: e.tensor_scalar(
                        out=cols[:, 7:8], in0=cols[:, 4:5], scalar1=cols[:, 4:5],
                        scalar2=None, op0=OP.mult), "dv")
                    wait("vector", "dv", c7)
                    ev2 = op("vector", lambda e: e.tensor_tensor(
                        out=cols[:, 8:9], in0=cols[:, 6:7], in1=cols[:, 7:8],
                        op=OP.subtract), "dv")
                    wait("scalar", "dv", ev2)
                    a9 = op("scalar", lambda e: e.activation(
                        out=cols[:, 9:10], in_=cols[:, 8:9], func=AF.Ln,
                        bias=cst[:, 1:2]), "ac")
                    wait("scalar", "ac", a9)
                    er = op("scalar", lambda e: e.activation(
                        out=cols[:, 10:11], in_=cols[:, 9:10], func=AF.Exp,
                        scale=cst[:, 2:3]), "ac")
                    wait("vector", "ac", er)
                    c11 = op("vector", lambda e: e.tensor_scalar(
                        out=cols[:, 11:12], in0=cols[:, 4:5], scalar1=cols[:, 10:11],
                        scalar2=-1.0, op0=OP.mult, op1=OP.mult), "dv")
                    wait("vector", "dv", c11)
                    e4 = op("vector", lambda e: e.tensor_scalar(
                        out=rn_tm[:], in0=z_x[:], scalar1=cols[:, 10:11],
                        scalar2=cols[:, 11:12], op0=OP.mult, op1=OP.add), "dv")
                    wait("gpsimd", "dv", e4)
                    wait("gpsimd", "qx", n["qx"])
                    dma("gpsimd", "qx", sc_x1[tsl], rn_tm[:])
                    wait("tensor", "dv", e4)
                    if half == 0 and jt == 0:
                        wait("scalar", "pe", st_["x1_use_pe"])
                    for kd in range(KD):
                        psb = kd % 2
                        wait("tensor", bank_free[psb][0], bank_free[psb][1])
                        pv = op("tensor", lambda e, kd=kd, psb=psb: e.transpose(
                            ps[psb][:, 0:P], rn_tm[:, kd * P:(kd + 1) * P], ident[:]), "pe")
                        wait("scalar", "pe", pv)
                        bank_free[psb] = _evs(op("scalar", lambda e, kd=kd, tsl=tsl,
                                                 psb=psb: e.activation(
                            out=x1_fm[:, kd * T + tsl * P: kd * T + (tsl + 1) * P],
                            in_=ps[psb][:, 0:P], func=AF.Identity), "ev"))
                    st_["lastdma_rn_pe"] = n["pe"]

        rn_ev0 = emit_ph1(0)
        emit_ph2(0, ev_zero, rn_ev0)
        emit_ph5(0)
        emit_ph34(0)
        prev_units = []
        pending_reset = ev_zero
        for mt in range(MT):
            SCAN_DV = emit_scan(mt, prev_units, pending_reset)
            Y_DV = emit_ph7(mt, SCAN_DV)
            emit_ph8(mt, Y_DV)
            prev_units = build_ffn_units(mt)
            if mt + 1 < MT:
                nxt = mt + 1
                if nxt % (L // T) == 0:
                    wait("gpsimd", "dv", n["dv"])
                    op("gpsimd", lambda e: e.memset(carry[:], 0.0))
                    reset_ev = op("gpsimd", lambda e: e.memset(tails[:], 0.0), "sc")
                else:
                    reset_ev = None
                pending_reset = reset_ev
                rn_evn = emit_ph1(nxt)
                emit_ph2(nxt, reset_ev, rn_evn)
                emit_ph5(nxt)
                emit_ph34(nxt)
        cur_scope[0] = "tail_ffn"
        for u in prev_units:
            u()

        wait("gpsimd", "qo", n["qo"])
        wait("sync", "qo", n["qo"])

        from contextlib import nullcontext

        def register(engname):
            def body(eng):
                for scope, f in plan[engname]:
                    cm = nc.named_scope(scope) if scope else nullcontext()
                    with cm:
                        f(eng)
            return body

        block.sync(register("sync"))
        block.gpsimd(register("gpsimd"))
        block.tensor(register("tensor"))
        block.vector(register("vector"))
        block.scalar(register("scalar"))

    return nc


def kernel(**inputs):
    h = _prep_inputs(inputs)
    resid = np.ascontiguousarray(np.asarray(inputs["L0_attn_resid"], dtype=np.float32))
    nc = build()
    in_maps = []
    for core in range(NCORES):
        in_maps.append({
            "resid": resid[core * SEQS:(core + 1) * SEQS],
            "inwT": h["inwT"], "skipwT": h["skipwT"], "outwT": h["outwT"],
            "wd1T": h["wd1T"], "wd2T": h["wd2T"], "wBT": h["wBT"], "wCT": h["wCT"],
            "gwT": h["gwT"], "uwT": h["uwT"], "dwT": h["dwT"],
            "s_gate": h["s_gate"], "s_up": h["s_up"], "s_down": h["s_down"],
            "normw": h["normw"], "convw": h["convw"], "convb": h["convb"],
            "wd2b": h["wd2b"], "WD": h["WD"], "Alog": h["Alog"],
        })
    res = run_bass_kernel_spmd(nc, in_maps, core_ids=list(range(NCORES)))
    outs = [np.asarray(r["out"]) for r in res.results]
    return np.concatenate(outs, axis=0).reshape(16, L, D).astype(np.float32)


# version 05
# speedup vs baseline: 1.1666x; 1.0901x over previous
import numpy as np
import concourse.bass as bass
import concourse.mybir as mybir
from concourse.bass_utils import run_bass_kernel_spmd

F32 = mybir.dt.float32
BF16 = mybir.dt.bfloat16
I32 = mybir.dt.int32
AF = mybir.ActivationFunctionType
OP = mybir.AluOpType
AX = mybir.AxisListType

P = 128
D = 2048
E = 4096
HID = 5632
R = 128
NS = 16
KCONV = 4
SEQS = 2
L = 2048
T = 512
MT = SEQS * (L // T)
TSB = T // P
KD = D // P
NE = E // P
KH = HID // P
EPS = 1e-5
NCORES = 8

DMA_SEMS = ["qw0", "qw1", "qd0", "qd1", "qz", "qo", "qx", "qb", "qp"]
ENG_SEMS = ["pe", "ev", "vv", "dv", "ac", "sc", "ad"]


def _rearr_col(v, nchunk):
    return np.ascontiguousarray(np.asarray(v).reshape(nchunk, P).T).astype(np.float32)


def _prep_inputs(inp):
    asc = np.ascontiguousarray
    h = {}
    h["inwT"] = asc(inp["L0_attn_in_w"].T)
    h["skipwT"] = asc(inp["L0_attn_skip_w"].T)
    h["outwT"] = asc(inp["L0_attn_out_w"].T)
    h["wd1T"] = asc(inp["L0_attn_wd1"].T)
    h["wd2T"] = asc(inp["L0_attn_wd2"].T)
    h["wBT"] = asc(inp["L0_attn_wB"].T)
    h["wCT"] = asc(inp["L0_attn_wC"].T)
    h["gwT"] = asc(inp["L0_ffn_wq_gate"].T)
    h["uwT"] = asc(inp["L0_ffn_wq_up"].T)
    h["dwT"] = asc(inp["L0_ffn_wq_down"].T)
    h["s_gate"] = asc(np.asarray(inp["L0_ffn_s_gate"], np.float32).reshape(1, HID))
    h["s_up"] = asc(np.asarray(inp["L0_ffn_s_up"], np.float32).reshape(1, HID))
    h["s_down"] = asc(np.asarray(inp["L0_ffn_s_down"], np.float32).reshape(1, D))
    h["normw"] = _rearr_col(inp["L0_attn_norm_w"], KD)
    h["convw"] = asc(np.asarray(inp["L0_attn_conv_w"], np.float32).reshape(NE, P, KCONV)
                     .transpose(1, 0, 2).reshape(P, NE * KCONV))
    h["convb"] = _rearr_col(inp["L0_attn_conv_b"], NE)
    h["wd2b"] = _rearr_col(inp["L0_attn_wd2_b"], NE)
    h["WD"] = _rearr_col(inp["L0_attn_W_D"], NE)
    h["Alog"] = asc(np.asarray(inp["L0_attn_A_log"], np.float32).reshape(NE, P, NS)
                    .transpose(1, 0, 2).reshape(P, NE * NS))
    for k in ["inwT", "skipwT", "outwT", "wd1T", "wd2T", "wBT", "wCT"]:
        h[k] = np.asarray(h[k], np.float32)
    for k in ["gwT", "uwT", "dwT"]:
        h[k] = np.asarray(h[k], np.int32)
    return h


def build():
    nc = bass.Bass(target_bir_lowering=False)

    resid = nc.dram_tensor("resid", [SEQS, L, D], F32, kind="ExternalInput")
    inwT = nc.dram_tensor("inwT", [D, E], F32, kind="ExternalInput")
    skipwT = nc.dram_tensor("skipwT", [D, E], F32, kind="ExternalInput")
    outwT = nc.dram_tensor("outwT", [E, D], F32, kind="ExternalInput")
    wd1T = nc.dram_tensor("wd1T", [E, R], F32, kind="ExternalInput")
    wd2T = nc.dram_tensor("wd2T", [R, E], F32, kind="ExternalInput")
    wBT = nc.dram_tensor("wBT", [E, NS], F32, kind="ExternalInput")
    wCT = nc.dram_tensor("wCT", [E, NS], F32, kind="ExternalInput")
    gwT = nc.dram_tensor("gwT", [D, HID], I32, kind="ExternalInput")
    uwT = nc.dram_tensor("uwT", [D, HID], I32, kind="ExternalInput")
    dwT = nc.dram_tensor("dwT", [HID, D], I32, kind="ExternalInput")
    s_gate = nc.dram_tensor("s_gate", [1, HID], F32, kind="ExternalInput")
    s_up = nc.dram_tensor("s_up", [1, HID], F32, kind="ExternalInput")
    s_down = nc.dram_tensor("s_down", [1, D], F32, kind="ExternalInput")
    normw_in = nc.dram_tensor("normw", [P, KD], F32, kind="ExternalInput")
    convw_in = nc.dram_tensor("convw", [P, NE * KCONV], F32, kind="ExternalInput")
    convb_in = nc.dram_tensor("convb", [P, NE], F32, kind="ExternalInput")
    wd2b_in = nc.dram_tensor("wd2b", [P, NE], F32, kind="ExternalInput")
    WD_in = nc.dram_tensor("WD", [P, NE], F32, kind="ExternalInput")
    Alog_in = nc.dram_tensor("Alog", [P, NE * NS], F32, kind="ExternalInput")
    out = nc.dram_tensor("out", [SEQS, L, D], F32, kind="ExternalOutput")

    sc_in = nc.dram_tensor("sc_in", [NE, P, KD * P], BF16)
    sc_skip = nc.dram_tensor("sc_skip", [NE, P, KD * P], BF16)
    sc_out = nc.dram_tensor("sc_out", [NE, P, D], BF16)
    sc_g = nc.dram_tensor("sc_g", [KH, P, KD * P], BF16)
    sc_u = nc.dram_tensor("sc_u", [KH, P, KD * P], BF16)
    sc_d = nc.dram_tensor("sc_d", [KH, P, D], BF16)
    sc_bc = nc.dram_tensor("sc_bc", [2 * NS, T], BF16)
    sc_wd = nc.dram_tensor("sc_wd", [2, P, E], BF16)
    sc_x1 = nc.dram_tensor("sc_x1", [TSB, P, D], F32)

    from contextlib import ExitStack
    with ExitStack() as ctx:
        En = ctx.enter_context

        def sbt(name, shape, dt=BF16):
            return En(nc.sbuf_tensor(name, shape, dt))

        consts = sbt("consts", [P, 3328])
        ident = consts[:, 0:256].bitcast(F32)
        A_sb = consts[:, 256:1280].bitcast(F32)
        convw_sb = consts[:, 1280:1536].bitcast(F32)
        convb_sb = consts[:, 1536:1600].bitcast(F32)
        wd2b_sb = consts[:, 1600:1664].bitcast(F32)
        WD_sb = consts[:, 1664:1728].bitcast(F32)
        normw_sb = consts[:, 1728:1760].bitcast(F32)
        cols = consts[:, 1760:1792].bitcast(F32)
        cst = consts[:, 1792:1808].bitcast(F32)
        wB_sb = consts[:, 1808:2320]
        wC_sb = consts[:, 2320:2832]
        ones1 = consts[0:1, 2832:2960]
        ones1f = consts[0:1, 2960:3216].bitcast(F32)
        tails = consts[:, 3216:3312]

        x_fm = sbt("x_fm", [P, NE * T])
        dy_fm = sbt("dy_fm", [P, NE * T])
        z_x = x_fm[:, 0:4096].bitcast(F32)
        rn_fm = sbt("rn_fm", [P, KD * T])
        rn_tm = sbt("rn_tm", [P, D], F32)
        wbuf = sbt("wbuf", [P, 2, KD * P])
        dbuf = sbt("dbuf", [P, 2, D])

        arena = sbt("arena", [P, 49728])
        Brep = arena[:, 0:NS * T]
        Crep = arena[:, NS * T:2 * NS * T]
        h_all = arena[:, 16384:16384 + NS * T]
        i2 = arena[:, 24576:24576 + T]
        d_buf = arena[:, 25088:25088 + 2 * T]
        carry = arena[:, 26112:26112 + 2 * NE * NS].bitcast(F32)
        tap = arena[:, 27136:27136 + 4 * T]
        brow = arena[0:1, 27136:27136 + 4 * T]
        bc_nt = arena[0:2 * NS, 29184:29184 + T]
        bc_t_f = arena[:, 29696:29760].bitcast(F32)
        gt_b = arena[:, 29760:29760 + T // 2]
        up_b = arena[:, 30016:30016 + T // 2]
        x1_fm = arena[:, 30272:30272 + KD * T]
        gu_half = arena[:, 38464:38464 + KH * (T // 2)]
        z_ln2 = arena[:, 38464 + 7168:38464 + 11264].bitcast(F32)
        rms_f = arena[:, 38464:38464 + 2 * D].bitcast(F32)
        cpair = arena[:, 38464:38464 + 4096].bitcast(F32)
        conv_f = arena[:, 42560:43584].bitcast(F32)
        xin_buf = arena[:, 43584:43584 + KCONV - 1 + T]
        t1_sb = arena[:, 44168:44168 + T]
        wtile32 = arena[:, 0:2 * E].bitcast(F32)
        wq32 = arena[:, 0:2 * HID].bitcast(I32)
        wtile16 = arena[:, 2 * HID:3 * HID]
        srow = arena[0:1, 16896:16896 + 2 * HID].bitcast(F32)
        srep = arena[:, 28160:28160 + 2 * HID].bitcast(F32)

        ps = [En(nc.psum_tensor(f"ps{i}", [P, 512], F32)) for i in range(8)]

        sems = {}
        for name in DMA_SEMS + ENG_SEMS:
            sems[name] = En(nc.semaphore(name))

        block = En(nc.Block())
        plan = {k: [] for k in ["sync", "gpsimd", "tensor", "vector", "scalar"]}
        n = {k: 0 for k in sems}
        cur_scope = [None]

        def emit(eng, fn):
            plan[eng].append((cur_scope[0], fn))

        def wait(eng, sem, val):
            if val is None or val <= 0:
                return
            emit(eng, lambda e, sem=sem, val=val: e.wait_ge(sems[sem], val))

        def dma(eng, sem, dst, src):
            n[sem] += 16
            v = n[sem]
            emit(eng, lambda e, dst=dst, src=src, sem=sem:
                 e.dma_start(out=dst, in_=src).then_inc(sems[sem], 16))
            return v

        def op(eng, fn, sem=None):
            if sem is None:
                emit(eng, fn)
                return None
            n[sem] += 1
            v = n[sem]
            emit(eng, lambda e, fn=fn, sem=sem: fn(e).then_inc(sems[sem], 1))
            return v

        bank_free = {i: ("ev", 0) for i in range(8)}

        def _evs(v):
            return ("ev", v)

        def _vvs(v):
            return ("vv", v)

        def mm_group(psb, tiles, extra_wait=None):
            first = True
            nt = len(tiles)
            res = None
            for j, (lf, rf, of) in enumerate(tiles):
                if first:
                    wait("tensor", bank_free[psb][0], bank_free[psb][1])
                    if extra_wait is not None:
                        wait("tensor", extra_wait[0], extra_wait[1])
                    first = False
                st, sp = (j == 0), (j == nt - 1)
                f = (lambda e, lf=lf, rf=rf, of=of, st=st, sp=sp:
                     e.matmul(of(), lf(), rf(), start=st, stop=sp))
                if sp:
                    res = op("tensor", f, "pe")
                else:
                    op("tensor", f)
            return res

        ev_ms = op("gpsimd", lambda e: e.memset(ident[:], 0.0), "sc")
        wait("gpsimd", "sc", ev_ms)
        ev_ident = op("gpsimd", lambda e: e.affine_select(
            out=ident[:], in_=ident[:], compare_op=OP.not_equal, fill=1.0,
            base=0, pattern=[[-1, P]], channel_multiplier=1), "sc")
        ev_ones = op("gpsimd", lambda e: e.memset(ones1[:], 1.0), "sc")
        ev_onesf = op("gpsimd", lambda e: e.memset(ones1f[:], 1.0), "sc")
        op("gpsimd", lambda e: e.memset(cst[:, 0:1], -128.0))
        op("gpsimd", lambda e: e.memset(cst[:, 1:2], EPS))
        op("gpsimd", lambda e: e.memset(cst[:, 2:3], -0.5))
        op("gpsimd", lambda e: e.memset(cst[:, 3:4], 1.0 / D))
        ev_cst = op("gpsimd", lambda e: e.memset(cst[:, 4:5], 0.5), "sc")

        dma("sync", "qw0", normw_sb[:], normw_in[:])
        dma("sync", "qw0", convw_sb[:], convw_in[:])
        dma("sync", "qw0", convb_sb[:], convb_in[:])
        dma("sync", "qw0", wd2b_sb[:], wd2b_in[:])
        dma("sync", "qw0", WD_sb[:], WD_in[:])
        v = dma("sync", "qw0", A_sb[:], Alog_in[:])
        wait("sync", "qw0", v)
        wait("scalar", "qw0", v)
        ev_aexp = op("scalar", lambda e: e.activation(
            out=A_sb[:], in_=A_sb[:], func=AF.Exp), "ac")
        wait("vector", "ac", ev_aexp)
        ev_A = op("vector", lambda e: e.tensor_scalar(
            out=A_sb[:], in0=A_sb[:], scalar1=-1.0, scalar2=None, op0=OP.mult), "dv")

        for c in range(NE):
            dma("sync", "qw0", wtile32[:, c * R:(c + 1) * R], wd1T[c * P:(c + 1) * P, :])
        wait("sync", "qw0", n["qw0"])
        wait("vector", "qw0", n["qw0"])
        ewd1 = op("vector", lambda e: e.tensor_copy(
            out=wtile16[:, 0:NE * R], in_=wtile32[:, 0:NE * R]), "dv")
        wait("gpsimd", "dv", ewd1)
        vx = dma("gpsimd", "qp", sc_wd[0, :, 0:NE * R], wtile16[:, 0:NE * R])
        wait("sync", "dv", ewd1)
        v = dma("sync", "qw0", wtile32[:], wd2T[:])
        wait("sync", "qw0", v)
        wait("vector", "qw0", v)
        wait("vector", "qp", vx)
        ewd2 = op("vector", lambda e: e.tensor_copy(
            out=wtile16[:, 0:E], in_=wtile32[:]), "dv")
        wait("gpsimd", "dv", ewd2)
        vx = dma("gpsimd", "qp", sc_wd[1], wtile16[:, 0:E])
        wait("sync", "dv", ewd2)
        for c in range(NE):
            dma("sync", "qw0", wtile32[:, c * NS:(c + 1) * NS], wBT[c * P:(c + 1) * P, :])
        wait("sync", "qw0", n["qw0"])
        wait("vector", "qw0", n["qw0"])
        ebc1 = op("vector", lambda e: e.tensor_copy(
            out=wB_sb[:], in_=wtile32[:, 0:NE * NS]), "dv")
        wait("sync", "dv", ebc1)
        for c in range(NE):
            dma("sync", "qw0", wtile32[:, c * NS:(c + 1) * NS], wCT[c * P:(c + 1) * P, :])
        wait("sync", "qw0", n["qw0"])
        wait("vector", "qw0", n["qw0"])
        ev_smallw = op("vector", lambda e: e.tensor_copy(
            out=wC_sb[:], in_=wtile32[:, 0:NE * NS]), "dv")
        wait("sync", "dv", ev_smallw)

        def prep_folded(src, scratch):
            for kd in range(KD):
                v = dma("sync", "qw0", wtile32[:], src[kd * P:(kd + 1) * P, :])
                wait("scalar", "qw0", v)
                wait("scalar", "qp", n["qp"])
                ec = op("scalar", lambda e, kd=kd: e.activation(
                    out=wtile16[:, 0:E], in_=wtile32[:], func=AF.Identity,
                    scale=normw_sb[:, kd:kd + 1]), "ac")
                wait("gpsimd", "ac", ec)
                for c in range(NE):
                    dma("gpsimd", "qp", scratch[c, :, kd * P:(kd + 1) * P],
                        wtile16[:, c * P:(c + 1) * P])
                wait("sync", "ac", ec)

        prep_folded(inwT, sc_in)
        prep_folded(skipwT, sc_skip)

        for c in range(NE):
            v = dma("sync", "qw0", wtile32[:, 0:D], outwT[c * P:(c + 1) * P, :])
            wait("scalar", "qw0", v)
            wait("scalar", "qp", n["qp"])
            ec = op("scalar", lambda e: e.activation(
                out=wtile16[:, 0:D], in_=wtile32[:, 0:D], func=AF.Identity), "ac")
            wait("gpsimd", "ac", ec)
            dma("gpsimd", "qp", sc_out[c], wtile16[:, 0:D])
            wait("sync", "ac", ec)

        prev_em = [0]

        def prep_q(src, srcscale, scratch, ncols, nm, interleave):
            v = dma("sync", "qw0", srow[:, 0:ncols], srcscale[:])
            wait("sync", "qw0", v)
            wait("tensor", "qw0", v)
            wait("tensor", "sc", ev_onesf)
            nb = (ncols + 511) // 512
            for b_ in range(nb):
                lo, hi = b_ * 512, min(ncols, (b_ + 1) * 512)
                wait("tensor", bank_free[7][0], bank_free[7][1])
                pv = op("tensor", lambda e, lo=lo, hi=hi: e.matmul(
                    ps[7][:, 0:hi - lo], ones1f[0:1, :], srow[0:1, lo:hi],
                    start=True, stop=True), "pe")
                wait("vector", "pe", pv)
                if b_ == 0:
                    wait("vector", "dv", prev_em[0])
                bank_free[7] = _vvs(op("vector", lambda e, lo=lo, hi=hi: e.tensor_copy(
                    out=srep[:, lo:hi], in_=ps[7][:, 0:hi - lo]), "vv"))
            nk = src.shape[0] // P
            for kd in range(nk):
                v = dma("sync", "qw0", wq32[:, 0:ncols], src[kd * P:(kd + 1) * P, :])
                wait("scalar", "qw0", v)
                wait("scalar", "qp", n["qp"])
                wait("scalar", "sc", ev_cst)
                ec = op("scalar", lambda e, ncols=ncols: e.activation(
                    out=wtile16[:, 0:ncols], in_=wq32[:, 0:ncols], func=AF.Identity,
                    bias=cst[:, 0:1]), "ac")
                wait("vector", "ac", ec)
                if kd == 0:
                    wait("vector", bank_free[7][0], bank_free[7][1])
                em = op("vector", lambda e, ncols=ncols: e.tensor_tensor(
                    out=wtile16[:, 0:ncols], in0=wtile16[:, 0:ncols],
                    in1=srep[:, 0:ncols], op=OP.mult), "dv")
                wait("gpsimd", "dv", em)
                if interleave:
                    for m in range(nm):
                        dma("gpsimd", "qp", scratch[m, :, kd * P:(kd + 1) * P],
                            wtile16[:, m * P:(m + 1) * P])
                else:
                    dma("gpsimd", "qp", scratch[kd], wtile16[:, 0:ncols])
                wait("sync", "ac", ec)
                prev_em[0] = em

        prep_q(gwT, s_gate, sc_g, HID, KH, True)
        prep_q(uwT, s_up, sc_u, HID, KH, True)
        prep_q(dwT, s_down, sc_d, D, KD, False)

        PREP_QP = n["qp"]
        wait("vector", "qp", PREP_QP)
        wait("scalar", "qp", PREP_QP)
        wait("tensor", "qp", PREP_QP)
        wait("gpsimd", "qp", PREP_QP)
        wait("gpsimd", "dv", n["dv"])
        op("gpsimd", lambda e: e.memset(carry[:], 0.0))
        ev_zero = op("gpsimd", lambda e: e.memset(tails[:], 0.0), "sc")

        guard_w = {0: 0, 1: 0}
        guard_d = {0: 0, 1: 0}
        scan_ev = []
        st_ = {"lastdma_rn_pe": 0, "x1_use_pe": 0, "ffn_par": 0, "prev_gu": 0,
               "prev_sil_g": 0}
        TH = T // 2

        def seq_of(mt):
            return mt // (L // T), (mt % (L // T)) * T

        def emit_ph1(mt):
            cur_scope[0] = f"m{mt}_ph1"
            sq, tok0 = seq_of(mt)
            e3_prev = None
            for ts in range(TSB):
                wait("gpsimd", "dv", n["dv"])
                va = dma("gpsimd", "qz", z_x[:],
                         resid[sq, tok0 + ts * P: tok0 + (ts + 1) * P, :])
                wait("scalar", "qz", va)
                if e3_prev:
                    wait("scalar", "dv", e3_prev)
                if mt == 0 and ts == 0:
                    wait("scalar", "sc", ev_cst)
                a1 = op("scalar", lambda e: e.activation(
                    out=rms_f[:], in_=z_x[:], func=AF.Square, accum_out=cols[:, 0:1]), "ac")
                wait("scalar", "ac", a1)
                a2 = op("scalar", lambda e: e.activation(
                    out=cols[:, 1:2], in_=cols[:, 0:1], func=AF.Ln,
                    scale=cst[:, 3:4], bias=cst[:, 1:2]), "ac")
                wait("scalar", "ac", a2)
                e2 = op("scalar", lambda e: e.activation(
                    out=cols[:, 2:3], in_=cols[:, 1:2], func=AF.Exp, scale=cst[:, 2:3]), "ac")
                wait("vector", "ac", e2)
                e3 = op("vector", lambda e: e.tensor_scalar(
                    out=rms_f[:], in0=z_x[:], scalar1=cols[:, 2:3], scalar2=None,
                    op0=OP.mult), "dv")
                e3_prev = e3
                wait("tensor", "dv", e3)
                if mt == 0 and ts == 0:
                    wait("tensor", "sc", ev_ident)
                for kd in range(KD):
                    psb = 2 + kd % 2
                    wait("tensor", bank_free[psb][0], bank_free[psb][1])
                    pv = op("tensor", lambda e, kd=kd, psb=psb: e.transpose(
                        ps[psb][:, 0:P], rms_f[:, kd * P:(kd + 1) * P], ident[:]), "pe")
                    wait("scalar", "pe", pv)
                    bank_free[psb] = _evs(op("scalar", lambda e, kd=kd, ts=ts, psb=psb:
                                             e.activation(
                        out=rn_fm[:, kd * T + ts * P: kd * T + (ts + 1) * P],
                        in_=ps[psb][:, 0:P], func=AF.Identity), "ev"))
            return n["ev"]

        def emit_ph2(mt, reset_ev, rn_ev):
            cur_scope[0] = f"m{mt}_ph2"
            if reset_ev:
                wait("vector", "sc", reset_ev)
            prev_sil = None
            for c in range(NE):
                par = c % 2
                qsem = f"qw{par}"
                wait("sync", "pe", guard_w[par])
                vw = dma("sync", qsem, wbuf[:, par], sc_in[c])
                wait("tensor", qsem, vw)
                psb = par
                pev = mm_group(psb, [
                    (lambda kd=kd, par=par: wbuf[:, par, kd * P:(kd + 1) * P],
                     lambda kd=kd: rn_fm[:, kd * T:(kd + 1) * T],
                     lambda psb=psb: ps[psb][:]) for kd in range(KD)],
                    extra_wait=("ev", rn_ev) if c < 2 else None)
                guard_w[par] = pev
                wait("scalar", "pe", pev)
                if prev_sil:
                    wait("scalar", "dv", prev_sil)
                eev = op("scalar", lambda e, psb=psb: e.activation(
                    out=xin_buf[:, KCONV - 1:KCONV - 1 + T], in_=ps[psb][:],
                    func=AF.Identity), "ev")
                bank_free[psb] = _evs(eev)
                wait("vector", "ev", eev)
                v1 = op("vector", lambda e, c=c: e.tensor_copy(
                    out=xin_buf[:, 0:KCONV - 1],
                    in_=tails[:, c * (KCONV - 1):(c + 1) * (KCONV - 1)]), "dv")
                wait("vector", "dv", v1)
                op("vector", lambda e, c=c: e.tensor_copy(
                    out=tails[:, c * (KCONV - 1):(c + 1) * (KCONV - 1)],
                    in_=xin_buf[:, T:T + KCONV - 1]))
                v2 = None
                for k in range(KCONV):
                    v2 = op("vector", lambda e, c=c, k=k: e.tensor_scalar(
                        out=tap[:, k * T:(k + 1) * T], in0=xin_buf[:, k:k + T],
                        scalar1=convw_sb[:, c * KCONV + k:c * KCONV + k + 1],
                        scalar2=None, op0=OP.mult), "dv" if k == KCONV - 1 else None)
                wait("vector", "dv", v2)
                op("vector", lambda e: e.tensor_tensor(
                    out=cpair[:, 0:T], in0=tap[:, 0:T], in1=tap[:, T:2 * T], op=OP.add))
                v3 = op("vector", lambda e: e.tensor_tensor(
                    out=cpair[:, T:2 * T], in0=tap[:, 2 * T:3 * T],
                    in1=tap[:, 3 * T:4 * T], op=OP.add), "dv")
                wait("vector", "dv", v3)
                edv = op("vector", lambda e: e.tensor_tensor(
                    out=conv_f[:, 0:T], in0=cpair[:, 0:T], in1=cpair[:, T:2 * T],
                    op=OP.add), "dv")
                prev_sil = edv
                wait("scalar", "dv", edv)
                op("scalar", lambda e, c=c: e.activation(
                    out=x_fm[:, c * T:(c + 1) * T], in_=conv_f[:, 0:T], func=AF.Silu,
                    bias=convb_sb[:, c:c + 1]), "ac")

        def emit_ph34(mt):
            cur_scope[0] = f"m{mt}_ph34"
            wdv = wbuf[:].rearrange("p a b -> p (a b)")
            wait("sync", "pe", guard_w[0])
            wait("sync", "pe", guard_w[1])
            vw1 = dma("sync", "qw0", wdv[:, 0:NE * P], sc_wd[0, :, 0:NE * P])
            wait("tensor", "qw0", vw1)
            X_AC = n["ac"]
            pev = mm_group(0, [
                (lambda c=c: wdv[:, c * P:(c + 1) * P],
                 lambda c=c: x_fm[:, c * T:(c + 1) * T],
                 lambda: ps[0][:]) for c in range(NE)],
                extra_wait=("ac", X_AC))
            guard_w[0] = pev
            guard_w[1] = pev
            wait("scalar", "pe", pev)
            e_t1 = op("scalar", lambda e: e.activation(
                out=t1_sb, in_=ps[0][:], func=AF.Identity), "ev")
            bank_free[0] = _evs(e_t1)
            wait("sync", "pe", pev)
            vw2 = dma("sync", "qw0", wdv[:, 0:E], sc_wd[1, :, 0:E])
            wait("tensor", "qw0", vw2)
            prev_ln = None
            for c in range(NE):
                psb = c % 2
                wait("tensor", bank_free[psb][0], bank_free[psb][1])
                if c == 0:
                    wait("tensor", "ev", e_t1)
                pev = op("tensor", lambda e, c=c, psb=psb: e.matmul(
                    ps[psb][:], wdv[:, c * P:(c + 1) * P], t1_sb,
                    start=True, stop=True), "pe")
                wait("scalar", "pe", pev)
                if prev_ln:
                    wait("scalar", "ev", prev_ln)
                a4 = op("scalar", lambda e, c=c, psb=psb: e.activation(
                    out=conv_f[:, 0:T], in_=ps[psb][:], func=AF.Exp,
                    bias=wd2b_sb[:, c:c + 1]), "ac")
                wait("scalar", "ac", a4)
                lv = op("scalar", lambda e, c=c: e.activation(
                    out=dy_fm[:, c * T:(c + 1) * T], in_=conv_f[:, 0:T],
                    func=AF.Ln, bias=1.0), "ev")
                bank_free[psb] = _evs(lv)
                prev_ln = lv
            guard_w[0] = n["pe"]
            guard_w[1] = n["pe"]

        def emit_ph5(mt):
            cur_scope[0] = f"m{mt}_ph5"
            X_AC = n["ac"]
            for ts in range(TSB):
                for wi, (wsb, psb, off) in enumerate([(wB_sb, 2, 0), (wC_sb, 3, NS)]):
                    pev = mm_group(psb, [
                        (lambda c=c, ts=ts: x_fm[:, c * T + ts * P: c * T + (ts + 1) * P],
                         lambda c=c, wsb=wsb: wsb[:, c * NS:(c + 1) * NS],
                         lambda psb=psb: ps[psb][:, 0:NS]) for c in range(NE)],
                        extra_wait=("ac", X_AC) if ts == 0 and wi == 0 else None)
                    wait("scalar", "pe", pev)
                    bank_free[psb] = _evs(op("scalar", lambda e, psb=psb, off=off:
                                             e.activation(
                        out=bc_t_f[:, off:off + NS], in_=ps[psb][:, 0:NS],
                        func=AF.Identity), "ev"))
                wait("tensor", "ev", n["ev"])
                wait("tensor", bank_free[2][0], bank_free[2][1])
                pv = op("tensor", lambda e: e.transpose(
                    ps[2][0:2 * NS, 0:P], bc_t_f[:], ident[:]), "pe")
                wait("scalar", "pe", pv)
                bank_free[2] = _evs(op("scalar", lambda e, ts=ts: e.activation(
                    out=bc_nt[:, ts * P:(ts + 1) * P], in_=ps[2][0:2 * NS, 0:P],
                    func=AF.Identity), "ev"))
            wait("gpsimd", "ev", n["ev"])
            wait("gpsimd", "qb", n["qb"])
            dma("gpsimd", "qb", sc_bc[:], bc_nt[:])
            wait("gpsimd", "qb", n["qb"])
            BCDST = [Brep, Crep]
            wait("tensor", "sc", ev_ones)
            for g in range(8):
                dma("gpsimd", "qb", brow[:, 0:4 * T],
                    sc_bc[:].rearrange("a b -> (a b)")[None, g * 4 * T:(g + 1) * 4 * T])
                wait("tensor", "qb", n["qb"])
                for j in range(4):
                    nidx = (g * 4 + j) % NS
                    dst = BCDST[(g * 4 + j) // NS]
                    psb = 4 + (j % 2)
                    wait("tensor", bank_free[psb][0], bank_free[psb][1])
                    pv = op("tensor", lambda e, j=j, psb=psb: e.matmul(
                        ps[psb][:], ones1[0:1, :], brow[:, j * T:(j + 1) * T],
                        start=True, stop=True), "pe")
                    wait("scalar", "pe", pv)
                    bank_free[psb] = _evs(op("scalar", lambda e, nidx=nidx, dst=dst,
                                             psb=psb: e.activation(
                        out=dst[:, nidx * T:(nidx + 1) * T],
                        in_=ps[psb][:], func=AF.Identity), "ev"))
                if g < 7:
                    wait("gpsimd", "pe", n["pe"])

        def build_ffn_units(mt):
            sq, tok0 = seq_of(mt)
            units = []
            gu_ev = {}

            pend = {}

            def gu_issue(th, kh):
                parw = (th * KH + kh) % 2
                rot = kh % 2
                bg, bu = (0, 1) if rot == 0 else (2, 3)
                qsw = f"qw{parw}"
                qsd = f"qd{parw}"
                wait("sync", "pe", guard_w[parw])
                vg = dma("sync", qsw, wbuf[:, parw], sc_g[kh])
                wait("sync", "pe", guard_d[parw])
                vu = dma("sync", qsd, dbuf[:, parw], sc_u[kh])
                wait("tensor", qsw, vg)
                gev = mm_group(bg, [
                    (lambda kd=kd, parw=parw: wbuf[:, parw, kd * P:(kd + 1) * P],
                     lambda kd=kd, th=th: x1_fm[:, kd * T + th * TH: kd * T + (th + 1) * TH],
                     lambda bg=bg: ps[bg][:, 0:TH]) for kd in range(KD)])
                guard_w[parw] = gev
                wait("tensor", qsd, vu)
                uev = mm_group(bu, [
                    (lambda kd=kd, parw=parw: dbuf[:, parw, kd * P:(kd + 1) * P],
                     lambda kd=kd, th=th: x1_fm[:, kd * T + th * TH: kd * T + (th + 1) * TH],
                     lambda bu=bu: ps[bu][:, 0:TH]) for kd in range(KD)])
                guard_d[parw] = uev
                st_["x1_use_pe"] = uev
                pend[(th, kh)] = (bg, bu, gev, uev)

            def gu_drain(th, kh):
                bg, bu, gev, uev = pend.pop((th, kh))
                par = kh % 2
                gb = tap[:, (2 + par) * TH:(3 + par) * TH]
                ub = tap[:, (4 + par) * TH:(5 + par) * TH]
                tb = tap[:, (6 + par) * TH:(7 + par) * TH]
                wait("scalar", "pe", gev)
                if st_["prev_gu"]:
                    wait("scalar", "dv", st_["prev_gu"])
                a_t = op("scalar", lambda e, bg=bg, tb=tb: e.activation(
                    out=tb, in_=ps[bg][:, 0:TH], func=AF.Tanh, scale=cst[:, 4:5]), "ac")
                eg = op("scalar", lambda e, bg=bg, gb=gb: e.activation(
                    out=gb, in_=ps[bg][:, 0:TH], func=AF.Identity), "ac")
                bank_free[bg] = ("ac", eg)
                wait("scalar", "pe", uev)
                eu = op("scalar", lambda e, bu=bu, ub=ub: e.activation(
                    out=ub, in_=ps[bu][:, 0:TH], func=AF.Identity, scale=cst[:, 4:5]), "ac")
                bank_free[bu] = ("ac", eu)
                wait("vector", "ac", eu)
                m1 = op("vector", lambda e, gb=gb, ub=ub: e.tensor_tensor(
                    out=ub, in0=ub, in1=gb, op=OP.mult), "vv")
                wait("vector", "vv", m1)
                g_ = op("vector", lambda e, kh=kh, tb=tb, ub=ub: e.scalar_tensor_tensor(
                    out=gu_half[:, kh * TH:(kh + 1) * TH], in0=tb, scalar=1.0, in1=ub,
                    op0=OP.add, op1=OP.mult), "dv")
                st_["prev_gu"] = g_
                gu_ev[(th, kh)] = g_

            def down_unit(th, kh):
                parw = kh % 2
                qsd = f"qd{parw}"
                wait("sync", "pe", guard_d[parw])
                vw = dma("sync", qsd, dbuf[:, parw], sc_d[kh])
                wait("tensor", qsd, vw)
                wait("tensor", "dv", gu_ev[(th, kh)])
                for j in range(2):
                    for dg in range(4):
                        psb = j * 4 + dg
                        stt, sp = (kh == 0), (kh == KH - 1)
                        if stt:
                            wait("tensor", bank_free[psb][0], bank_free[psb][1])
                        f = (lambda e, kh=kh, parw=parw, j=j, dg=dg, psb=psb,
                             stt=stt, sp=sp:
                             e.matmul(ps[psb][:],
                                      gu_half[:, kh * TH + j * P: kh * TH + (j + 1) * P],
                                      dbuf[:, parw, dg * 512:(dg + 1) * 512],
                                      start=stt, stop=sp))
                        if j == 1 and dg == 3:
                            op("tensor", f, "pe")
                        else:
                            op("tensor", f)
                guard_d[parw] = n["pe"]
                st_["down_pe"] = n["pe"]

            def ln2_unit(th, jt):
                tsl = 2 * th + jt
                DOWN_PE = st_["down_pe"]
                wait("gpsimd", "dv", n["dv"])
                wait("gpsimd", "qx", n["qx"])
                wait("gpsimd", "pe", DOWN_PE)
                va = dma("gpsimd", "qx", z_ln2[:], sc_x1[tsl])
                wait("vector", "qx", va)
                wait("vector", "pe", DOWN_PE)
                for dg in range(4):
                    psb = jt * 4 + dg
                    bank_free[psb] = _vvs(op("vector", lambda e, psb=psb, dg=dg:
                                             e.tensor_tensor(
                        out=z_ln2[:, dg * 512:(dg + 1) * 512],
                        in0=z_ln2[:, dg * 512:(dg + 1) * 512],
                        in1=ps[psb][:], op=OP.add), "vv"))
                wait("vector", bank_free[jt * 4 + 3][0], bank_free[jt * 4 + 3][1])
                zr = op("vector", lambda e: e.tensor_reduce(
                    out=cols[:, 3:4], in_=z_ln2[:], axis=AX.X, op=OP.add), "dv")
                wait("vector", "dv", zr)
                edv = op("vector", lambda e: e.tensor_scalar(
                    out=cols[:, 4:5], in0=cols[:, 3:4], scalar1=1.0 / D,
                    scalar2=None, op0=OP.mult), "dv")
                wait("scalar", "dv", edv)
                wait("scalar", "qo", n["qo"])
                ea = op("scalar", lambda e: e.activation(
                    out=rn_tm[:], in_=z_ln2[:], func=AF.Square,
                    accum_out=cols[:, 5:6]), "ac")
                wait("vector", "ac", ea)
                op("vector", lambda e: e.tensor_scalar(
                    out=cols[:, 6:7], in0=cols[:, 5:6], scalar1=1.0 / D,
                    scalar2=None, op0=OP.mult))
                c7 = op("vector", lambda e: e.tensor_scalar(
                    out=cols[:, 7:8], in0=cols[:, 4:5], scalar1=cols[:, 4:5],
                    scalar2=None, op0=OP.mult), "dv")
                wait("vector", "dv", c7)
                ev2 = op("vector", lambda e: e.tensor_tensor(
                    out=cols[:, 8:9], in0=cols[:, 6:7], in1=cols[:, 7:8],
                    op=OP.subtract), "dv")
                wait("scalar", "dv", ev2)
                a9 = op("scalar", lambda e: e.activation(
                    out=cols[:, 9:10], in_=cols[:, 8:9], func=AF.Ln,
                    bias=cst[:, 1:2]), "ac")
                wait("scalar", "ac", a9)
                er = op("scalar", lambda e: e.activation(
                    out=cols[:, 10:11], in_=cols[:, 9:10], func=AF.Exp,
                    scale=cst[:, 2:3]), "ac")
                wait("vector", "ac", er)
                c11 = op("vector", lambda e: e.tensor_scalar(
                    out=cols[:, 11:12], in0=cols[:, 4:5], scalar1=cols[:, 10:11],
                    scalar2=-1.0, op0=OP.mult, op1=OP.mult), "dv")
                wait("vector", "dv", c11)
                e5 = op("vector", lambda e: e.tensor_scalar(
                    out=rn_tm[:], in0=z_ln2[:], scalar1=cols[:, 10:11],
                    scalar2=cols[:, 11:12], op0=OP.mult, op1=OP.add), "dv")
                wait("gpsimd", "dv", e5)
                dma("gpsimd", "qo", out[sq, tok0 + tsl * P: tok0 + (tsl + 1) * P, :],
                    rn_tm[:])

            for th in range(2):
                for kh in range(KH + 1):
                    def stage(th=th, kh=kh):
                        if kh < KH:
                            gu_issue(th, kh)
                        if kh >= 1:
                            gu_drain(th, kh - 1)
                    units.append(stage)
                for kh in range(KH):
                    units.append(lambda th=th, kh=kh: down_unit(th, kh))
                units.append(lambda th=th: ln2_unit(th, 0))
                units.append(lambda th=th: ln2_unit(th, 1))
            return units

        def emit_scan(mt, units, reset_ev):
            cur_scope[0] = f"m{mt}_ph6"
            wait("vector", "ev", n["ev"])
            wait("vector", "dv", ev_A)
            wait("scalar", "dv", ev_A)
            if reset_ev:
                wait("vector", "sc", reset_ev)
                wait("scalar", "sc", reset_ev)
            ui = 0
            nu = len(units)
            for c in range(NE):
                vdx = op("vector", lambda e, c=c: e.tensor_tensor(
                    out=tap[:, 0:T], in0=dy_fm[:, c * T:(c + 1) * T],
                    in1=x_fm[:, c * T:(c + 1) * T], op=OP.mult), "dv")
                prev_i = vdx
                for nn_ in range(NS):
                    kkg = len(scan_ev)
                    par = kkg % 2
                    if kkg >= 2:
                        wait("scalar", "dv", scan_ev[kkg - 2])
                    ad_v = op("scalar", lambda e, c=c, nn_=nn_, par=par: e.activation(
                        out=d_buf[:, par * T:(par + 1) * T],
                        in_=dy_fm[:, c * T:(c + 1) * T],
                        func=AF.Exp, scale=A_sb[:, c * NS + nn_:c * NS + nn_ + 1]), "ad")
                    wait("vector", "dv", prev_i if nn_ == 0 else scan_ev[-1])
                    iv = op("vector", lambda e, nn_=nn_: e.tensor_tensor(
                        out=i2[:], in0=tap[:, 0:T],
                        in1=Brep[:, nn_ * T:(nn_ + 1) * T], op=OP.mult), "dv")
                    wait("vector", "ad", ad_v)
                    wait("vector", "dv", iv)
                    sev = op("vector", lambda e, nn_=nn_, c=c, par=par:
                             e.tensor_tensor_scan(
                        out=h_all[:, nn_ * T:(nn_ + 1) * T],
                        data0=d_buf[:, par * T:(par + 1) * T],
                        data1=i2[:],
                        initial=carry[:, c * NS + nn_:c * NS + nn_ + 1],
                        op0=OP.mult, op1=OP.add), "dv")
                    scan_ev.append(sev)
                wait("vector", "dv", scan_ev[-1])
                vcc = op("vector", lambda e, c=c: e.tensor_copy(
                    out=carry[:, c * NS:(c + 1) * NS],
                    in_=h_all.rearrange("p (n t) -> p n t", n=NS)[:, :, T - 1]), "dv")
                wait("vector", "dv", vcc)
                vt0 = op("vector", lambda e: e.tensor_tensor(
                    out=h_all[:], in0=h_all[:], in1=Crep[:], op=OP.mult), "dv")
                wait("vector", "dv", vt0)
                vt1 = op("vector", lambda e: e.tensor_tensor(
                    out=h_all[:, 0:8 * T], in0=h_all[:, 0:8 * T],
                    in1=h_all[:, 8 * T:16 * T], op=OP.add), "dv")
                wait("vector", "dv", vt1)
                vt2 = op("vector", lambda e: e.tensor_tensor(
                    out=h_all[:, 0:4 * T], in0=h_all[:, 0:4 * T],
                    in1=h_all[:, 4 * T:8 * T], op=OP.add), "dv")
                wait("vector", "dv", vt2)
                vt3 = op("vector", lambda e: e.tensor_tensor(
                    out=h_all[:, 0:2 * T], in0=h_all[:, 0:2 * T],
                    in1=h_all[:, 2 * T:4 * T], op=OP.add), "dv")
                wait("vector", "dv", vt3)
                op("vector", lambda e, c=c: e.tensor_tensor(
                    out=dy_fm[:, c * T:(c + 1) * T], in0=h_all[:, 0:T],
                    in1=h_all[:, T:2 * T], op=OP.add), "dv")
                tgt = (c + 1) * nu // NE
                while ui < tgt:
                    units[ui]()
                    ui += 1
            return n["dv"]

        def emit_ph7(mt, SCAN_DV):
            cur_scope[0] = f"m{mt}_ph7"
            sskip = tap[:, T:2 * T]
            ytmp = tap[:, 2 * T:3 * T]
            prev_ym = None
            for c in range(NE):
                par = c % 2
                qsem = f"qw{par}"
                wait("sync", "pe", guard_w[par])
                vw = dma("sync", qsem, wbuf[:, par], sc_skip[c])
                wait("tensor", qsem, vw)
                psb = par
                pev = mm_group(psb, [
                    (lambda kd=kd, par=par: wbuf[:, par, kd * P:(kd + 1) * P],
                     lambda kd=kd: rn_fm[:, kd * T:(kd + 1) * T],
                     lambda psb=psb: ps[psb][:]) for kd in range(KD)],
                    extra_wait=("dv", SCAN_DV) if c < 2 else None)
                guard_w[par] = pev
                wait("scalar", "pe", pev)
                if prev_ym:
                    wait("scalar", "dv", prev_ym)
                esk = op("scalar", lambda e, psb=psb: e.activation(
                    out=sskip, in_=ps[psb][:], func=AF.Silu), "ev")
                bank_free[psb] = _evs(esk)
                wait("vector", "ev", esk)
                y1 = op("vector", lambda e, c=c: e.tensor_scalar(
                    out=ytmp, in0=x_fm[:, c * T:(c + 1) * T],
                    scalar1=WD_sb[:, c:c + 1], scalar2=None, op0=OP.mult), "dv")
                wait("vector", "dv", y1)
                y2 = op("vector", lambda e, c=c: e.tensor_tensor(
                    out=ytmp, in0=ytmp, in1=dy_fm[:, c * T:(c + 1) * T], op=OP.add), "dv")
                wait("vector", "dv", y2)
                prev_ym = op("vector", lambda e, c=c: e.tensor_tensor(
                    out=dy_fm[:, c * T:(c + 1) * T], in0=ytmp, in1=sskip,
                    op=OP.mult), "dv")
            return n["dv"]

        def emit_ph8(mt, Y_DV):
            cur_scope[0] = f"m{mt}_ph8"
            sq, tok0 = seq_of(mt)
            for half in range(2):
                for c in range(NE):
                    par = c % 2
                    qsem = f"qw{par}"
                    wait("sync", "pe", guard_w[par])
                    vw = dma("sync", qsem, wbuf[:, par], sc_out[c])
                    wait("tensor", qsem, vw)
                    if half == 0 and c < 2:
                        wait("tensor", "dv", Y_DV)
                    for j in range(8):
                        tsl = 2 * half + j // 4
                        dg = j % 4
                        psb = j
                        stt, sp = (c == 0), (c == NE - 1)
                        if stt:
                            wait("tensor", bank_free[psb][0], bank_free[psb][1])
                        f = (lambda e, c=c, par=par, tsl=tsl, dg=dg, psb=psb, stt=stt,
                             sp=sp:
                             e.matmul(ps[psb][:],
                                      dy_fm[:, c * T + tsl * P: c * T + (tsl + 1) * P],
                                      wbuf[:, par, dg * 512:(dg + 1) * 512],
                                      start=stt, stop=sp))
                        if j == 7:
                            op("tensor", f, "pe")
                        else:
                            op("tensor", f)
                    guard_w[par] = n["pe"]
                OUT_PE = n["pe"]
                for jt in range(2):
                    tsl = 2 * half + jt
                    wait("gpsimd", "dv", n["dv"])
                    va = dma("gpsimd", "qz", z_x[:],
                             resid[sq, tok0 + tsl * P: tok0 + (tsl + 1) * P, :])
                    wait("vector", "qz", va)
                    wait("vector", "pe", OUT_PE)
                    z0 = op("vector", lambda e: e.tensor_scalar(
                        out=z_x[:], in0=z_x[:], scalar1=2.0, scalar2=None,
                        op0=OP.mult), "dv")
                    wait("vector", "dv", z0)
                    for dg in range(4):
                        psb = jt * 4 + dg
                        bank_free[psb] = _vvs(op("vector", lambda e, psb=psb, dg=dg:
                                                 e.tensor_tensor(
                            out=z_x[:, dg * 512:(dg + 1) * 512],
                            in0=z_x[:, dg * 512:(dg + 1) * 512],
                            in1=ps[psb][:], op=OP.add), "vv"))
                    wait("vector", bank_free[jt * 4 + 3][0], bank_free[jt * 4 + 3][1])
                    zr = op("vector", lambda e: e.tensor_reduce(
                        out=cols[:, 3:4], in_=z_x[:], axis=AX.X, op=OP.add), "dv")
                    wait("vector", "dv", zr)
                    edv = op("vector", lambda e: e.tensor_scalar(
                        out=cols[:, 4:5], in0=cols[:, 3:4], scalar1=1.0 / D,
                        scalar2=None, op0=OP.mult), "dv")
                    wait("scalar", "dv", edv)
                    wait("scalar", "qx", n["qx"])
                    wait("scalar", "qo", n["qo"])
                    wait("scalar", "pe", st_["lastdma_rn_pe"])
                    ea = op("scalar", lambda e: e.activation(
                        out=rn_tm[:], in_=z_x[:], func=AF.Square,
                        accum_out=cols[:, 5:6]), "ac")
                    wait("vector", "ac", ea)
                    op("vector", lambda e: e.tensor_scalar(
                        out=cols[:, 6:7], in0=cols[:, 5:6], scalar1=1.0 / D,
                        scalar2=None, op0=OP.mult))
                    c7 = op("vector", lambda e: e.tensor_scalar(
                        out=cols[:, 7:8], in0=cols[:, 4:5], scalar1=cols[:, 4:5],
                        scalar2=None, op0=OP.mult), "dv")
                    wait("vector", "dv", c7)
                    ev2 = op("vector", lambda e: e.tensor_tensor(
                        out=cols[:, 8:9], in0=cols[:, 6:7], in1=cols[:, 7:8],
                        op=OP.subtract), "dv")
                    wait("scalar", "dv", ev2)
                    a9 = op("scalar", lambda e: e.activation(
                        out=cols[:, 9:10], in_=cols[:, 8:9], func=AF.Ln,
                        bias=cst[:, 1:2]), "ac")
                    wait("scalar", "ac", a9)
                    er = op("scalar", lambda e: e.activation(
                        out=cols[:, 10:11], in_=cols[:, 9:10], func=AF.Exp,
                        scale=cst[:, 2:3]), "ac")
                    wait("vector", "ac", er)
                    c11 = op("vector", lambda e: e.tensor_scalar(
                        out=cols[:, 11:12], in0=cols[:, 4:5], scalar1=cols[:, 10:11],
                        scalar2=-1.0, op0=OP.mult, op1=OP.mult), "dv")
                    wait("vector", "dv", c11)
                    e4 = op("vector", lambda e: e.tensor_scalar(
                        out=rn_tm[:], in0=z_x[:], scalar1=cols[:, 10:11],
                        scalar2=cols[:, 11:12], op0=OP.mult, op1=OP.add), "dv")
                    wait("gpsimd", "dv", e4)
                    wait("gpsimd", "qx", n["qx"])
                    dma("gpsimd", "qx", sc_x1[tsl], rn_tm[:])
                    wait("tensor", "dv", e4)
                    if half == 0 and jt == 0:
                        wait("scalar", "pe", st_["x1_use_pe"])
                    for kd in range(KD):
                        psb = kd % 2
                        wait("tensor", bank_free[psb][0], bank_free[psb][1])
                        pv = op("tensor", lambda e, kd=kd, psb=psb: e.transpose(
                            ps[psb][:, 0:P], rn_tm[:, kd * P:(kd + 1) * P], ident[:]), "pe")
                        wait("scalar", "pe", pv)
                        bank_free[psb] = _evs(op("scalar", lambda e, kd=kd, tsl=tsl,
                                                 psb=psb: e.activation(
                            out=x1_fm[:, kd * T + tsl * P: kd * T + (tsl + 1) * P],
                            in_=ps[psb][:, 0:P], func=AF.Identity), "ev"))
                    st_["lastdma_rn_pe"] = n["pe"]

        rn_ev0 = emit_ph1(0)
        emit_ph2(0, ev_zero, rn_ev0)
        emit_ph5(0)
        emit_ph34(0)
        prev_units = []
        pending_reset = ev_zero
        for mt in range(MT):
            SCAN_DV = emit_scan(mt, prev_units, pending_reset)
            Y_DV = emit_ph7(mt, SCAN_DV)
            emit_ph8(mt, Y_DV)
            prev_units = build_ffn_units(mt)
            if mt + 1 < MT:
                nxt = mt + 1
                if nxt % (L // T) == 0:
                    wait("gpsimd", "dv", n["dv"])
                    op("gpsimd", lambda e: e.memset(carry[:], 0.0))
                    reset_ev = op("gpsimd", lambda e: e.memset(tails[:], 0.0), "sc")
                else:
                    reset_ev = None
                pending_reset = reset_ev
                rn_evn = emit_ph1(nxt)
                emit_ph2(nxt, reset_ev, rn_evn)
                emit_ph5(nxt)
                emit_ph34(nxt)
        cur_scope[0] = "tail_ffn"
        for u in prev_units:
            u()

        wait("gpsimd", "qo", n["qo"])
        wait("sync", "qo", n["qo"])

        from contextlib import nullcontext

        def register(engname):
            def body(eng):
                for scope, f in plan[engname]:
                    cm = nc.named_scope(scope) if scope else nullcontext()
                    with cm:
                        f(eng)
            return body

        block.sync(register("sync"))
        block.gpsimd(register("gpsimd"))
        block.tensor(register("tensor"))
        block.vector(register("vector"))
        block.scalar(register("scalar"))

    return nc


def kernel(**inputs):
    h = _prep_inputs(inputs)
    resid = np.ascontiguousarray(np.asarray(inputs["L0_attn_resid"], dtype=np.float32))
    nc = build()
    in_maps = []
    for core in range(NCORES):
        in_maps.append({
            "resid": resid[core * SEQS:(core + 1) * SEQS],
            "inwT": h["inwT"], "skipwT": h["skipwT"], "outwT": h["outwT"],
            "wd1T": h["wd1T"], "wd2T": h["wd2T"], "wBT": h["wBT"], "wCT": h["wCT"],
            "gwT": h["gwT"], "uwT": h["uwT"], "dwT": h["dwT"],
            "s_gate": h["s_gate"], "s_up": h["s_up"], "s_down": h["s_down"],
            "normw": h["normw"], "convw": h["convw"], "convb": h["convb"],
            "wd2b": h["wd2b"], "WD": h["WD"], "Alog": h["Alog"],
        })
    res = run_bass_kernel_spmd(nc, in_maps, core_ids=list(range(NCORES)))
    outs = [np.asarray(r["out"]) for r in res.results]
    return np.concatenate(outs, axis=0).reshape(16, L, D).astype(np.float32)


# version 06
# speedup vs baseline: 1.1805x; 1.0119x over previous
import numpy as np
import concourse.bass as bass
import concourse.mybir as mybir
from concourse.bass_utils import run_bass_kernel_spmd

F32 = mybir.dt.float32
BF16 = mybir.dt.bfloat16
I32 = mybir.dt.int32
AF = mybir.ActivationFunctionType
OP = mybir.AluOpType
AX = mybir.AxisListType

P = 128
D = 2048
E = 4096
HID = 5632
R = 128
NS = 16
KCONV = 4
SEQS = 2
L = 2048
T = 512
MT = SEQS * (L // T)
TSB = T // P
KD = D // P
NE = E // P
KH = HID // P
EPS = 1e-5
NCORES = 8

DMA_SEMS = ["qw0", "qw1", "qd0", "qd1", "qz", "qo", "qx", "qb", "qp"]
ENG_SEMS = ["pe", "ev", "vv", "dv", "ac", "sc", "ad"]


def _rearr_col(v, nchunk):
    return np.ascontiguousarray(np.asarray(v).reshape(nchunk, P).T).astype(np.float32)


def _prep_inputs(inp):
    asc = np.ascontiguousarray
    h = {}
    h["inwT"] = asc(inp["L0_attn_in_w"].T)
    h["skipwT"] = asc(inp["L0_attn_skip_w"].T)
    h["outwT"] = asc(inp["L0_attn_out_w"].T)
    h["wd1T"] = asc(inp["L0_attn_wd1"].T)
    h["wd2T"] = asc(inp["L0_attn_wd2"].T)
    h["wBT"] = asc(inp["L0_attn_wB"].T)
    h["wCT"] = asc(inp["L0_attn_wC"].T)
    h["gwT"] = asc(inp["L0_ffn_wq_gate"].T)
    h["uwT"] = asc(inp["L0_ffn_wq_up"].T)
    h["dwT"] = asc(inp["L0_ffn_wq_down"].T)
    h["s_gate"] = asc(np.asarray(inp["L0_ffn_s_gate"], np.float32).reshape(1, HID))
    h["s_up"] = asc(np.asarray(inp["L0_ffn_s_up"], np.float32).reshape(1, HID))
    h["s_down"] = asc(np.asarray(inp["L0_ffn_s_down"], np.float32).reshape(1, D))
    h["normw"] = _rearr_col(inp["L0_attn_norm_w"], KD)
    h["convw"] = asc(np.asarray(inp["L0_attn_conv_w"], np.float32).reshape(NE, P, KCONV)
                     .transpose(1, 0, 2).reshape(P, NE * KCONV))
    h["convb"] = _rearr_col(inp["L0_attn_conv_b"], NE)
    h["wd2b"] = _rearr_col(inp["L0_attn_wd2_b"], NE)
    h["WD"] = _rearr_col(inp["L0_attn_W_D"], NE)
    h["Alog"] = asc(np.asarray(inp["L0_attn_A_log"], np.float32).reshape(NE, P, NS)
                    .transpose(1, 0, 2).reshape(P, NE * NS))
    for k in ["inwT", "skipwT", "outwT", "wd1T", "wd2T", "wBT", "wCT"]:
        h[k] = np.asarray(h[k], np.float32)
    for k in ["gwT", "uwT", "dwT"]:
        h[k] = np.asarray(h[k], np.int32)
    return h


def build():
    nc = bass.Bass(target_bir_lowering=False)

    resid = nc.dram_tensor("resid", [SEQS, L, D], F32, kind="ExternalInput")
    inwT = nc.dram_tensor("inwT", [D, E], F32, kind="ExternalInput")
    skipwT = nc.dram_tensor("skipwT", [D, E], F32, kind="ExternalInput")
    outwT = nc.dram_tensor("outwT", [E, D], F32, kind="ExternalInput")
    wd1T = nc.dram_tensor("wd1T", [E, R], F32, kind="ExternalInput")
    wd2T = nc.dram_tensor("wd2T", [R, E], F32, kind="ExternalInput")
    wBT = nc.dram_tensor("wBT", [E, NS], F32, kind="ExternalInput")
    wCT = nc.dram_tensor("wCT", [E, NS], F32, kind="ExternalInput")
    gwT = nc.dram_tensor("gwT", [D, HID], I32, kind="ExternalInput")
    uwT = nc.dram_tensor("uwT", [D, HID], I32, kind="ExternalInput")
    dwT = nc.dram_tensor("dwT", [HID, D], I32, kind="ExternalInput")
    s_gate = nc.dram_tensor("s_gate", [1, HID], F32, kind="ExternalInput")
    s_up = nc.dram_tensor("s_up", [1, HID], F32, kind="ExternalInput")
    s_down = nc.dram_tensor("s_down", [1, D], F32, kind="ExternalInput")
    normw_in = nc.dram_tensor("normw", [P, KD], F32, kind="ExternalInput")
    convw_in = nc.dram_tensor("convw", [P, NE * KCONV], F32, kind="ExternalInput")
    convb_in = nc.dram_tensor("convb", [P, NE], F32, kind="ExternalInput")
    wd2b_in = nc.dram_tensor("wd2b", [P, NE], F32, kind="ExternalInput")
    WD_in = nc.dram_tensor("WD", [P, NE], F32, kind="ExternalInput")
    Alog_in = nc.dram_tensor("Alog", [P, NE * NS], F32, kind="ExternalInput")
    out = nc.dram_tensor("out", [SEQS, L, D], F32, kind="ExternalOutput")

    sc_in = nc.dram_tensor("sc_in", [NE, P, KD * P], BF16)
    sc_skip = nc.dram_tensor("sc_skip", [NE, P, KD * P], BF16)
    sc_out = nc.dram_tensor("sc_out", [NE, P, D], BF16)
    sc_g = nc.dram_tensor("sc_g", [KH, P, KD * P], BF16)
    sc_u = nc.dram_tensor("sc_u", [KH, P, KD * P], BF16)
    sc_d = nc.dram_tensor("sc_d", [KH, P, D], BF16)
    sc_bc = nc.dram_tensor("sc_bc", [2 * NS, T], BF16)
    sc_wd = nc.dram_tensor("sc_wd", [2, P, E], BF16)
    sc_x1 = nc.dram_tensor("sc_x1", [TSB, P, D], F32)

    from contextlib import ExitStack
    with ExitStack() as ctx:
        En = ctx.enter_context

        def sbt(name, shape, dt=BF16):
            return En(nc.sbuf_tensor(name, shape, dt))

        consts = sbt("consts", [P, 3328])
        ident = consts[:, 0:256].bitcast(F32)
        A_sb = consts[:, 256:1280].bitcast(F32)
        convw_sb = consts[:, 1280:1536].bitcast(F32)
        convb_sb = consts[:, 1536:1600].bitcast(F32)
        wd2b_sb = consts[:, 1600:1664].bitcast(F32)
        WD_sb = consts[:, 1664:1728].bitcast(F32)
        normw_sb = consts[:, 1728:1760].bitcast(F32)
        cols = consts[:, 1760:1792].bitcast(F32)
        cst = consts[:, 1792:1808].bitcast(F32)
        wB_sb = consts[:, 1808:2320]
        wC_sb = consts[:, 2320:2832]
        ones1 = consts[0:1, 2832:2960]
        ones1f = consts[0:1, 2960:3216].bitcast(F32)
        tails = consts[:, 3216:3312]

        x_fm = sbt("x_fm", [P, NE * T])
        dy_fm = sbt("dy_fm", [P, NE * T])
        z_x = x_fm[:, 0:4096].bitcast(F32)
        rn_fm = sbt("rn_fm", [P, KD * T])
        rn_tm = sbt("rn_tm", [P, D], F32)
        wbuf = sbt("wbuf", [P, 2, KD * P])
        dbuf = sbt("dbuf", [P, 2, D])

        arena = sbt("arena", [P, 49728])
        Brep = arena[:, 0:NS * T]
        Crep = arena[:, NS * T:2 * NS * T]
        h_all = arena[:, 16384:16384 + NS * T]
        i2 = arena[:, 24576:24576 + T]
        d_buf = arena[:, 25088:25088 + 2 * T]
        carry = arena[:, 26112:26112 + 2 * NE * NS].bitcast(F32)
        tap = arena[:, 27136:27136 + 4 * T]
        brow = arena[0:1, 27136:27136 + 4 * T]
        bc_nt = arena[0:2 * NS, 29184:29184 + T]
        bc_t_f = arena[:, 29696:29760].bitcast(F32)
        gt_b = arena[:, 29760:29760 + T // 2]
        up_b = arena[:, 30016:30016 + T // 2]
        x1_fm = arena[:, 30272:30272 + KD * T]
        gu_half = arena[:, 38464:38464 + KH * (T // 2)]
        z_ln2 = arena[:, 38464 + 7168:38464 + 11264].bitcast(F32)
        rms_f = arena[:, 38464:38464 + 2 * D].bitcast(F32)
        cpair = arena[:, 38464:38464 + 4096].bitcast(F32)
        conv_f = arena[:, 42560:43584].bitcast(F32)
        xin_buf = arena[:, 43584:43584 + KCONV - 1 + T]
        t1_sb = arena[:, 44168:44168 + T]
        wtile32 = arena[:, 0:2 * E].bitcast(F32)
        wq32 = arena[:, 0:2 * HID].bitcast(I32)
        wtile16 = arena[:, 2 * HID:3 * HID]
        srow = arena[0:1, 16896:16896 + 2 * HID].bitcast(F32)
        srep = arena[:, 28160:28160 + 2 * HID].bitcast(F32)

        ps = [En(nc.psum_tensor(f"ps{i}", [P, 512], F32)) for i in range(8)]

        sems = {}
        for name in DMA_SEMS + ENG_SEMS:
            sems[name] = En(nc.semaphore(name))

        block = En(nc.Block())
        plan = {k: [] for k in ["sync", "gpsimd", "tensor", "vector", "scalar"]}
        n = {k: 0 for k in sems}
        cur_scope = [None]

        def emit(eng, fn):
            plan[eng].append((cur_scope[0], fn))

        def wait(eng, sem, val):
            if val is None or val <= 0:
                return
            emit(eng, lambda e, sem=sem, val=val: e.wait_ge(sems[sem], val))

        def dma(eng, sem, dst, src):
            n[sem] += 16
            v = n[sem]
            emit(eng, lambda e, dst=dst, src=src, sem=sem:
                 e.dma_start(out=dst, in_=src).then_inc(sems[sem], 16))
            return v

        def op(eng, fn, sem=None):
            if sem is None:
                emit(eng, fn)
                return None
            n[sem] += 1
            v = n[sem]
            emit(eng, lambda e, fn=fn, sem=sem: fn(e).then_inc(sems[sem], 1))
            return v

        bank_free = {i: ("ev", 0) for i in range(8)}

        def _evs(v):
            return ("ev", v)

        def _vvs(v):
            return ("vv", v)

        def mm_group(psb, tiles, extra_wait=None):
            first = True
            nt = len(tiles)
            res = None
            for j, (lf, rf, of) in enumerate(tiles):
                if first:
                    wait("tensor", bank_free[psb][0], bank_free[psb][1])
                    if extra_wait is not None:
                        wait("tensor", extra_wait[0], extra_wait[1])
                    first = False
                st, sp = (j == 0), (j == nt - 1)
                f = (lambda e, lf=lf, rf=rf, of=of, st=st, sp=sp:
                     e.matmul(of(), lf(), rf(), start=st, stop=sp))
                if sp:
                    res = op("tensor", f, "pe")
                else:
                    op("tensor", f)
            return res

        ev_ms = op("gpsimd", lambda e: e.memset(ident[:], 0.0), "sc")
        wait("gpsimd", "sc", ev_ms)
        ev_ident = op("gpsimd", lambda e: e.affine_select(
            out=ident[:], in_=ident[:], compare_op=OP.not_equal, fill=1.0,
            base=0, pattern=[[-1, P]], channel_multiplier=1), "sc")
        ev_ones = op("gpsimd", lambda e: e.memset(ones1[:], 1.0), "sc")
        ev_onesf = op("gpsimd", lambda e: e.memset(ones1f[:], 1.0), "sc")
        op("gpsimd", lambda e: e.memset(cst[:, 0:1], -128.0))
        op("gpsimd", lambda e: e.memset(cst[:, 1:2], EPS))
        op("gpsimd", lambda e: e.memset(cst[:, 2:3], -0.5))
        op("gpsimd", lambda e: e.memset(cst[:, 3:4], 1.0 / D))
        ev_cst = op("gpsimd", lambda e: e.memset(cst[:, 4:5], 0.5), "sc")

        dma("sync", "qw0", normw_sb[:], normw_in[:])
        dma("sync", "qw0", convw_sb[:], convw_in[:])
        dma("sync", "qw0", convb_sb[:], convb_in[:])
        dma("sync", "qw0", wd2b_sb[:], wd2b_in[:])
        dma("sync", "qw0", WD_sb[:], WD_in[:])
        v = dma("sync", "qw0", A_sb[:], Alog_in[:])
        wait("sync", "qw0", v)
        wait("scalar", "qw0", v)
        ev_aexp = op("scalar", lambda e: e.activation(
            out=A_sb[:], in_=A_sb[:], func=AF.Exp), "ac")
        wait("vector", "ac", ev_aexp)
        ev_A = op("vector", lambda e: e.tensor_scalar(
            out=A_sb[:], in0=A_sb[:], scalar1=-1.0, scalar2=None, op0=OP.mult), "dv")

        for c in range(NE):
            dma("sync", "qw0", wtile32[:, c * R:(c + 1) * R], wd1T[c * P:(c + 1) * P, :])
        wait("sync", "qw0", n["qw0"])
        wait("vector", "qw0", n["qw0"])
        ewd1 = op("vector", lambda e: e.tensor_copy(
            out=wtile16[:, 0:NE * R], in_=wtile32[:, 0:NE * R]), "dv")
        wait("gpsimd", "dv", ewd1)
        vx = dma("gpsimd", "qp", sc_wd[0, :, 0:NE * R], wtile16[:, 0:NE * R])
        wait("sync", "dv", ewd1)
        v = dma("sync", "qw0", wtile32[:], wd2T[:])
        wait("sync", "qw0", v)
        wait("vector", "qw0", v)
        wait("vector", "qp", vx)
        ewd2 = op("vector", lambda e: e.tensor_copy(
            out=wtile16[:, 0:E], in_=wtile32[:]), "dv")
        wait("gpsimd", "dv", ewd2)
        vx = dma("gpsimd", "qp", sc_wd[1], wtile16[:, 0:E])
        wait("sync", "dv", ewd2)
        for c in range(NE):
            dma("sync", "qw0", wtile32[:, c * NS:(c + 1) * NS], wBT[c * P:(c + 1) * P, :])
        wait("sync", "qw0", n["qw0"])
        wait("vector", "qw0", n["qw0"])
        ebc1 = op("vector", lambda e: e.tensor_copy(
            out=wB_sb[:], in_=wtile32[:, 0:NE * NS]), "dv")
        wait("sync", "dv", ebc1)
        for c in range(NE):
            dma("sync", "qw0", wtile32[:, c * NS:(c + 1) * NS], wCT[c * P:(c + 1) * P, :])
        wait("sync", "qw0", n["qw0"])
        wait("vector", "qw0", n["qw0"])
        ev_smallw = op("vector", lambda e: e.tensor_copy(
            out=wC_sb[:], in_=wtile32[:, 0:NE * NS]), "dv")
        wait("sync", "dv", ev_smallw)

        def prep_folded(src, scratch):
            for kd in range(KD):
                v = dma("sync", "qw0", wtile32[:], src[kd * P:(kd + 1) * P, :])
                wait("scalar", "qw0", v)
                wait("scalar", "qp", n["qp"])
                ec = op("scalar", lambda e, kd=kd: e.activation(
                    out=wtile16[:, 0:E], in_=wtile32[:], func=AF.Identity,
                    scale=normw_sb[:, kd:kd + 1]), "ac")
                wait("gpsimd", "ac", ec)
                for c in range(NE):
                    dma("gpsimd", "qp", scratch[c, :, kd * P:(kd + 1) * P],
                        wtile16[:, c * P:(c + 1) * P])
                wait("sync", "ac", ec)

        prep_folded(inwT, sc_in)
        prep_folded(skipwT, sc_skip)

        for c in range(NE):
            v = dma("sync", "qw0", wtile32[:, 0:D], outwT[c * P:(c + 1) * P, :])
            wait("scalar", "qw0", v)
            wait("scalar", "qp", n["qp"])
            ec = op("scalar", lambda e: e.activation(
                out=wtile16[:, 0:D], in_=wtile32[:, 0:D], func=AF.Identity), "ac")
            wait("gpsimd", "ac", ec)
            dma("gpsimd", "qp", sc_out[c], wtile16[:, 0:D])
            wait("sync", "ac", ec)

        prev_em = [0]

        def prep_q(src, srcscale, scratch, ncols, nm, interleave):
            v = dma("sync", "qw0", srow[:, 0:ncols], srcscale[:])
            wait("sync", "qw0", v)
            wait("tensor", "qw0", v)
            wait("tensor", "sc", ev_onesf)
            nb = (ncols + 511) // 512
            for b_ in range(nb):
                lo, hi = b_ * 512, min(ncols, (b_ + 1) * 512)
                wait("tensor", bank_free[7][0], bank_free[7][1])
                pv = op("tensor", lambda e, lo=lo, hi=hi: e.matmul(
                    ps[7][:, 0:hi - lo], ones1f[0:1, :], srow[0:1, lo:hi],
                    start=True, stop=True), "pe")
                wait("vector", "pe", pv)
                if b_ == 0:
                    wait("vector", "dv", prev_em[0])
                bank_free[7] = _vvs(op("vector", lambda e, lo=lo, hi=hi: e.tensor_copy(
                    out=srep[:, lo:hi], in_=ps[7][:, 0:hi - lo]), "vv"))
            nk = src.shape[0] // P
            for kd in range(nk):
                v = dma("sync", "qw0", wq32[:, 0:ncols], src[kd * P:(kd + 1) * P, :])
                wait("scalar", "qw0", v)
                wait("scalar", "qp", n["qp"])
                wait("scalar", "sc", ev_cst)
                ec = op("scalar", lambda e, ncols=ncols: e.activation(
                    out=wtile16[:, 0:ncols], in_=wq32[:, 0:ncols], func=AF.Identity,
                    bias=cst[:, 0:1]), "ac")
                wait("vector", "ac", ec)
                if kd == 0:
                    wait("vector", bank_free[7][0], bank_free[7][1])
                em = op("vector", lambda e, ncols=ncols: e.tensor_tensor(
                    out=wtile16[:, 0:ncols], in0=wtile16[:, 0:ncols],
                    in1=srep[:, 0:ncols], op=OP.mult), "dv")
                wait("gpsimd", "dv", em)
                if interleave:
                    for m in range(nm):
                        dma("gpsimd", "qp", scratch[m, :, kd * P:(kd + 1) * P],
                            wtile16[:, m * P:(m + 1) * P])
                else:
                    dma("gpsimd", "qp", scratch[kd], wtile16[:, 0:ncols])
                wait("sync", "ac", ec)
                prev_em[0] = em

        prep_q(gwT, s_gate, sc_g, HID, KH, True)
        prep_q(uwT, s_up, sc_u, HID, KH, True)
        prep_q(dwT, s_down, sc_d, D, KD, False)

        PREP_QP = n["qp"]
        wait("vector", "qp", PREP_QP)
        wait("scalar", "qp", PREP_QP)
        wait("tensor", "qp", PREP_QP)
        wait("gpsimd", "qp", PREP_QP)
        wait("gpsimd", "dv", n["dv"])
        op("gpsimd", lambda e: e.memset(carry[:], 0.0))
        ev_zero = op("gpsimd", lambda e: e.memset(tails[:], 0.0), "sc")

        guard_w = {0: 0, 1: 0}
        guard_d = {0: 0, 1: 0}
        scan_ev = []
        st_ = {"lastdma_rn_pe": 0, "x1_use_pe": 0, "ffn_par": 0, "prev_gu": 0,
               "prev_sil_g": 0}
        TH = T // 2

        def seq_of(mt):
            return mt // (L // T), (mt % (L // T)) * T

        def emit_ph1(mt):
            cur_scope[0] = f"m{mt}_ph1"
            sq, tok0 = seq_of(mt)
            e3_prev = None
            for ts in range(TSB):
                wait("gpsimd", "dv", n["dv"])
                va = dma("gpsimd", "qz", z_x[:],
                         resid[sq, tok0 + ts * P: tok0 + (ts + 1) * P, :])
                wait("scalar", "qz", va)
                if e3_prev:
                    wait("scalar", "dv", e3_prev)
                if mt == 0 and ts == 0:
                    wait("scalar", "sc", ev_cst)
                a1 = op("scalar", lambda e: e.activation(
                    out=rms_f[:], in_=z_x[:], func=AF.Square, accum_out=cols[:, 0:1]), "ac")
                wait("scalar", "ac", a1)
                a2 = op("scalar", lambda e: e.activation(
                    out=cols[:, 1:2], in_=cols[:, 0:1], func=AF.Ln,
                    scale=cst[:, 3:4], bias=cst[:, 1:2]), "ac")
                wait("scalar", "ac", a2)
                e2 = op("scalar", lambda e: e.activation(
                    out=cols[:, 2:3], in_=cols[:, 1:2], func=AF.Exp, scale=cst[:, 2:3]), "ac")
                wait("vector", "ac", e2)
                e3 = op("vector", lambda e: e.tensor_scalar(
                    out=rms_f[:], in0=z_x[:], scalar1=cols[:, 2:3], scalar2=None,
                    op0=OP.mult), "dv")
                e3_prev = e3
                wait("tensor", "dv", e3)
                if mt == 0 and ts == 0:
                    wait("tensor", "sc", ev_ident)
                for kd in range(KD):
                    psb = 2 + kd % 2
                    wait("tensor", bank_free[psb][0], bank_free[psb][1])
                    pv = op("tensor", lambda e, kd=kd, psb=psb: e.transpose(
                        ps[psb][:, 0:P], rms_f[:, kd * P:(kd + 1) * P], ident[:]), "pe")
                    wait("scalar", "pe", pv)
                    bank_free[psb] = _evs(op("scalar", lambda e, kd=kd, ts=ts, psb=psb:
                                             e.activation(
                        out=rn_fm[:, kd * T + ts * P: kd * T + (ts + 1) * P],
                        in_=ps[psb][:, 0:P], func=AF.Identity), "ev"))
            return n["ev"]

        def emit_ph2(mt, reset_ev, rn_ev):
            cur_scope[0] = f"m{mt}_ph2"
            if reset_ev:
                wait("vector", "sc", reset_ev)
            prev_sil = None
            for c in range(NE):
                par = c % 2
                qsem = f"qw{par}"
                wait("sync", "pe", guard_w[par])
                vw = dma("sync", qsem, wbuf[:, par], sc_in[c])
                wait("tensor", qsem, vw)
                psb = par
                pev = mm_group(psb, [
                    (lambda kd=kd, par=par: wbuf[:, par, kd * P:(kd + 1) * P],
                     lambda kd=kd: rn_fm[:, kd * T:(kd + 1) * T],
                     lambda psb=psb: ps[psb][:]) for kd in range(KD)],
                    extra_wait=("ev", rn_ev) if c < 2 else None)
                guard_w[par] = pev
                wait("scalar", "pe", pev)
                if prev_sil:
                    wait("scalar", "dv", prev_sil)
                eev = op("scalar", lambda e, psb=psb: e.activation(
                    out=xin_buf[:, KCONV - 1:KCONV - 1 + T], in_=ps[psb][:],
                    func=AF.Identity), "ev")
                bank_free[psb] = _evs(eev)
                wait("vector", "ev", eev)
                v1 = op("vector", lambda e, c=c: e.tensor_copy(
                    out=xin_buf[:, 0:KCONV - 1],
                    in_=tails[:, c * (KCONV - 1):(c + 1) * (KCONV - 1)]), "dv")
                wait("vector", "dv", v1)
                op("vector", lambda e, c=c: e.tensor_copy(
                    out=tails[:, c * (KCONV - 1):(c + 1) * (KCONV - 1)],
                    in_=xin_buf[:, T:T + KCONV - 1]))
                acc = tap[:, 3 * T:4 * T]
                pv_ = op("vector", lambda e, c=c, acc=acc: e.tensor_scalar(
                    out=acc, in0=xin_buf[:, 0:T],
                    scalar1=convw_sb[:, c * KCONV:c * KCONV + 1],
                    scalar2=None, op0=OP.mult), "dv")
                for k in range(1, KCONV):
                    wait("vector", "dv", pv_)
                    pv_ = op("vector", lambda e, c=c, k=k, acc=acc: e.scalar_tensor_tensor(
                        out=acc, in0=xin_buf[:, k:k + T],
                        scalar=convw_sb[:, c * KCONV + k:c * KCONV + k + 1],
                        in1=acc, op0=OP.mult, op1=OP.add), "dv")
                edv = pv_
                prev_sil = edv
                wait("scalar", "dv", edv)
                op("scalar", lambda e, c=c, acc=acc: e.activation(
                    out=x_fm[:, c * T:(c + 1) * T], in_=acc, func=AF.Silu,
                    bias=convb_sb[:, c:c + 1]), "ac")

        def emit_ph34(mt):
            cur_scope[0] = f"m{mt}_ph34"
            wdv = wbuf[:].rearrange("p a b -> p (a b)")
            wait("sync", "pe", guard_w[0])
            wait("sync", "pe", guard_w[1])
            vw1 = dma("sync", "qw0", wdv[:, 0:NE * P], sc_wd[0, :, 0:NE * P])
            wait("tensor", "qw0", vw1)
            X_AC = n["ac"]
            pev = mm_group(0, [
                (lambda c=c: wdv[:, c * P:(c + 1) * P],
                 lambda c=c: x_fm[:, c * T:(c + 1) * T],
                 lambda: ps[0][:]) for c in range(NE)],
                extra_wait=("ac", X_AC))
            guard_w[0] = pev
            guard_w[1] = pev
            wait("scalar", "pe", pev)
            e_t1 = op("scalar", lambda e: e.activation(
                out=t1_sb, in_=ps[0][:], func=AF.Identity), "ev")
            bank_free[0] = _evs(e_t1)
            wait("sync", "pe", pev)
            vw2 = dma("sync", "qw0", wdv[:, 0:E], sc_wd[1, :, 0:E])
            wait("tensor", "qw0", vw2)
            prev_ln = None
            for c in range(NE):
                psb = c % 2
                wait("tensor", bank_free[psb][0], bank_free[psb][1])
                if c == 0:
                    wait("tensor", "ev", e_t1)
                pev = op("tensor", lambda e, c=c, psb=psb: e.matmul(
                    ps[psb][:], wdv[:, c * P:(c + 1) * P], t1_sb,
                    start=True, stop=True), "pe")
                wait("scalar", "pe", pev)
                if prev_ln:
                    wait("scalar", "ev", prev_ln)
                a4 = op("scalar", lambda e, c=c, psb=psb: e.activation(
                    out=conv_f[:, 0:T], in_=ps[psb][:], func=AF.Exp,
                    bias=wd2b_sb[:, c:c + 1]), "ac")
                wait("scalar", "ac", a4)
                lv = op("scalar", lambda e, c=c: e.activation(
                    out=dy_fm[:, c * T:(c + 1) * T], in_=conv_f[:, 0:T],
                    func=AF.Ln, bias=1.0), "ev")
                bank_free[psb] = _evs(lv)
                prev_ln = lv
            guard_w[0] = n["pe"]
            guard_w[1] = n["pe"]

        def emit_ph5(mt):
            cur_scope[0] = f"m{mt}_ph5"
            X_AC = n["ac"]
            for ts in range(TSB):
                for wi, (wsb, psb, off) in enumerate([(wB_sb, 2, 0), (wC_sb, 3, NS)]):
                    pev = mm_group(psb, [
                        (lambda c=c, ts=ts: x_fm[:, c * T + ts * P: c * T + (ts + 1) * P],
                         lambda c=c, wsb=wsb: wsb[:, c * NS:(c + 1) * NS],
                         lambda psb=psb: ps[psb][:, 0:NS]) for c in range(NE)],
                        extra_wait=("ac", X_AC) if ts == 0 and wi == 0 else None)
                    wait("scalar", "pe", pev)
                    bank_free[psb] = _evs(op("scalar", lambda e, psb=psb, off=off:
                                             e.activation(
                        out=bc_t_f[:, off:off + NS], in_=ps[psb][:, 0:NS],
                        func=AF.Identity), "ev"))
                wait("tensor", "ev", n["ev"])
                wait("tensor", bank_free[2][0], bank_free[2][1])
                pv = op("tensor", lambda e: e.transpose(
                    ps[2][0:2 * NS, 0:P], bc_t_f[:], ident[:]), "pe")
                wait("scalar", "pe", pv)
                bank_free[2] = _evs(op("scalar", lambda e, ts=ts: e.activation(
                    out=bc_nt[:, ts * P:(ts + 1) * P], in_=ps[2][0:2 * NS, 0:P],
                    func=AF.Identity), "ev"))
            wait("gpsimd", "ev", n["ev"])
            wait("gpsimd", "qb", n["qb"])
            dma("gpsimd", "qb", sc_bc[:], bc_nt[:])
            wait("gpsimd", "qb", n["qb"])
            BCDST = [Brep, Crep]
            wait("tensor", "sc", ev_ones)
            for g in range(8):
                dma("gpsimd", "qb", brow[:, 0:4 * T],
                    sc_bc[:].rearrange("a b -> (a b)")[None, g * 4 * T:(g + 1) * 4 * T])
                wait("tensor", "qb", n["qb"])
                for j in range(4):
                    nidx = (g * 4 + j) % NS
                    dst = BCDST[(g * 4 + j) // NS]
                    psb = 4 + (j % 2)
                    wait("tensor", bank_free[psb][0], bank_free[psb][1])
                    pv = op("tensor", lambda e, j=j, psb=psb: e.matmul(
                        ps[psb][:], ones1[0:1, :], brow[:, j * T:(j + 1) * T],
                        start=True, stop=True), "pe")
                    wait("scalar", "pe", pv)
                    bank_free[psb] = _evs(op("scalar", lambda e, nidx=nidx, dst=dst,
                                             psb=psb: e.activation(
                        out=dst[:, nidx * T:(nidx + 1) * T],
                        in_=ps[psb][:], func=AF.Identity), "ev"))
                if g < 7:
                    wait("gpsimd", "pe", n["pe"])

        def build_ffn_units(mt):
            sq, tok0 = seq_of(mt)
            units = []
            gu_ev = {}

            pend = {}

            def gu_issue(th, kh):
                parw = (th * KH + kh) % 2
                rot = kh % 2
                bg, bu = (0, 1) if rot == 0 else (2, 3)
                qsw = f"qw{parw}"
                qsd = f"qd{parw}"
                wait("sync", "pe", guard_w[parw])
                vg = dma("sync", qsw, wbuf[:, parw], sc_g[kh])
                wait("sync", "pe", guard_d[parw])
                vu = dma("sync", qsd, dbuf[:, parw], sc_u[kh])
                wait("tensor", qsw, vg)
                gev = mm_group(bg, [
                    (lambda kd=kd, parw=parw: wbuf[:, parw, kd * P:(kd + 1) * P],
                     lambda kd=kd, th=th: x1_fm[:, kd * T + th * TH: kd * T + (th + 1) * TH],
                     lambda bg=bg: ps[bg][:, 0:TH]) for kd in range(KD)])
                guard_w[parw] = gev
                wait("tensor", qsd, vu)
                uev = mm_group(bu, [
                    (lambda kd=kd, parw=parw: dbuf[:, parw, kd * P:(kd + 1) * P],
                     lambda kd=kd, th=th: x1_fm[:, kd * T + th * TH: kd * T + (th + 1) * TH],
                     lambda bu=bu: ps[bu][:, 0:TH]) for kd in range(KD)])
                guard_d[parw] = uev
                st_["x1_use_pe"] = uev
                pend[(th, kh)] = (bg, bu, gev, uev)

            def gu_drain(th, kh):
                bg, bu, gev, uev = pend.pop((th, kh))
                par = kh % 2
                gb = tap[:, (2 + par) * TH:(3 + par) * TH]
                ub = tap[:, (4 + par) * TH:(5 + par) * TH]
                tb = tap[:, (6 + par) * TH:(7 + par) * TH]
                wait("scalar", "pe", gev)
                if st_["prev_gu"]:
                    wait("scalar", "dv", st_["prev_gu"])
                a_t = op("scalar", lambda e, bg=bg, tb=tb: e.activation(
                    out=tb, in_=ps[bg][:, 0:TH], func=AF.Tanh, scale=cst[:, 4:5]), "ac")
                eg = op("scalar", lambda e, bg=bg, gb=gb: e.activation(
                    out=gb, in_=ps[bg][:, 0:TH], func=AF.Identity), "ac")
                bank_free[bg] = ("ac", eg)
                wait("scalar", "pe", uev)
                eu = op("scalar", lambda e, bu=bu, ub=ub: e.activation(
                    out=ub, in_=ps[bu][:, 0:TH], func=AF.Identity, scale=cst[:, 4:5]), "ac")
                bank_free[bu] = ("ac", eu)
                wait("vector", "ac", eu)
                m1 = op("vector", lambda e, gb=gb, ub=ub: e.tensor_tensor(
                    out=ub, in0=ub, in1=gb, op=OP.mult), "vv")
                wait("vector", "vv", m1)
                g_ = op("vector", lambda e, kh=kh, tb=tb, ub=ub: e.scalar_tensor_tensor(
                    out=gu_half[:, kh * TH:(kh + 1) * TH], in0=tb, scalar=1.0, in1=ub,
                    op0=OP.add, op1=OP.mult), "dv")
                st_["prev_gu"] = g_
                gu_ev[(th, kh)] = g_

            def down_unit(th, kh):
                parw = kh % 2
                qsd = f"qd{parw}"
                wait("sync", "pe", guard_d[parw])
                vw = dma("sync", qsd, dbuf[:, parw], sc_d[kh])
                wait("tensor", qsd, vw)
                wait("tensor", "dv", gu_ev[(th, kh)])
                for j in range(2):
                    for dg in range(4):
                        psb = j * 4 + dg
                        stt, sp = (kh == 0), (kh == KH - 1)
                        if stt:
                            wait("tensor", bank_free[psb][0], bank_free[psb][1])
                        f = (lambda e, kh=kh, parw=parw, j=j, dg=dg, psb=psb,
                             stt=stt, sp=sp:
                             e.matmul(ps[psb][:],
                                      gu_half[:, kh * TH + j * P: kh * TH + (j + 1) * P],
                                      dbuf[:, parw, dg * 512:(dg + 1) * 512],
                                      start=stt, stop=sp))
                        if j == 1 and dg == 3:
                            op("tensor", f, "pe")
                        else:
                            op("tensor", f)
                guard_d[parw] = n["pe"]
                st_["down_pe"] = n["pe"]

            def ln2_unit(th, jt):
                tsl = 2 * th + jt
                DOWN_PE = st_["down_pe"]
                wait("gpsimd", "dv", n["dv"])
                wait("gpsimd", "qx", n["qx"])
                wait("gpsimd", "pe", DOWN_PE)
                va = dma("gpsimd", "qx", z_ln2[:], sc_x1[tsl])
                wait("vector", "qx", va)
                wait("vector", "pe", DOWN_PE)
                for dg in range(4):
                    psb = jt * 4 + dg
                    bank_free[psb] = _vvs(op("vector", lambda e, psb=psb, dg=dg:
                                             e.tensor_tensor(
                        out=z_ln2[:, dg * 512:(dg + 1) * 512],
                        in0=z_ln2[:, dg * 512:(dg + 1) * 512],
                        in1=ps[psb][:], op=OP.add), "vv"))
                wait("vector", bank_free[jt * 4 + 3][0], bank_free[jt * 4 + 3][1])
                zr = op("vector", lambda e: e.tensor_reduce(
                    out=cols[:, 3:4], in_=z_ln2[:], axis=AX.X, op=OP.add), "dv")
                wait("vector", "dv", zr)
                edv = op("vector", lambda e: e.tensor_scalar(
                    out=cols[:, 4:5], in0=cols[:, 3:4], scalar1=1.0 / D,
                    scalar2=None, op0=OP.mult), "dv")
                wait("scalar", "dv", edv)
                wait("scalar", "qo", n["qo"])
                ea = op("scalar", lambda e: e.activation(
                    out=rn_tm[:], in_=z_ln2[:], func=AF.Square,
                    accum_out=cols[:, 5:6]), "ac")
                wait("vector", "ac", ea)
                op("vector", lambda e: e.tensor_scalar(
                    out=cols[:, 6:7], in0=cols[:, 5:6], scalar1=1.0 / D,
                    scalar2=None, op0=OP.mult))
                c7 = op("vector", lambda e: e.tensor_scalar(
                    out=cols[:, 7:8], in0=cols[:, 4:5], scalar1=cols[:, 4:5],
                    scalar2=None, op0=OP.mult), "dv")
                wait("vector", "dv", c7)
                ev2 = op("vector", lambda e: e.tensor_tensor(
                    out=cols[:, 8:9], in0=cols[:, 6:7], in1=cols[:, 7:8],
                    op=OP.subtract), "dv")
                wait("scalar", "dv", ev2)
                a9 = op("scalar", lambda e: e.activation(
                    out=cols[:, 9:10], in_=cols[:, 8:9], func=AF.Ln,
                    bias=cst[:, 1:2]), "ac")
                wait("scalar", "ac", a9)
                er = op("scalar", lambda e: e.activation(
                    out=cols[:, 10:11], in_=cols[:, 9:10], func=AF.Exp,
                    scale=cst[:, 2:3]), "ac")
                wait("vector", "ac", er)
                c11 = op("vector", lambda e: e.tensor_scalar(
                    out=cols[:, 11:12], in0=cols[:, 4:5], scalar1=cols[:, 10:11],
                    scalar2=-1.0, op0=OP.mult, op1=OP.mult), "dv")
                wait("vector", "dv", c11)
                e5 = op("vector", lambda e: e.tensor_scalar(
                    out=rn_tm[:], in0=z_ln2[:], scalar1=cols[:, 10:11],
                    scalar2=cols[:, 11:12], op0=OP.mult, op1=OP.add), "dv")
                wait("gpsimd", "dv", e5)
                dma("gpsimd", "qo", out[sq, tok0 + tsl * P: tok0 + (tsl + 1) * P, :],
                    rn_tm[:])

            for th in range(2):
                for kh in range(KH + 1):
                    def stage(th=th, kh=kh):
                        if kh < KH:
                            gu_issue(th, kh)
                        if kh >= 1:
                            gu_drain(th, kh - 1)
                    units.append(stage)
                for kh in range(KH):
                    units.append(lambda th=th, kh=kh: down_unit(th, kh))
                units.append(lambda th=th: ln2_unit(th, 0))
                units.append(lambda th=th: ln2_unit(th, 1))
            return units

        def emit_scan(mt, units, reset_ev):
            cur_scope[0] = f"m{mt}_ph6"
            wait("vector", "ev", n["ev"])
            wait("vector", "dv", ev_A)
            wait("scalar", "dv", ev_A)
            if reset_ev:
                wait("vector", "sc", reset_ev)
                wait("scalar", "sc", reset_ev)
            ui = 0
            nu = len(units)
            for c in range(NE):
                vdx = op("vector", lambda e, c=c: e.tensor_tensor(
                    out=tap[:, 0:T], in0=dy_fm[:, c * T:(c + 1) * T],
                    in1=x_fm[:, c * T:(c + 1) * T], op=OP.mult), "dv")
                prev_i = vdx
                for nn_ in range(NS):
                    kkg = len(scan_ev)
                    par = kkg % 2
                    if kkg >= 2:
                        wait("scalar", "dv", scan_ev[kkg - 2])
                    ad_v = op("scalar", lambda e, c=c, nn_=nn_, par=par: e.activation(
                        out=d_buf[:, par * T:(par + 1) * T],
                        in_=dy_fm[:, c * T:(c + 1) * T],
                        func=AF.Exp, scale=A_sb[:, c * NS + nn_:c * NS + nn_ + 1]), "ad")
                    wait("vector", "dv", prev_i if nn_ == 0 else scan_ev[-1])
                    iv = op("vector", lambda e, nn_=nn_: e.tensor_tensor(
                        out=i2[:], in0=tap[:, 0:T],
                        in1=Brep[:, nn_ * T:(nn_ + 1) * T], op=OP.mult), "dv")
                    wait("vector", "ad", ad_v)
                    wait("vector", "dv", iv)
                    sev = op("vector", lambda e, nn_=nn_, c=c, par=par:
                             e.tensor_tensor_scan(
                        out=h_all[:, nn_ * T:(nn_ + 1) * T],
                        data0=d_buf[:, par * T:(par + 1) * T],
                        data1=i2[:],
                        initial=carry[:, c * NS + nn_:c * NS + nn_ + 1],
                        op0=OP.mult, op1=OP.add), "dv")
                    scan_ev.append(sev)
                wait("vector", "dv", scan_ev[-1])
                vcc = op("vector", lambda e, c=c: e.tensor_copy(
                    out=carry[:, c * NS:(c + 1) * NS],
                    in_=h_all.rearrange("p (n t) -> p n t", n=NS)[:, :, T - 1]), "dv")
                wait("vector", "dv", vcc)
                vt0 = op("vector", lambda e: e.tensor_tensor(
                    out=h_all[:], in0=h_all[:], in1=Crep[:], op=OP.mult), "dv")
                wait("vector", "dv", vt0)
                vt1 = op("vector", lambda e: e.tensor_tensor(
                    out=h_all[:, 0:8 * T], in0=h_all[:, 0:8 * T],
                    in1=h_all[:, 8 * T:16 * T], op=OP.add), "dv")
                wait("vector", "dv", vt1)
                vt2 = op("vector", lambda e: e.tensor_tensor(
                    out=h_all[:, 0:4 * T], in0=h_all[:, 0:4 * T],
                    in1=h_all[:, 4 * T:8 * T], op=OP.add), "dv")
                wait("vector", "dv", vt2)
                vt3 = op("vector", lambda e: e.tensor_tensor(
                    out=h_all[:, 0:2 * T], in0=h_all[:, 0:2 * T],
                    in1=h_all[:, 2 * T:4 * T], op=OP.add), "dv")
                wait("vector", "dv", vt3)
                op("vector", lambda e, c=c: e.tensor_tensor(
                    out=dy_fm[:, c * T:(c + 1) * T], in0=h_all[:, 0:T],
                    in1=h_all[:, T:2 * T], op=OP.add), "dv")
                tgt = (c + 1) * nu // NE
                while ui < tgt:
                    units[ui]()
                    ui += 1
            return n["dv"]

        def emit_ph7(mt, SCAN_DV):
            cur_scope[0] = f"m{mt}_ph7"
            ym_ev = {}
            for c in range(NE):
                par = c % 2
                sskip = tap[:, (1 + par) * T:(2 + par) * T]
                ytmp = tap[:, 0:T] if par == 0 else tap[:, 3 * T:4 * T]
                qsem = f"qw{par}"
                wait("sync", "pe", guard_w[par])
                vw = dma("sync", qsem, wbuf[:, par], sc_skip[c])
                wait("tensor", qsem, vw)
                psb = par
                pev = mm_group(psb, [
                    (lambda kd=kd, par=par: wbuf[:, par, kd * P:(kd + 1) * P],
                     lambda kd=kd: rn_fm[:, kd * T:(kd + 1) * T],
                     lambda psb=psb: ps[psb][:]) for kd in range(KD)],
                    extra_wait=("dv", SCAN_DV) if c < 2 else None)
                guard_w[par] = pev
                wait("scalar", "pe", pev)
                if c >= 2:
                    wait("scalar", "dv", ym_ev[c - 2])
                esk = op("scalar", lambda e, psb=psb, sskip=sskip: e.activation(
                    out=sskip, in_=ps[psb][:], func=AF.Silu), "ev")
                bank_free[psb] = _evs(esk)
                wait("vector", "ev", esk)
                y1 = op("vector", lambda e, c=c, ytmp=ytmp: e.tensor_scalar(
                    out=ytmp, in0=x_fm[:, c * T:(c + 1) * T],
                    scalar1=WD_sb[:, c:c + 1], scalar2=None, op0=OP.mult), "dv")
                wait("vector", "dv", y1)
                y2 = op("vector", lambda e, c=c, ytmp=ytmp: e.tensor_tensor(
                    out=ytmp, in0=ytmp, in1=dy_fm[:, c * T:(c + 1) * T], op=OP.add), "dv")
                wait("vector", "dv", y2)
                ym_ev[c] = op("vector", lambda e, c=c, ytmp=ytmp, sskip=sskip:
                              e.tensor_tensor(
                    out=dy_fm[:, c * T:(c + 1) * T], in0=ytmp, in1=sskip,
                    op=OP.mult), "dv")
            return n["dv"]

        def emit_ph8(mt, Y_DV):
            cur_scope[0] = f"m{mt}_ph8"
            sq, tok0 = seq_of(mt)
            for half in range(2):
                for c in range(NE):
                    par = c % 2
                    qsem = f"qw{par}"
                    wait("sync", "pe", guard_w[par])
                    vw = dma("sync", qsem, wbuf[:, par], sc_out[c])
                    wait("tensor", qsem, vw)
                    if half == 0 and c < 2:
                        wait("tensor", "dv", Y_DV)
                    for j in range(8):
                        tsl = 2 * half + j // 4
                        dg = j % 4
                        psb = j
                        stt, sp = (c == 0), (c == NE - 1)
                        if stt:
                            wait("tensor", bank_free[psb][0], bank_free[psb][1])
                        f = (lambda e, c=c, par=par, tsl=tsl, dg=dg, psb=psb, stt=stt,
                             sp=sp:
                             e.matmul(ps[psb][:],
                                      dy_fm[:, c * T + tsl * P: c * T + (tsl + 1) * P],
                                      wbuf[:, par, dg * 512:(dg + 1) * 512],
                                      start=stt, stop=sp))
                        if j == 7:
                            op("tensor", f, "pe")
                        else:
                            op("tensor", f)
                    guard_w[par] = n["pe"]
                OUT_PE = n["pe"]
                for jt in range(2):
                    tsl = 2 * half + jt
                    wait("gpsimd", "dv", n["dv"])
                    va = dma("gpsimd", "qz", z_x[:],
                             resid[sq, tok0 + tsl * P: tok0 + (tsl + 1) * P, :])
                    wait("vector", "qz", va)
                    wait("vector", "pe", OUT_PE)
                    z0 = op("vector", lambda e: e.tensor_scalar(
                        out=z_x[:], in0=z_x[:], scalar1=2.0, scalar2=None,
                        op0=OP.mult), "dv")
                    wait("vector", "dv", z0)
                    for dg in range(4):
                        psb = jt * 4 + dg
                        bank_free[psb] = _vvs(op("vector", lambda e, psb=psb, dg=dg:
                                                 e.tensor_tensor(
                            out=z_x[:, dg * 512:(dg + 1) * 512],
                            in0=z_x[:, dg * 512:(dg + 1) * 512],
                            in1=ps[psb][:], op=OP.add), "vv"))
                    wait("vector", bank_free[jt * 4 + 3][0], bank_free[jt * 4 + 3][1])
                    zr = op("vector", lambda e: e.tensor_reduce(
                        out=cols[:, 3:4], in_=z_x[:], axis=AX.X, op=OP.add), "dv")
                    wait("vector", "dv", zr)
                    edv = op("vector", lambda e: e.tensor_scalar(
                        out=cols[:, 4:5], in0=cols[:, 3:4], scalar1=1.0 / D,
                        scalar2=None, op0=OP.mult), "dv")
                    wait("scalar", "dv", edv)
                    wait("scalar", "qx", n["qx"])
                    wait("scalar", "qo", n["qo"])
                    wait("scalar", "pe", st_["lastdma_rn_pe"])
                    ea = op("scalar", lambda e: e.activation(
                        out=rn_tm[:], in_=z_x[:], func=AF.Square,
                        accum_out=cols[:, 5:6]), "ac")
                    wait("vector", "ac", ea)
                    op("vector", lambda e: e.tensor_scalar(
                        out=cols[:, 6:7], in0=cols[:, 5:6], scalar1=1.0 / D,
                        scalar2=None, op0=OP.mult))
                    c7 = op("vector", lambda e: e.tensor_scalar(
                        out=cols[:, 7:8], in0=cols[:, 4:5], scalar1=cols[:, 4:5],
                        scalar2=None, op0=OP.mult), "dv")
                    wait("vector", "dv", c7)
                    ev2 = op("vector", lambda e: e.tensor_tensor(
                        out=cols[:, 8:9], in0=cols[:, 6:7], in1=cols[:, 7:8],
                        op=OP.subtract), "dv")
                    wait("scalar", "dv", ev2)
                    a9 = op("scalar", lambda e: e.activation(
                        out=cols[:, 9:10], in_=cols[:, 8:9], func=AF.Ln,
                        bias=cst[:, 1:2]), "ac")
                    wait("scalar", "ac", a9)
                    er = op("scalar", lambda e: e.activation(
                        out=cols[:, 10:11], in_=cols[:, 9:10], func=AF.Exp,
                        scale=cst[:, 2:3]), "ac")
                    wait("vector", "ac", er)
                    c11 = op("vector", lambda e: e.tensor_scalar(
                        out=cols[:, 11:12], in0=cols[:, 4:5], scalar1=cols[:, 10:11],
                        scalar2=-1.0, op0=OP.mult, op1=OP.mult), "dv")
                    wait("vector", "dv", c11)
                    e4 = op("vector", lambda e: e.tensor_scalar(
                        out=rn_tm[:], in0=z_x[:], scalar1=cols[:, 10:11],
                        scalar2=cols[:, 11:12], op0=OP.mult, op1=OP.add), "dv")
                    wait("gpsimd", "dv", e4)
                    wait("gpsimd", "qx", n["qx"])
                    dma("gpsimd", "qx", sc_x1[tsl], rn_tm[:])
                    wait("tensor", "dv", e4)
                    if half == 0 and jt == 0:
                        wait("scalar", "pe", st_["x1_use_pe"])
                    for kd in range(KD):
                        psb = kd % 2
                        wait("tensor", bank_free[psb][0], bank_free[psb][1])
                        pv = op("tensor", lambda e, kd=kd, psb=psb: e.transpose(
                            ps[psb][:, 0:P], rn_tm[:, kd * P:(kd + 1) * P], ident[:]), "pe")
                        wait("scalar", "pe", pv)
                        bank_free[psb] = _evs(op("scalar", lambda e, kd=kd, tsl=tsl,
                                                 psb=psb: e.activation(
                            out=x1_fm[:, kd * T + tsl * P: kd * T + (tsl + 1) * P],
                            in_=ps[psb][:, 0:P], func=AF.Identity), "ev"))
                    st_["lastdma_rn_pe"] = n["pe"]

        rn_ev0 = emit_ph1(0)
        emit_ph2(0, ev_zero, rn_ev0)
        emit_ph5(0)
        emit_ph34(0)
        prev_units = []
        pending_reset = ev_zero
        for mt in range(MT):
            SCAN_DV = emit_scan(mt, prev_units, pending_reset)
            Y_DV = emit_ph7(mt, SCAN_DV)
            emit_ph8(mt, Y_DV)
            prev_units = build_ffn_units(mt)
            if mt + 1 < MT:
                nxt = mt + 1
                if nxt % (L // T) == 0:
                    wait("gpsimd", "dv", n["dv"])
                    op("gpsimd", lambda e: e.memset(carry[:], 0.0))
                    reset_ev = op("gpsimd", lambda e: e.memset(tails[:], 0.0), "sc")
                else:
                    reset_ev = None
                pending_reset = reset_ev
                rn_evn = emit_ph1(nxt)
                emit_ph2(nxt, reset_ev, rn_evn)
                emit_ph5(nxt)
                emit_ph34(nxt)
        cur_scope[0] = "tail_ffn"
        for u in prev_units:
            u()

        wait("gpsimd", "qo", n["qo"])
        wait("sync", "qo", n["qo"])

        from contextlib import nullcontext

        def register(engname):
            def body(eng):
                for scope, f in plan[engname]:
                    cm = nc.named_scope(scope) if scope else nullcontext()
                    with cm:
                        f(eng)
            return body

        block.sync(register("sync"))
        block.gpsimd(register("gpsimd"))
        block.tensor(register("tensor"))
        block.vector(register("vector"))
        block.scalar(register("scalar"))

    return nc


def kernel(**inputs):
    h = _prep_inputs(inputs)
    resid = np.ascontiguousarray(np.asarray(inputs["L0_attn_resid"], dtype=np.float32))
    nc = build()
    in_maps = []
    for core in range(NCORES):
        in_maps.append({
            "resid": resid[core * SEQS:(core + 1) * SEQS],
            "inwT": h["inwT"], "skipwT": h["skipwT"], "outwT": h["outwT"],
            "wd1T": h["wd1T"], "wd2T": h["wd2T"], "wBT": h["wBT"], "wCT": h["wCT"],
            "gwT": h["gwT"], "uwT": h["uwT"], "dwT": h["dwT"],
            "s_gate": h["s_gate"], "s_up": h["s_up"], "s_down": h["s_down"],
            "normw": h["normw"], "convw": h["convw"], "convb": h["convb"],
            "wd2b": h["wd2b"], "WD": h["WD"], "Alog": h["Alog"],
        })
    res = run_bass_kernel_spmd(nc, in_maps, core_ids=list(range(NCORES)))
    outs = [np.asarray(r["out"]) for r in res.results]
    return np.concatenate(outs, axis=0).reshape(16, L, D).astype(np.float32)
